# version 04
# baseline (speedup 1.0000x reference)
import numpy as np
from contextlib import ExitStack

import concourse.bass as bass
import concourse.mybir as mybir
from concourse.bass_utils import run_bass_kernel_spmd

F32 = mybir.dt.float32
BF16 = mybir.dt.bfloat16
I32 = mybir.dt.int32
ALU = mybir.AluOpType
ACT = mybir.ActivationFunctionType
AX = mybir.AxisListType

D = 1024
DC = 8
NST = 16
BL = 2
NE = 16
NI = 512
IC = 4
HC = 8

SP_C = (0.001 + 0.1) / 2.0
SP_R = (0.1 - 0.001) / 2.0
ATANH_SC = 0.5 / (SP_R + 1e-8)
ATANH_BI = float(-SP_C * 0.5 / (SP_R + 1e-8))
LN17 = float(np.log(np.float32(1e-7)))


class Stream:
    def __init__(self, name):
        self.name = name
        self.q = []


class Sem:
    def __init__(self, ctx, nc, name):
        self.h = ctx.enter_context(nc.semaphore(name))
        self.n = 0


class Emit:
    def __init__(self, ctx, nc):
        self.nc = nc
        self.sync = Stream("sync")
        self.vec = Stream("vec")
        self.sca = Stream("sca")
        self.ten = Stream("ten")
        self.gps = Stream("gps")
        self.s_x = Sem(ctx, nc, "s_x")
        self.s_w = Sem(ctx, nc, "s_w")
        self.s_v = Sem(ctx, nc, "s_v")
        self.s_s = Sem(ctx, nc, "s_s")
        self.s_t = Sem(ctx, nc, "s_t")
        self.s_g = Sem(ctx, nc, "s_g")
        self.s_b = Sem(ctx, nc, "s_b")
        self.s_wsl = [Sem(ctx, nc, f"s_wsl{i}") for i in range(4)]
        self.s_wsh = Sem(ctx, nc, "s_wsh")
        self.ninstr = 0

    def op(self, stream, method, *args, inc=None, inck=1, **kw):
        self.ninstr += 1

        def f(eng, method=method, args=args, kw=kw, inc=inc, inck=inck):
            ins = getattr(eng, method)(*args, **kw)
            if inc is not None:
                ins.then_inc(inc.h, inck)
        stream.q.append(f)
        if inc is not None:
            inc.n += inck
            return inc.n
        return None

    def wait(self, stream, sem, val):
        stream.q.append(lambda eng, s=sem, v=val: eng.wait_ge(s.h, v))

    def drain(self, stream):
        self.ninstr += 1
        stream.q.append(lambda eng: eng.drain())

    def barrier(self):
        streams = [self.sync, self.vec, self.sca, self.ten, self.gps]
        self.wait(self.sync, self.s_x, self.s_x.n)
        self.wait(self.gps, self.s_w, self.s_w.n)
        for sm in self.s_wsl:
            if sm.n:
                self.wait(self.gps, sm, sm.n)
        if self.s_wsh.n:
            self.wait(self.gps, self.s_wsh, self.s_wsh.n)
        for st in streams:
            if st not in (self.sync,):
                self.drain(st)
            st.q.append(lambda eng, h=self.s_b.h: eng.sem_inc(h, 1))
            self.s_b.n += 1
        tgt = self.s_b.n
        for st in streams:
            self.wait(st, self.s_b, tgt)


def split_ranges(total, step, first=None):
    out = []
    t = 0
    if first is not None and total > 0:
        f = min(first, total)
        out.append((0, f))
        t = f
    while t < total:
        out.append((t, min(step, total - t)))
        t += min(step, total - t)
    return out


def emit_rsqrt(E, st, out, in_, t1, t2, iters=3):
    ib, t1b = in_.bitcast(I32), t1.bitcast(I32)
    E.op(st, "tensor_single_scalar", t1b, ib, 1, ALU.arith_shift_right)
    E.drain(st)
    E.op(st, "tensor_single_scalar", t1b, t1b, -1, ALU.bitwise_xor)
    E.drain(st)
    E.op(st, "tensor_single_scalar", t1b, t1b, int(0x5F3759DF + 1), ALU.add)
    E.drain(st)
    for _ in range(iters):
        E.op(st, "tensor_mul", t2, t1, t1)
        E.drain(st)
        E.op(st, "tensor_mul", t2, t2, in_)
        E.drain(st)
        E.op(st, "tensor_scalar", out=t2, in0=t2, scalar1=-0.5, scalar2=1.5,
             op0=ALU.mult, op1=ALU.add)
        E.drain(st)
        E.op(st, "tensor_mul", t1, t1, t2)
        E.drain(st)
    E.op(st, "tensor_copy", out, t1)
    E.drain(st)


def attn_a_chunks(Ll):
    out = []
    t = 0
    first = True
    while t < Ll:
        if first:
            n = min(121, Ll - t)
            first = False
        else:
            n = min(120, Ll - t)
        out.append((t, n))
        t += n
    return out


def subchunks_of(tc0, tcn):
    subs = []
    t = tc0
    end = tc0 + tcn
    if tc0 == 0:
        subs.append((0, 1))
        t = 1
    while t < end:
        n = min(10, end - t)
        subs.append((t, n))
        t += n
    return subs


def build_program(Ll=2048, FF_T=256, B_TC=64,
                  phases=("aa", "f1", "ab", "f2"), dbg=()):
    nc = bass.Bass("TRN2", target_bir_lowering=False)

    def din(name, shape, dt=F32):
        return nc.dram_tensor(name, list(shape), dt, kind="ExternalInput").ap()

    x_in = din("x_in", [D, BL, Ll])
    x_out = nc.dram_tensor("x_out", [D, BL, Ll], F32, kind="ExternalOutput").ap()
    xs_dram = nc.dram_tensor("xs_dram", [D, BL, Ll], F32, kind="Internal").ap()
    ph_order = [p for p in ("aa", "f1", "ab", "f2") if p in phases]

    def phase_io(p):
        src = x_in if (ph_order and p == ph_order[0]) else xs_dram
        dst = x_out if (ph_order and p == ph_order[-1]) else xs_dram
        return (src.rearrange("(dc p) b t -> p dc b t", p=128),
                dst.rearrange("(dc p) b t -> p dc b t", p=128))
    w_dtT = din("w_dtT", [D, D])
    w_BT = din("w_BT", [D, NST])
    w_CT = din("w_CT", [D, NST])
    b_dt_r = din("b_dt_r", [128, DC])
    A_log_r = din("A_log_r", [128, DC * NST])
    D_par_r = din("D_par_r", [128, DC])
    conv_w_r = din("conv_w_r", [128, DC * 4])
    conv_b_r = din("conv_b_r", [128, DC])
    w_rT = din("w_rT", [D, D])
    w_iT = din("w_iT", [D, D])
    wr_b_r = din("wr_b_r", [128, DC])
    wi_b_r = din("wi_b_r", [128, DC])
    log_a_r = din("log_a_r", [128, DC])
    w_oT = din("w_oT", [D, D])
    norm_w_r = din("norm_w_r", [128, DC])
    gate_wT = din("gate_wT", [D, NE])
    e_bias_bc = din("e_bias_bc", [128, NE])
    gprojT = din("gprojT", [NE, D, NI])
    uprojT = din("uprojT", [NE, D, NI])
    dprojT = din("dprojT", [NE, NI, D])
    sgT = din("sgT", [D, NI])
    suT = din("suT", [D, NI])
    sdT = din("sdT", [NI, D])
    c_ones1 = din("c_ones1", [1, 128], BF16)
    c_onescol = din("c_onescol", [128, 1], BF16)
    c_ident = din("c_ident", [128, 128], BF16)
    c_sel = din("c_sel", [NST, NST * 128], BF16)
    c_bias = din("c_bias", [128, 4])

    dbg_outs = {}
    for name, shape in dbg:
        dbg_outs[name] = nc.dram_tensor(
            "dbg_" + name, list(shape), F32, kind="ExternalOutput").ap()

    ctx = ExitStack()

    def sb(name, shape, dt=F32):
        return ctx.enter_context(nc.sbuf_tensor(name, list(shape), dt))

    A_tile = sb("A_tile", [128, DC * NST])
    biases = sb("biases", [128, DC * 8])
    conv_w = sb("conv_w", [128, DC * 4])
    gw_sb = sb("gw_sb", [128, DC * NE], BF16)
    wB_sb = sb("wB_sb", [128, DC * NST], BF16)
    wC_sb = sb("wC_sb", [128, DC * NST], BF16)
    ebias_sb = sb("ebias_sb", [128, NE])
    ones1 = sb("ones1", [1, 128], BF16)
    onescol = sb("onescol", [128, 1], BF16)
    ident = sb("ident", [128, 128], BF16)
    sel16 = sb("sel16", [NST, NST * 128], BF16)
    smalls = sb("smalls", [128, 3072])
    cbias = sb("cbias", [128, 4])
    hlrows = sb("hlrows", [128, 2560], BF16)

    E = Emit(ctx, nc)

    def psum(name, shape):
        return ctx.enter_context(nc.psum_tensor(name, list(shape), F32))

    pA = psum("pA", [128, 1024])
    pB = psum("pB", [128, 1024])
    pC = psum("pC", [128, 1024])
    pD = psum("pD", [128, 1024])

    bview = biases.rearrange("p (k dc) -> p k dc", k=8)

    for dst, src in ((ones1[:, :], c_ones1), (onescol[:, :], c_onescol),
                     (ident[:, :], c_ident), (sel16[:, :], c_sel),
                     (ebias_sb[:, :], e_bias_bc), (A_tile[:, :], A_log_r),
                     (conv_w[:, :], conv_w_r), (cbias[:, :], c_bias)):
        E.op(E.sync, "dma_start", out=dst, in_=src, inc=E.s_x, inck=16)
    for i, src in enumerate((b_dt_r, D_par_r, conv_b_r, wr_b_r, wi_b_r,
                             log_a_r, norm_w_r)):
        E.op(E.sync, "dma_start", out=bview[:, i, :], in_=src,
             inc=E.s_x, inck=16)
    E.op(E.gps, "dma_start",
         out=gw_sb.rearrange("p (kc e) -> p kc e", kc=DC),
         in_=gate_wT.rearrange("(kc p) e -> p kc e", p=128), inc=E.s_w, inck=16)
    E.op(E.gps, "dma_start",
         out=wB_sb.rearrange("p (kc n) -> p kc n", kc=DC),
         in_=w_BT.rearrange("(kc p) n -> p kc n", p=128), inc=E.s_w, inck=16)
    E.op(E.gps, "dma_start",
         out=wC_sb.rearrange("p (kc n) -> p kc n", kc=DC),
         in_=w_CT.rearrange("(kc p) n -> p kc n", p=128), inc=E.s_w, inck=16)

    E.wait(E.sca, E.s_x, E.s_x.n)
    E.op(E.sca, "activation", A_tile[:, :], A_tile[:, :], ACT.Tanh, scale=0.1)
    E.drain(E.sca)
    E.op(E.sca, "activation", A_tile[:, :], A_tile[:, :], ACT.Exp, scale=10.0)
    E.drain(E.sca)
    la = bview[:, 5, :]
    t1i = smalls[:, 0:DC]
    t2i = smalls[:, DC:2 * DC]
    E.op(E.sca, "activation", t1i, la, ACT.Abs)
    E.drain(E.sca)
    E.op(E.sca, "activation", t1i, t1i, ACT.Exp, scale=-1.0)
    E.drain(E.sca)
    E.op(E.sca, "activation", t1i, t1i, ACT.Ln, bias=1.0)
    E.drain(E.sca)
    E.op(E.sca, "activation", t2i, la, ACT.Relu, scale=-1.0, inc=E.s_s)
    E.wait(E.vec, E.s_s, E.s_s.n)
    E.op(E.vec, "tensor_add", t1i, t1i, t2i)
    E.drain(E.vec)
    E.op(E.vec, "tensor_single_scalar", la, t1i, -8.0, ALU.mult, inc=E.s_v)
    E.barrier()

    def emit_ln(tc0, tcn, out_f32, out_bf, sq_bf, toff=0, obn=None,
                f32view=None, pre_wait_t=None, xs=None, wait_x=None):
        TT = BL * tcn
        assert TT <= 512
        obv = out_bf.rearrange("p (dc b t) -> p dc b t", dc=DC, b=BL,
                               t=obn if False else None) if False else             out_bf.rearrange("p (dc b t) -> p dc b t", dc=DC, b=BL)[
                :, :, :, toff:toff + tcn]
        sqv = sq_bf.rearrange("p (dc b t) -> p dc b t", dc=DC, b=BL)[
            :, :, :, 0:tcn]

        if pre_wait_t is not None:
            E.wait(E.sca, E.s_t, pre_wait_t)
        if wait_x is not None:
            E.wait(E.sca, E.s_x, wait_x)
            E.wait(E.vec, E.s_x, wait_x)
        E.op(E.sca, "activation", sqv, xs, ACT.Square, inc=E.s_s)
        m_sq = E.s_s.n
        E.op(E.vec, "tensor_copy", obv, xs, inc=E.s_v)
        m_xc = E.s_v.n

        E.wait(E.ten, E.s_s, m_sq)
        E.wait(E.ten, E.s_v, m_xc)
        last = None
        for q, src in ((0, obv), (1, sqv)):
            for b in range(BL):
                dst = pB[0:1, q * 512 + b * tcn: q * 512 + (b + 1) * tcn]
                for kc in range(DC):
                    last = E.op(E.ten, "matmul", dst, onescol[:, :],
                                src[:, kc, b, :], start=(kc == 0),
                                stop=(kc == DC - 1),
                                inc=(E.s_t if (q, b, kc) == (1, BL - 1, DC - 1)
                                     else None))
        m_st = E.s_t.n
        E.last_ln_mst = m_st

        E.wait(E.vec, E.s_t, m_st)
        m = smalls[0:1, 0:TT]
        v = smalls[0:1, 512:512 + TT]
        w1 = smalls[0:1, 1024:1024 + TT]
        w2 = smalls[0:1, 1536:1536 + TT]
        mm_ = [pB[0:1, 0 * 512 + b * tcn:0 * 512 + (b + 1) * tcn] for b in range(BL)]
        vv_ = [pB[0:1, 1 * 512 + b * tcn:1 * 512 + (b + 1) * tcn] for b in range(BL)]
        E.op(E.vec, "tensor_single_scalar", m,
             pB[0:1, 0:TT] if tcn * BL == TT else None, 1.0 / D, ALU.mult)
        E.op(E.vec, "tensor_single_scalar", v, pB[0:1, 512:512 + TT],
             1.0 / D, ALU.mult)
        E.drain(E.vec)
        E.op(E.vec, "tensor_mul", w1, m, m)
        E.drain(E.vec)
        E.op(E.vec, "tensor_sub", v, v, w1)
        E.drain(E.vec)
        E.op(E.vec, "tensor_scalar_add", v, v, 1e-5)
        E.drain(E.vec)
        emit_rsqrt(E, E.vec, v, v, w1, w2)
        mh, ml = hlrows[0:1, 0:TT], hlrows[0:1, 512:512 + TT]
        ih, il = hlrows[0:1, 1024:1024 + TT], hlrows[0:1, 1536:1536 + TT]
        E.op(E.vec, "tensor_copy", mh, m)
        E.op(E.vec, "tensor_copy", ih, v)
        E.drain(E.vec)
        E.op(E.vec, "tensor_sub", w1, m, mh)
        E.op(E.vec, "tensor_sub", w2, v, ih)
        E.drain(E.vec)
        E.op(E.vec, "tensor_copy", ml, w1)
        E.op(E.vec, "tensor_copy", il, w2, inc=E.s_v)
        m_rows = E.s_v.n

        E.wait(E.ten, E.s_v, m_rows)
        E.op(E.ten, "matmul", pC[:, 0:TT], ones1[:, :], mh, start=True, stop=False)
        E.op(E.ten, "matmul", pC[:, 0:TT], ones1[:, :], ml, start=False, stop=True)
        E.op(E.ten, "matmul", pD[:, 0:TT], ones1[:, :], ih, start=True, stop=False)
        E.op(E.ten, "matmul", pD[:, 0:TT], ones1[:, :], il, start=False, stop=True,
             inc=E.s_t)
        m_bc = E.s_t.n

        E.wait(E.vec, E.s_t, m_bc)
        mbc = pC[:, 0:TT].rearrange("p (b t) -> p b t", b=BL)
        ibc = pD[:, 0:TT].rearrange("p (b t) -> p b t", b=BL)
        if f32view is not None:
            ov = f32view
        else:
            ov = out_f32.rearrange("p (dc b t) -> p dc b t", dc=DC, b=BL)[
                :, :, :, toff:toff + tcn]
        E.op(E.vec, "tensor_sub", ov, xs,
             mbc[:, None, :, :].broadcast_to([128, DC, BL, tcn]))
        E.drain(E.vec)
        E.op(E.vec, "tensor_mul", ov, ov,
             ibc[:, None, :, :].broadcast_to([128, DC, BL, tcn]))
        E.drain(E.vec)
        E.op(E.vec, "tensor_copy", obv, ov, inc=E.s_v)
        return E.s_v.n

    if "aa" in phases:
        A_TCmax = 121
        ph = ExitStack()

        def psb(name, shape, dt=F32):
            return ph.enter_context(nc.sbuf_tensor(name, list(shape), dt))

        W = BL * A_TCmax
        wdt_sb = psb("wdt_sb", [128, DC * D], BF16)
        ln1 = psb("aa_ln1", [128, DC * W])
        ln1b = psb("aa_ln1b", [128, DC * W], BF16)
        sqb = psb("aa_sqb", [128, DC * W], BF16)
        dt_b = psb("aa_dt", [128, DC * W])
        wdtx = psb("aa_wdtx", [128, DC * W])
        ybuf = psb("aa_y", [128, DC * W])
        zbuf = [psb("aa_z0", [128, 256 * 10]), psb("aa_z1", [128, 256 * 10])]
        ubuf0 = psb("aa_u0", [128, 256 * 10])
        ubuf = [ubuf0, ubuf0]
        dabuf = [psb("aa_da0", [128, 256 * 11]), psb("aa_da1", [128, 256 * 11])]
        btbuf = psb("aa_bt", [128, 256 * 11])
        Hbuf = [psb("aa_H0", [128, 256 * 11]), psb("aa_H1", [128, 256 * 11])]
        ytmp = [psb("aa_yt0", [128, 256 * 10]), psb("aa_yt1", [128, 256 * 10])]
        sp1 = ytmp[0]
        Bp = psb("aa_Bp", [NST, W])
        Cp = psb("aa_Cp", [NST, W])
        Bph = psb("aa_Bph", [NST, 2 * W], BF16)
        Cph = psb("aa_Cph", [NST, 2 * W], BF16)
        nst_t = psb("aa_nst", [NST, W])
        xbuf_a = psb("aa_x", [128, DC * W])
        A_exp = psb("aa_Aexp", [128, DC * BL * NST])
        bc_sb = psb("aa_bcsb", [128, 2048])
        scale_sb = psb("aa_scale", [128, 8])

        E.op(E.gps, "dma_start",
             out=wdt_sb.rearrange("p (kc m) -> p kc m", kc=DC),
             in_=w_dtT.rearrange("(kc p) m -> p kc m", p=128),
             inc=E.s_w, inck=16)
        m_wdt = E.s_w.n

        if dbg_outs:
            for buf in (ln1, dt_b, ybuf, sp1, wdtx):
                E.op(E.vec, "memset", buf[:, :], 0.0)
            E.drain(E.vec)

        aev = A_exp.rearrange("p (dc b n) -> p dc b n", dc=DC, b=BL)
        atv = A_tile.rearrange("p (dc n) -> p dc n", dc=DC)
        E.op(E.vec, "tensor_copy", aev,
             atv[:, :, None, :].broadcast_to([128, DC, BL, NST]))
        E.drain(E.vec)

        cur_S = [None]

        def set_S(S):
            if cur_S[0] != S:
                cur_S[0] = S
                for db in dabuf:
                    E.op(E.vec, "memset",
                         db[:, 0:256 * S].rearrange("p (r c) -> p r c", c=S)[
                             :, :, 0:1], 0.0)
                E.drain(E.vec)

        sp_marks_prev = None
        m_scale_sb = 0
        m_bt_free = 0
        first_sub_global = True
        sub_idx = [0]
        zda_marks = [0, 0]
        hq_marks = [0, 0]
        ymul_done = [0, 0]
        prev_par = 0

        src_aa, dst_aa = phase_io("aa")
        m_xst = 0
        chunks = attn_a_chunks(Ll)
        for ci, (tc0, tcn) in enumerate(chunks):
            TT = BL * tcn
            xbv = xbuf_a.rearrange("p (dc b t) -> p dc b t", dc=DC, b=BL)
            if m_xst:
                E.wait(E.sync, E.s_x, m_xst)
            for b in range(BL):
                E.op(E.sync, "dma_start", out=xbv[:, :, b, 0:tcn],
                     in_=src_aa[:, :, b, tc0:tc0 + tcn], inc=E.s_x, inck=16)
            m_xld = E.s_x.n
            m_ln = emit_ln(tc0, tcn, ln1, ln1b, sqb,
                           xs=xbv[:, :, :, 0:tcn], wait_x=m_xld)

            lbv = ln1b.rearrange("p (dc b t) -> p dc b t", dc=DC, b=BL)
            lnv = ln1.rearrange("p (dc b t) -> p dc b t", dc=DC, b=BL)
            dtv = dt_b.rearrange("p (dc b t) -> p dc b t", dc=DC, b=BL)
            wdv = wdt_sb.rearrange("p (kc m) -> p kc m", kc=DC)
            wxv = wdtx.rearrange("p (dc b t) -> p dc b t", dc=DC, b=BL)
            yv = ybuf.rearrange("p (dc b t) -> p dc b t", dc=DC, b=BL)

            E.wait(E.ten, E.s_v, m_ln)
            if ci == 0:
                E.wait(E.ten, E.s_w, m_wdt)
            dt_mm = []
            sp_marks = []
            for oc in range(DC):
                dst = pA[:, (oc % 2) * 512:(oc % 2) * 512 + TT]
                if oc >= 2:
                    E.wait(E.ten, E.s_s, sp_marks[oc - 2])
                elif ci > 0:
                    E.wait(E.ten, E.s_s, sp_marks_prev[oc + DC - 2])
                dst3 = dst.rearrange("p (b t) -> p b t", b=BL)
                for kc in range(DC):
                    E.op(E.ten, "matmul", dst3,
                         wdv[:, kc, oc * 128:(oc + 1) * 128],
                         lbv[:, kc, :, 0:tcn],
                         start=(kc == 0), stop=(kc == DC - 1),
                         inc=(E.s_t if kc == DC - 1 else None))
                dt_mm.append(E.s_t.n)
                E.wait(E.sca, E.s_t, dt_mm[oc])
                src = dst3
                a1 = sp1.rearrange("p (dc b t) -> p dc b t", dc=DC, b=BL)[
                    :, oc, :, 0:tcn]
                a2 = dtv[:, oc, :, 0:tcn]
                E.op(E.sca, "activation", a1, src, ACT.Abs,
                     bias=bview[:, 0, oc:oc + 1])
                E.drain(E.sca)
                E.op(E.sca, "activation", a1, a1, ACT.Exp, scale=-1.0)
                E.drain(E.sca)
                E.op(E.sca, "activation", a1, a1, ACT.Ln, bias=1.0)
                E.drain(E.sca)
                E.op(E.sca, "activation", a2, src, ACT.Relu,
                     bias=bview[:, 0, oc:oc + 1], inc=E.s_s)
                sp_marks.append(E.s_s.n)

            wbv = wB_sb.rearrange("p (kc n) -> p kc n", kc=DC)
            wcv = wC_sb.rearrange("p (kc n) -> p kc n", kc=DC)
            for dstoff, wsrc in ((0, wbv), (512, wcv)):
                dst3 = pB[0:NST, dstoff:dstoff + TT].rearrange(
                    "p (b t) -> p b t", b=BL)
                for kc in range(DC):
                    E.op(E.ten, "matmul", dst3,
                         wsrc[:, kc, :],
                         lbv[:, kc, :, 0:tcn],
                         start=(kc == 0), stop=(kc == DC - 1),
                         inc=(E.s_t if (dstoff == 512 and kc == DC - 1) else None))
            m_bpc = E.s_t.n

            sp_marks_prev = sp_marks

            for oc in range(DC):
                E.wait(E.vec, E.s_s, sp_marks[oc])
                a1 = sp1.rearrange("p (dc b t) -> p dc b t", dc=DC, b=BL)[
                    :, oc, :, 0:tcn]
                a2 = dtv[:, oc, :, 0:tcn]
                E.op(E.vec, "tensor_add", a2, a2, a1,
                     inc=(E.s_v if oc == DC - 1 else None))
            E.drain(E.vec)
            m_spsum = E.s_v.n

            E.wait(E.sca, E.s_v, m_spsum)
            for oc in range(DC):
                sl = dtv[:, oc, :, 0:tcn]
                E.op(E.sca, "activation", sl, sl, ACT.Tanh,
                     scale=ATANH_SC, bias=cbias[:, 0:1],
                     inc=(E.s_s if oc == DC - 1 else None))
            m_dtt = E.s_s.n

            E.wait(E.vec, E.s_s, m_dtt)
            sl = dtv[:, :, :, 0:tcn]
            E.op(E.vec, "tensor_scalar", out=sl, in0=sl,
                 scalar1=SP_R / 0.5, scalar2=SP_C, op0=ALU.mult, op1=ALU.add)
            E.drain(E.vec)
            E.op(E.vec, "tensor_mul", wxv[:, :, :, 0:tcn], sl,
                 lnv[:, :, :, 0:tcn], inc=E.s_v)
            E.drain(E.vec)
            m_wdtx = E.s_v.n

            E.wait(E.vec, E.s_t, m_bpc)
            E.op(E.vec, "tensor_copy", Bp[:, 0:TT], pB[0:NST, 0:TT])
            E.op(E.vec, "tensor_copy", Cp[:, 0:TT], pB[0:NST, 512:512 + TT])
            E.drain(E.vec)
            for src, dsthl in ((Bp, Bph), (Cp, Cph)):
                E.op(E.vec, "tensor_copy", dsthl[:, 0:TT], src[:, 0:TT])
                E.drain(E.vec)
                E.op(E.vec, "tensor_sub", nst_t[:, 0:TT], src[:, 0:TT],
                     dsthl[:, 0:TT])
                E.drain(E.vec)
                E.op(E.vec, "tensor_copy", dsthl[:, TT:2 * TT], nst_t[:, 0:TT])
                E.drain(E.vec)
            E.op(E.vec, "memset", smalls[0:1, 2047:2048], 0.0, inc=E.s_v)
            m_bpchl = E.s_v.n

            subs = subchunks_of(tc0, tcn)
            groups = []
            for i in range(0, len(subs), 3):
                gs = subs[i:i + 3]
                groups.append((gs[0][0], gs[-1][0] + gs[-1][1] - gs[0][0], gs))

            m_grp_consumed = None
            m_grp_evac = None
            for gi, (g0, gtn, gsubs) in enumerate(groups):
                assert BL * gtn <= 64
                E.wait(E.ten, E.s_v, m_bpchl)
                if m_grp_evac is not None:
                    E.wait(E.ten, E.s_s, m_grp_evac)
                for srchl, dstp in ((Bph, pC), (Cph, pD)):
                    hlv = srchl[:, 0:2 * TT].rearrange(
                        "q (h b t) -> q h b t", h=2, b=BL)
                    for n in range(NST):
                        dst = dstp[:, n * 64:n * 64 + BL * gtn].rearrange(
                            "p (b t) -> p b t", b=BL)
                        E.op(E.ten, "matmul", dst, sel16[:, n * 128:(n + 1) * 128],
                             hlv[:, 0, :, g0 - tc0:g0 - tc0 + gtn],
                             start=True, stop=False)
                        E.op(E.ten, "matmul", dst, sel16[:, n * 128:(n + 1) * 128],
                             hlv[:, 1, :, g0 - tc0:g0 - tc0 + gtn],
                             start=False, stop=True,
                             inc=(E.s_t if (srchl is Cph and n == NST - 1)
                                  else None))
                m_grp_fill = E.s_t.n

                E.wait(E.sca, E.s_t, m_grp_fill)
                if m_grp_consumed is not None:
                    E.wait(E.sca, E.s_v, m_grp_consumed)
                for qi, srcp in ((0, pC), (1, pD)):
                    inap = srcp[:, 0:NST * 64].rearrange(
                        "p (n t) -> p n t", n=NST)[:, :, 0:BL * gtn].rearrange(
                        "p n (b t) -> p b n t", b=BL)
                    outap = bc_sb[:, qi * 1024:(qi + 1) * 1024].rearrange(
                        "p (b n t) -> p b n t", b=BL, n=NST)[:, :, :, 0:gtn]
                    E.op(E.sca, "copy", outap, inap,
                         inc=(E.s_s if qi == 1 else None))
                m_grp_evac = E.s_s.n

                bcB = bc_sb[:, 0:1024].rearrange("p (bn t) -> p bn t", t=32)
                bcC = bc_sb[:, 1024:2048].rearrange("p (bn t) -> p bn t", t=32)

                for (st0, sln) in gsubs:
                    S = sln + 1
                    set_S(S)
                    lo = st0 - g0
                    flag = (st0 + sln - 1) % 10 == 0
                    par = sub_idx[0] % 2

                    dslm = dtv[:, :, :, st0 - tc0:st0 - tc0 + sln].rearrange(
                        "p dc b t -> p (dc b) t")
                    Aslm = A_exp[:, :].rearrange("p (a n) -> p a n", n=NST)
                    z3 = zbuf[par][:, 0:256 * sln].rearrange(
                        "p (a n t) -> p a n t", a=16, n=NST)
                    E.wait(E.vec, E.s_v, m_wdtx)
                    if zda_marks[par]:
                        E.wait(E.vec, E.s_v, zda_marks[par])
                    E.op(E.vec, "tensor_mul", z3,
                         dslm[:, :, None, :].broadcast_to([128, 16, NST, sln]),
                         Aslm[:, :, :, None].broadcast_to([128, 16, NST, sln]),
                         inc=E.s_v)
                    m_z = E.s_v.n
                    E.wait(E.sca, E.s_v, m_z)
                    E.op(E.sca, "activation", ubuf[par][:, 0:256 * sln],
                         zbuf[par][:, 0:256 * sln], ACT.Tanh, scale=1.0 / 16.0)
                    E.drain(E.sca)
                    dar = dabuf[par][:, 0:256 * S].rearrange(
                        "p (r t) -> p r t", t=S)
                    E.op(E.sca, "activation", dar[:, :, 1:1 + sln],
                         ubuf[par][:, 0:256 * sln].rearrange(
                             "p (r t) -> p r t", t=sln),
                         ACT.Exp, scale=-16.0, inc=E.s_s)
                    m_da = E.s_s.n

                    bt3 = btbuf[:, 0:256 * S].rearrange(
                        "p (dc bn t) -> p dc bn t", dc=DC, bn=32)
                    E.wait(E.sca, E.s_s, m_grp_evac)
                    E.wait(E.sca, E.s_v, m_bt_free)
                    E.op(E.sca, "copy", bt3[:, :, :, 1:1 + sln],
                         bcB[:, None, :, lo:lo + sln].broadcast_to(
                             [128, DC, 32, sln]), inc=E.s_s)
                    m_btcp = E.s_s.n
                    wsl = wxv[:, :, :, st0 - tc0:st0 - tc0 + sln]
                    bta = btbuf[:, 0:256 * S].rearrange(
                        "p (a n t) -> p a n t", a=16, n=NST)
                    E.wait(E.vec, E.s_s, m_btcp)
                    E.op(E.vec, "tensor_mul", bta[:, :, :, 1:1 + sln],
                         bta[:, :, :, 1:1 + sln],
                         wsl.rearrange("p dc b t -> p (dc b) t")[
                             :, :, None, :].broadcast_to([128, 16, NST, sln]))
                    btc0 = btbuf[:, 0:256 * S].rearrange(
                        "p (dc b n t) -> p dc b n t",
                        dc=DC, b=BL, n=NST)[:, :, :, :, 0]
                    if first_sub_global:
                        E.op(E.vec, "memset", btc0, 0.0)
                    else:
                        pvr = Hbuf[prev_par][:, 0:256 * prev_S].rearrange(
                            "p (dc b n t) -> p dc b n t",
                            dc=DC, b=BL, n=NST)[:, :, :, :, prev_S - 1]
                        if prev_flag:
                            E.wait(E.vec, E.s_v, m_scale_sb)
                            E.op(E.vec, "tensor_mul", btc0, pvr,
                                 scale_sb[:, None, 0:BL, None].broadcast_to(
                                     [128, DC, BL, NST]))
                        else:
                            E.op(E.vec, "tensor_copy", btc0, pvr)
                    E.drain(E.vec)
                    E.wait(E.vec, E.s_s, m_da)
                    if hq_marks[par]:
                        E.wait(E.vec, E.s_g, hq_marks[par])
                    E.op(E.vec, "tensor_tensor_scan",
                         Hbuf[par][:, 0:256 * S], dabuf[par][:, 0:256 * S],
                         btbuf[:, 0:256 * S], 0.0, ALU.mult, ALU.add,
                         inc=E.s_v)
                    m_bt_free = E.s_v.n
                    zda_marks[par] = E.s_v.n
                    m_tts = E.s_v.n
                    E.drain(E.vec)

                    E.wait(E.gps, E.s_v, m_tts)
                    E.wait(E.gps, E.s_s, m_grp_evac)
                    if ymul_done[par]:
                        E.wait(E.gps, E.s_v, ymul_done[par])
                    y4 = ytmp[par][:, 0:256 * sln].rearrange(
                        "p (dc bn t) -> p dc bn t", dc=DC, bn=32)
                    H4 = Hbuf[par][:, 0:256 * S].rearrange(
                        "p (dc bn t) -> p dc bn t", dc=DC, bn=32)
                    E.op(E.gps, "tensor_mul", y4,
                         H4[:, :, :, 1:1 + sln],
                         bcC[:, None, :, lo:lo + sln].broadcast_to(
                             [128, DC, 32, sln]))
                    E.drain(E.gps)
                    E.op(E.gps, "engine_nop", inc=E.s_g)
                    hq_marks[par] = E.s_g.n
                    m_ymul = E.s_g.n

                    E.wait(E.vec, E.s_g, m_ymul)
                    yin = ytmp[par][:, 0:256 * sln].rearrange(
                        "p (a n t) -> p a t n", a=16, n=NST)
                    yout = yv[:, :, :, st0 - tc0:st0 - tc0 + sln].rearrange(
                        "p dc b t -> p (dc b) t")
                    E.op(E.vec, "tensor_reduce", yout, yin, AX.X, ALU.add,
                         inc=E.s_v)
                    ymul_done[par] = E.s_v.n
                    E.drain(E.vec)

                    if flag:
                        Hc = Hbuf[par][:, 0:256 * S].rearrange(
                            "p (dc b n t) -> p dc b n t",
                            dc=DC, b=BL, n=NST)[:, :, :, :, S - 1]
                        outc = smalls[:, 2176:2176 + 256].rearrange(
                            "p (b dc n) -> p b dc n", b=BL, dc=DC)
                        E.op(E.vec, "tensor_mul", outc,
                             Hc.transpose([0, 2, 1, 3]),
                             Hc.transpose([0, 2, 1, 3]))
                        E.drain(E.vec)
                        comb = smalls[:, 2176:2176 + 256].rearrange(
                            "p (b rest) -> p b rest", b=BL)
                        red = smalls[:, 2432:2432 + BL]
                        E.op(E.vec, "tensor_reduce", red, comb, AX.X, ALU.add)
                        E.drain(E.vec)
                        redb = smalls[:, 2440:2444].bitcast(BF16)[:, 0:BL]
                        E.op(E.vec, "tensor_copy", redb, red, inc=E.s_v)
                        m_hsq = E.s_v.n
                        E.wait(E.ten, E.s_v, m_hsq)
                        E.op(E.ten, "matmul", pB[0:1, 512:512 + BL],
                             onescol[:, :], redb, start=True, stop=True,
                             inc=E.s_t)
                        m_nmm = E.s_t.n
                        E.wait(E.vec, E.s_t, m_nmm)
                        nr = smalls[0:1, 2448:2448 + BL]
                        E.op(E.vec, "tensor_copy", nr, pB[0:1, 512:512 + BL],
                             inc=E.s_v)
                        E.drain(E.vec)
                        g1 = smalls[0:1, 2456:2456 + BL]
                        g2 = smalls[0:1, 2464:2464 + BL]
                        emit_rsqrt(E, E.vec, g1, nr, g1, g2, iters=2)
                        E.op(E.vec, "tensor_scalar", out=g1, in0=g1,
                             scalar1=20.0, scalar2=1.0, op0=ALU.mult,
                             op1=ALU.min)
                        E.drain(E.vec)
                        sch = hlrows[0:1, 2048:2048 + BL]
                        scl = hlrows[0:1, 2056:2056 + BL]
                        E.op(E.vec, "tensor_copy", sch, g1)
                        E.drain(E.vec)
                        E.op(E.vec, "tensor_sub", g2, g1, sch)
                        E.drain(E.vec)
                        E.op(E.vec, "tensor_copy", scl, g2, inc=E.s_v)
                        m_schl = E.s_v.n
                        E.wait(E.ten, E.s_v, m_schl)
                        E.op(E.ten, "matmul", pB[:, 768:768 + BL], ones1[:, :],
                             sch, start=True, stop=False)
                        E.op(E.ten, "matmul", pB[:, 768:768 + BL], ones1[:, :],
                             scl, start=False, stop=True, inc=E.s_t)
                        m_scready = E.s_t.n
                        E.wait(E.vec, E.s_t, m_scready)
                        E.op(E.vec, "tensor_copy", scale_sb[:, 0:BL],
                             pB[:, 768:768 + BL])
                        E.drain(E.vec)
                        ycol = yv[:, :, :, st0 - tc0 + sln - 1]
                        E.op(E.vec, "tensor_mul", ycol, ycol,
                             scale_sb[:, None, 0:BL].broadcast_to(
                                 [128, DC, BL]))
                        E.op(E.vec, "engine_nop", inc=E.s_v)
                        E.drain(E.vec)
                        m_scale_sb = E.s_v.n

                    prev_par = par
                    prev_S = S
                    prev_flag = flag
                    first_sub_global = False
                    sub_idx[0] += 1

                E.op(E.vec, "engine_nop", inc=E.s_v)
                m_grp_consumed = E.s_v.n

            xs = xbv[:, :, :, 0:tcn]
            Dbc = bview[:, 1, :]
            tmp = sp1.rearrange("p (dc b t) -> p dc b t", dc=DC, b=BL)[
                :, :, :, 0:tcn]
            E.op(E.vec, "tensor_mul", tmp, lnv[:, :, :, 0:tcn],
                 Dbc[:, :, None, None].broadcast_to([128, DC, BL, tcn]))
            E.drain(E.vec)
            E.op(E.vec, "tensor_add", xs, xs, yv[:, :, :, 0:tcn])
            E.drain(E.vec)
            E.op(E.vec, "tensor_add", xs, xs, tmp, inc=E.s_v)
            E.drain(E.vec)
            m_xres = E.s_v.n
            E.wait(E.sync, E.s_v, m_xres)
            for b in range(BL):
                E.op(E.sync, "dma_start", out=dst_aa[:, :, b, tc0:tc0 + tcn],
                     in_=xbv[:, :, b, 0:tcn], inc=E.s_x, inck=16)
            m_xst = E.s_x.n

        E.wait(E.sync, E.s_v, E.s_v.n)
        E.wait(E.sync, E.s_s, E.s_s.n)
        if "aa_ln1" in dbg_outs:
            E.op(E.sync, "dma_start", out=dbg_outs["aa_ln1"], in_=ln1[:, :],
                 inc=E.s_x, inck=16)
        if "aa_dt" in dbg_outs:
            E.op(E.sync, "dma_start", out=dbg_outs["aa_dt"], in_=dt_b[:, :],
                 inc=E.s_x, inck=16)
        if "aa_y" in dbg_outs:
            E.op(E.sync, "dma_start", out=dbg_outs["aa_y"], in_=ybuf[:, :],
                 inc=E.s_x, inck=16)
        E.barrier()
        ph.close()

    def emit_ffn(tag):
        FT = 512
        TOK = BL * FT
        ph = ExitStack()

        def psb(name, shape, dt=F32):
            return ph.enter_context(nc.sbuf_tensor(tag + name, list(shape), dt))

        wslot = [psb(f"w{i}", [128, 4096], BF16) for i in range(4)]
        shg = psb("shg", [128, 4096], BF16)
        shu = psb("shu", [128, 4096], BF16)
        shd = psb("shd", [128, 4096], BF16)
        ln2b = psb("ln2b", [128, DC * TOK], BF16)
        sqb2 = psb("sqb", [128, DC * BL * 256], BF16)
        ybuf = psb("y", [128, DC * TOK])
        interb = psb("inter", [128, IC * FT], BF16)
        sgbuf = psb("sg", [128, 2 * FT])
        uwbuf = psb("uw", [128, 2 * FT])
        wbcs = psb("wbc", [128, 2 * TOK])
        gbuf = psb("g", [128, 1536])
        wfm = psb("wfm", [NST, TOK])
        xbuf_f = psb("x", [128, DC * TOK])
        wfmh = psb("wfmh", [NST, 2 * TOK], BF16)
        whl = psb("whl", [128, 16 * NST], BF16)

        E.op(E.gps, "dma_start",
             out=shg.rearrange("p (kc m) -> p kc m", kc=DC),
             in_=sgT.rearrange("(kc p) m -> p kc m", p=128), inc=E.s_wsh, inck=16)
        E.op(E.gps, "dma_start",
             out=shu.rearrange("p (kc m) -> p kc m", kc=DC),
             in_=suT.rearrange("(kc p) m -> p kc m", p=128), inc=E.s_wsh, inck=16)
        E.op(E.gps, "dma_start",
             out=shd.rearrange("p (kc m) -> p kc m", kc=IC),
             in_=sdT.rearrange("(kc p) m -> p kc m", p=128), inc=E.s_wsh, inck=16)
        m_shw = E.s_wsh.n

        n_chunks = (Ll + FT - 1) // FT
        slot_free = [0, 0, 0, 0]
        wm = {}
        sidx = [0]

        def stream_mat(ci, e, j):
            src = (gprojT, uprojT, dprojT)[j][e]
            sl = sidx[0] % 4
            sidx[0] += 1
            E.wait(E.gps, E.s_t, slot_free[sl])
            kcn = DC if j < 2 else IC
            E.op(E.gps, "dma_start",
                 out=wslot[sl].rearrange("p (kc m) -> p kc m", kc=kcn),
                 in_=src.rearrange("(kc p) m -> p kc m", p=128),
                 inc=E.s_wsl[sl], inck=16)
            wm[(ci, e, j)] = (sl, E.s_wsl[sl].n)

        if dbg_outs:
            E.op(E.vec, "memset", ln2b[:, :].bitcast(I32), 0)
            E.op(E.vec, "memset", ybuf[:, :], 0.0)
            E.op(E.vec, "memset", gbuf[:, :], 0.0)
            E.drain(E.vec)

        for j in range(3):
            stream_mat(0, 0, j)

        src_f, dst_f = phase_io(tag[:2])
        m_xst = 0
        for ci in range(n_chunks):
            tc0 = ci * FT
            tcn = min(FT, Ll - tc0)
            TTOK = BL * tcn

            xbv = xbuf_f.rearrange("p (dc b t) -> p dc b t", dc=DC, b=BL)
            if m_xst:
                E.wait(E.sync, E.s_x, m_xst)
            for b in range(BL):
                E.op(E.sync, "dma_start", out=xbv[:, :, b, 0:tcn],
                     in_=src_f[:, :, b, tc0:tc0 + tcn], inc=E.s_x, inck=16)
            m_xld = E.s_x.n
            for h0 in range(0, tcn, 256):
                hn = min(256, tcn - h0)
                m_ln = emit_ln(tc0 + h0, hn, ybuf, ln2b, sqb2, toff=h0,
                               xs=xbv[:, :, :, h0:h0 + hn], wait_x=m_xld,
                               pre_wait_t=(E.last_ln_mst if h0 > 0 else None))
            lbv = ln2b.rearrange("p (dc b t) -> p dc b t", dc=DC, b=BL)

            E.wait(E.vec, E.s_v, m_ln)
            E.op(E.vec, "memset", ybuf[:, :], 0.0, inc=E.s_v)
            E.drain(E.vec)
            m_y0 = E.s_v.n

            tpb = (tcn + 127) // 128
            ntile = BL * tpb
            E.op(E.vec, "memset", pB[:, 0:128], 0.0, inc=E.s_v)
            E.drain(E.vec)
            E.wait(E.ten, E.s_v, E.s_v.n)
            E.wait(E.ten, E.s_v, m_y0)
            for ti in range(ntile):
                b, tq = ti // tpb, ti % tpb
                tl = min(128, tcn - tq * 128)
                dst = pB[0:tl, ti * NST:(ti + 1) * NST]
                for kc in range(DC):
                    E.op(E.ten, "matmul", dst,
                         lbv[:, kc, b, tq * 128:tq * 128 + tl],
                         gw_sb.rearrange("p (kc e) -> p kc e", kc=DC)[:, kc, :],
                         start=(kc == 0), stop=(kc == DC - 1),
                         inc=(E.s_t if (ti == ntile - 1 and kc == DC - 1)
                              else None))
            m_glog = E.s_t.n

            gat = gbuf[:, 0:128]
            E.wait(E.sca, E.s_t, m_glog)
            E.op(E.sca, "activation", gat, pB[:, 0:128], ACT.Tanh, scale=0.5,
                 inc=E.s_s)
            m_gth = E.s_s.n
            E.wait(E.vec, E.s_s, m_gth)
            E.op(E.vec, "tensor_scalar", out=gat, in0=gat, scalar1=0.5,
                 scalar2=0.5, op0=ALU.mult, op1=ALU.add)
            E.drain(E.vec)
            sc = gbuf[:, 128:256]
            E.op(E.vec, "tensor_tensor",
                 sc.rearrange("p (ti e) -> p ti e", e=NST),
                 gat.rearrange("p (ti e) -> p ti e", e=NST),
                 ebias_sb[:, None, :].broadcast_to([128, 8, NST]), ALU.add)
            E.drain(E.vec)
            scg = sc.rearrange("p (ti g j) -> p ti g j", g=4, j=4)
            Mx = gbuf[:, 256:320].rearrange("p (ti g j) -> p ti g j", g=4, j=2)
            mn = gbuf[:, 320:384].rearrange("p (ti g j) -> p ti g j", g=4, j=2)
            E.op(E.vec, "tensor_max", Mx, scg[:, :, :, 0:2], scg[:, :, :, 2:4])
            E.op(E.vec, "tensor_tensor", mn, scg[:, :, :, 0:2],
                 scg[:, :, :, 2:4], ALU.min)
            E.drain(E.vec)
            gsv = gbuf[:, 384:416].rearrange("p (ti g) -> p ti g", g=4)
            t1g = gbuf[:, 416:448].rearrange("p (ti g) -> p ti g", g=4)
            t2g = gbuf[:, 448:480].rearrange("p (ti g) -> p ti g", g=4)
            E.op(E.vec, "tensor_max", gsv, Mx[:, :, :, 0], Mx[:, :, :, 1])
            E.op(E.vec, "tensor_tensor", t1g, Mx[:, :, :, 0], Mx[:, :, :, 1],
                 ALU.min)
            E.op(E.vec, "tensor_max", t2g, mn[:, :, :, 0], mn[:, :, :, 1])
            E.drain(E.vec)
            E.op(E.vec, "tensor_max", t1g, t1g, t2g)
            E.drain(E.vec)
            E.op(E.vec, "tensor_add", gsv, gsv, t1g)
            E.drain(E.vec)
            gm1 = gbuf[:, 480:488]
            hit = gbuf[:, 488:520].rearrange("p (ti g) -> p ti g", g=4)
            cums = gbuf[:, 520:552].rearrange("p (ti g) -> p ti g", g=4)
            c2 = gbuf[:, 552:584].rearrange("p (ti g) -> p ti g", g=4)
            gmask = t2g
            E.op(E.vec, "memset", gmask, 0.0)
            E.drain(E.vec)
            for _ in range(2):
                E.op(E.vec, "tensor_reduce", gm1, gsv, AX.X, ALU.max)
                E.drain(E.vec)
                E.op(E.vec, "tensor_tensor", hit, gsv,
                     gm1.rearrange("p ti -> p ti")[:, :, None].broadcast_to(
                         [128, 8, 4]), ALU.is_equal)
                E.drain(E.vec)
                E.op(E.vec, "tensor_copy", cums[:, :, 0:1], hit[:, :, 0:1])
                E.op(E.vec, "tensor_add", cums[:, :, 1:4], hit[:, :, 1:4],
                     hit[:, :, 0:3])
                E.drain(E.vec)
                E.op(E.vec, "tensor_copy", c2[:, :, 0:2], cums[:, :, 0:2])
                E.op(E.vec, "tensor_add", c2[:, :, 2:4], cums[:, :, 2:4],
                     cums[:, :, 0:2])
                E.drain(E.vec)
                E.op(E.vec, "tensor_single_scalar", c2, c2, 1.0, ALU.is_equal)
                E.drain(E.vec)
                E.op(E.vec, "tensor_mul", hit, hit, c2)
                E.drain(E.vec)
                E.op(E.vec, "tensor_add", gmask, gmask, hit)
                E.drain(E.vec)
                E.op(E.vec, "scalar_tensor_tensor", out=gsv, in0=hit,
                     scalar=-1e30, in1=gsv, op0=ALU.mult, op1=ALU.add)
                E.drain(E.vec)
            tmp = gbuf[:, 584:712].rearrange("p (ti g j) -> p ti g j", g=4, j=4)
            E.op(E.vec, "tensor_mul", tmp, scg,
                 gmask[:, :, :, None].broadcast_to([128, 8, 4, 4]))
            E.drain(E.vec)
            tmpv = gbuf[:, 584:712].rearrange("p (ti e) -> p ti e", e=NST)
            sel = gbuf[:, 712:840].rearrange("p (ti e) -> p ti e", e=NST)
            m16 = gbuf[:, 840:848]
            h16 = gbuf[:, 848:976].rearrange("p (ti e) -> p ti e", e=NST)
            c16a = gbuf[:, 976:1104].rearrange("p (ti e) -> p ti e", e=NST)
            c16b = gbuf[:, 1104:1232].rearrange("p (ti e) -> p ti e", e=NST)
            E.op(E.vec, "memset", sel, 0.0)
            E.drain(E.vec)
            for _ in range(4):
                E.op(E.vec, "tensor_reduce", m16, tmpv, AX.X, ALU.max)
                E.drain(E.vec)
                E.op(E.vec, "tensor_tensor", h16, tmpv,
                     m16[:, :, None].broadcast_to([128, 8, NST]), ALU.is_equal)
                E.drain(E.vec)
                srcc, dstc = h16, c16a
                for sh in (1, 2, 4, 8):
                    E.op(E.vec, "tensor_copy", dstc[:, :, 0:sh],
                         srcc[:, :, 0:sh])
                    E.op(E.vec, "tensor_add", dstc[:, :, sh:NST],
                         srcc[:, :, sh:NST], srcc[:, :, 0:NST - sh])
                    E.drain(E.vec)
                    srcc, dstc = dstc, (c16b if dstc is c16a else c16a)
                E.op(E.vec, "tensor_single_scalar", srcc, srcc, 1.0,
                     ALU.is_equal)
                E.drain(E.vec)
                E.op(E.vec, "tensor_mul", h16, h16, srcc)
                E.drain(E.vec)
                E.op(E.vec, "tensor_add", sel, sel, h16)
                E.drain(E.vec)
                E.op(E.vec, "scalar_tensor_tensor", out=tmpv, in0=h16,
                     scalar=-1e30, in1=tmpv, op0=ALU.mult, op1=ALU.add)
                E.drain(E.vec)
            wre = gbuf[:, 1232:1360].rearrange("p (ti e) -> p ti e", e=NST)
            rs = gbuf[:, 1360:1368]
            E.op(E.vec, "tensor_mul", wre, sel,
                 gat.rearrange("p (ti e) -> p ti e", e=NST))
            E.drain(E.vec)
            E.op(E.vec, "tensor_reduce", rs, wre, AX.X, ALU.add)
            E.drain(E.vec)
            E.op(E.vec, "tensor_scalar_add", rs, rs, 1e-20)
            E.drain(E.vec)
            E.op(E.vec, "reciprocal", rs, rs)
            E.drain(E.vec)
            E.op(E.vec, "scalar_tensor_tensor", out=wre, in0=wre, scalar=2.5,
                 in1=rs.rearrange("p ti -> p ti")[:, :, None].broadcast_to(
                     [128, 8, NST]), op0=ALU.mult, op1=ALU.mult)
            E.drain(E.vec)

            for ti in range(ntile):
                hi = whl[:, 2 * ti * NST:(2 * ti + 1) * NST]
                lo = whl[:, (2 * ti + 1) * NST:(2 * ti + 2) * NST]
                E.op(E.vec, "tensor_copy", hi, wre[:, ti, :])
                E.drain(E.vec)
                E.op(E.vec, "tensor_sub", gbuf[:, 1368:1384],
                     wre[:, ti, :], hi)
                E.drain(E.vec)
                E.op(E.vec, "tensor_copy", lo,
                     gbuf[:, 1368:1384], inc=E.s_v)
                m_whl = E.s_v.n
                E.wait(E.ten, E.s_v, m_whl)
                E.wait(E.ten, E.s_s, m_gth)
                dst = pB[0:NST, ti * 128:(ti + 1) * 128]
                E.op(E.ten, "matmul", dst, hi, ident[:, :],
                     start=True, stop=False)
                E.op(E.ten, "matmul", dst, lo, ident[:, :],
                     start=False, stop=True,
                     inc=(E.s_t if ti == ntile - 1 else None))
            m_wtr = E.s_t.n
            E.wait(E.vec, E.s_t, m_wtr)
            for ti in range(ntile):
                b, tq = ti // tpb, ti % tpb
                tl = min(128, tcn - tq * 128)
                E.op(E.vec, "tensor_copy",
                     wfm[:, b * tcn + tq * 128:b * tcn + tq * 128 + tl],
                     pB[0:NST, ti * 128:ti * 128 + tl])
            E.drain(E.vec)
            E.op(E.vec, "tensor_copy", wfmh[:, 0:TTOK], wfm[:, 0:TTOK])
            E.drain(E.vec)
            E.op(E.vec, "tensor_sub", wfm[:, 0:TTOK], wfm[:, 0:TTOK],
                 wfmh[:, 0:TTOK])
            E.drain(E.vec)
            E.op(E.vec, "tensor_copy", wfmh[:, TOK:TOK + TTOK], wfm[:, 0:TTOK],
                 inc=E.s_v)
            E.drain(E.vec)
            m_wfm = E.s_v.n

            wbc_used = [0, 0]
            m_int_done = 0
            sg_marks = [0, 0]
            uw_marks = [m_wfm, m_wfm]
            int_marks = [0, 0]
            m_yadd = m_y0
            iv = interb.rearrange("p (ic t) -> p ic t", ic=IC)
            for e in range(NE + 1):
                shared = (e == NE)
                if not shared:
                    epar = e % 2
                    E.wait(E.ten, E.s_v, m_wfm)
                    if wbc_used[epar]:
                        E.wait(E.ten, E.s_v, wbc_used[epar])
                    nq = (TTOK + 511) // 512
                    for q in range(nq):
                        qn = min(512, TTOK - q * 512)
                        dst = pD[:, q * 512:q * 512 + qn]
                        E.op(E.ten, "matmul", dst,
                             sel16[:, e * 128:(e + 1) * 128],
                             wfmh[:, q * 512:q * 512 + qn],
                             start=True, stop=False)
                        E.op(E.ten, "matmul", dst,
                             sel16[:, e * 128:(e + 1) * 128],
                             wfmh[:, TOK + q * 512:TOK + q * 512 + qn],
                             start=False, stop=True,
                             inc=(E.s_t if q == nq - 1 else None))
                    m_wbc = E.s_t.n
                    E.wait(E.vec, E.s_t, m_wbc)
                    E.op(E.vec, "tensor_copy", wbcs[:, epar * TOK:epar * TOK + TTOK],
                         pD[:, 0:TTOK], inc=E.s_v)
                    E.drain(E.vec)
                    m_wbcs_evac = E.s_v.n
                    slg, mg = wm[(ci, e, 0)]
                    slu, mu = wm[(ci, e, 1)]
                    sld, md = wm[(ci, e, 2)]
                    gv = wslot[slg].rearrange("p (kc m) -> p kc m", kc=DC)
                    uv = wslot[slu].rearrange("p (kc m) -> p kc m", kc=DC)
                    dv = wslot[sld].rearrange("p (kc m) -> p kc m", kc=IC)
                else:
                    E.wait(E.ten, E.s_wsh, m_shw)
                    gv = shg.rearrange("p (kc m) -> p kc m", kc=DC)
                    uv = shu.rearrange("p (kc m) -> p kc m", kc=DC)
                    dv = shd.rearrange("p (kc m) -> p kc m", kc=IC)

                for b in range(BL):
                    for icq in range(IC):
                        gp = pA[:, (icq % 2) * 512:(icq % 2) * 512 + tcn]
                        up = pB[:, (icq % 2) * 512:(icq % 2) * 512 + tcn]
                        if not shared:
                            E.wait(E.ten, E.s_wsl[slg], mg)
                            E.wait(E.ten, E.s_wsl[slu], mu)
                        if sg_marks[icq % 2]:
                            E.wait(E.ten, E.s_s, sg_marks[icq % 2])
                        for kc in range(DC):
                            E.op(E.ten, "matmul", gp,
                                 gv[:, kc, icq * 128:(icq + 1) * 128],
                                 lbv[:, kc, b, 0:tcn],
                                 start=(kc == 0), stop=(kc == DC - 1),
                                 inc=(E.s_t if kc == DC - 1 else None))
                        m_gmm = E.s_t.n
                        if uw_marks[icq % 2]:
                            E.wait(E.ten, E.s_v, uw_marks[icq % 2])
                        for kc in range(DC):
                            E.op(E.ten, "matmul", up,
                                 uv[:, kc, icq * 128:(icq + 1) * 128],
                                 lbv[:, kc, b, 0:tcn],
                                 start=(kc == 0), stop=(kc == DC - 1),
                                 inc=(E.s_t if kc == DC - 1 else None))
                        m_umm = E.s_t.n
                        sgsl = sgbuf[:, (icq % 2) * FT:(icq % 2) * FT + tcn]
                        uwsl = uwbuf[:, (icq % 2) * FT:(icq % 2) * FT + tcn]
                        E.wait(E.sca, E.s_t, m_gmm)
                        if int_marks[icq % 2]:
                            E.wait(E.sca, E.s_v, int_marks[icq % 2])
                        E.op(E.sca, "activation", sgsl, gp, ACT.Silu,
                             inc=E.s_s)
                        sg_marks[icq % 2] = E.s_s.n
                        E.wait(E.vec, E.s_t, m_umm)
                        if not shared:
                            E.op(E.vec, "tensor_mul", uwsl, up,
                                 wbcs[:, epar * TOK + b * tcn:
                                      epar * TOK + b * tcn + tcn], inc=E.s_v)
                        else:
                            E.op(E.vec, "tensor_copy", uwsl, up, inc=E.s_v)
                        uw_marks[icq % 2] = E.s_v.n
                        E.drain(E.vec)
                        E.wait(E.vec, E.s_s, sg_marks[icq % 2])
                        E.op(E.vec, "tensor_mul", iv[:, icq, 0:tcn],
                             sgsl, uwsl, inc=E.s_v)
                        E.drain(E.vec)
                        int_marks[icq % 2] = E.s_v.n
                    m_int_done = E.s_v.n

                    if not shared:
                        E.wait(E.ten, E.s_wsl[sld], md)
                    E.wait(E.ten, E.s_v, m_int_done)
                    yv = ybuf.rearrange("p (hc b t) -> p hc b t", hc=HC, b=BL)
                    for hc in range(HC):
                        off = (hc % 2) * 512
                        if hc >= 2:
                            E.wait(E.ten, E.s_v, yadd_marks[hc - 2])
                        elif m_yadd:
                            E.wait(E.ten, E.s_v, m_yadd)
                        if hc == 0:
                            yadd_marks = [0] * HC
                        for kci in range(IC):
                            E.op(E.ten, "matmul", pC[:, off:off + tcn],
                                 dv[:, kci, hc * 128:(hc + 1) * 128],
                                 iv[:, kci, 0:tcn],
                                 start=(kci == 0), stop=(kci == IC - 1),
                                 inc=(E.s_t if kci == IC - 1 else None))
                        m_dmm = E.s_t.n
                        E.wait(E.vec, E.s_t, m_dmm)
                        E.op(E.vec, "tensor_add", yv[:, hc, b, 0:tcn],
                             yv[:, hc, b, 0:tcn], pC[:, off:off + tcn],
                             inc=E.s_v)
                        E.drain(E.vec)
                        yadd_marks[hc] = E.s_v.n
                    m_yadd = E.s_v.n

                if not shared:
                    wbc_used[epar] = uw_marks[(IC - 1) % 2]
                    slot_free[slg] = m_gmm
                    slot_free[slu] = m_umm
                    slot_free[sld] = m_dmm
                    if e + 1 < NE:
                        for j in range(3):
                            stream_mat(ci, e + 1, j)
                    elif ci + 1 < n_chunks:
                        for j in range(3):
                            stream_mat(ci + 1, 0, j)

            if ci == 0 and ("f_w" in dbg_outs):
                E.wait(E.sync, E.s_v, E.s_v.n)
                E.op(E.sync, "dma_start", out=dbg_outs["f_w"],
                     in_=gbuf[:, 1232:1360], inc=E.s_x, inck=16)
                E.op(E.sync, "dma_start", out=dbg_outs["f_y"],
                     in_=ybuf[:, 0:DC * TOK], inc=E.s_x, inck=16)
                E.op(E.sync, "dma_start", out=dbg_outs["f_ln"],
                     in_=ln2b[:, :].bitcast(I32).bitcast(F32)[:, 0:DC * TOK // 2],
                     inc=E.s_x, inck=16)
                E.wait(E.sync, E.s_x, E.s_x.n)
            xs = xbv[:, :, :, 0:tcn]
            yvv = ybuf.rearrange("p (hc b t) -> p hc b t", hc=HC, b=BL)
            E.wait(E.vec, E.s_v, m_yadd)
            E.op(E.vec, "tensor_add", xs, xs, yvv[:, :, :, 0:tcn], inc=E.s_v)
            E.drain(E.vec)
            m_xres = E.s_v.n
            E.wait(E.sync, E.s_v, m_xres)
            for b in range(BL):
                E.op(E.sync, "dma_start", out=dst_f[:, :, b, tc0:tc0 + tcn],
                     in_=xbv[:, :, b, 0:tcn], inc=E.s_x, inck=16)
            m_xst = E.s_x.n

        E.barrier()
        ph.close()
    if "f1" in phases:
        emit_ffn("f1_")

    def emit_attn_b():
        BT = 96
        ph = ExitStack()

        def psb(name, shape, dt=F32):
            return ph.enter_context(nc.sbuf_tensor("ab" + name, list(shape), dt))

        CW = BL * BT
        SR = BT + 1
        wr_sb = psb("wr", [128, DC * D], BF16)
        wi_sb = psb("wi", [128, DC * D], BF16)
        wo_sb = psb("wo", [128, DC * D], BF16)
        ext = psb("ext", [128, DC * BL * (BT + 3)])
        xbuf_b = psb("x", [128, DC * CW])
        ln3b = psb("ln3b", [128, DC * CW], BF16)
        sqb3 = psb("sqb3", [128, DC * CW], BF16)
        xc = psb("xc", [128, DC * CW])
        xcb = psb("xcb", [128, DC * CW], BF16)
        thr = psb("thr", [128, DC * CW])
        thi = psb("thi", [128, DC * CW])
        u8 = psb("u8", [128, DC * CW])
        abuf = psb("a", [128, DC * CW])
        btb = psb("bt", [128, DC * CW])
        tmpb = psb("tmp", [128, DC * CW])
        Ab = psb("A", [128, DC * CW])
        hbf = psb("hbf", [128, DC * CW], BF16)
        obuf = psb("o", [128, DC * CW])
        la_s = psb("las", [128, 16 * SR])
        q2_s = psb("q2s", [128, 16 * SR])
        lacum = psb("lacum", [128, 16 * SR])
        q2cum = psb("q2cum", [128, 16 * SR])
        ones_t = psb("onest", [128, 16 * SR])
        carry2 = psb("carry2", [128, 32])

        for dst, src in ((wr_sb, w_rT), (wi_sb, w_iT), (wo_sb, w_oT)):
            E.op(E.gps, "dma_start",
                 out=dst.rearrange("p (kc m) -> p kc m", kc=DC),
                 in_=src.rearrange("(kc p) m -> p kc m", p=128),
                 inc=E.s_w, inck=16)
        m_wab = E.s_w.n

        E.op(E.vec, "tensor_single_scalar", bview[:, 3, :], bview[:, 3, :],
             0.5, ALU.mult)
        E.op(E.vec, "tensor_single_scalar", bview[:, 4, :], bview[:, 4, :],
             0.5, ALU.mult)
        E.op(E.vec, "tensor_single_scalar", bview[:, 5, :], bview[:, 5, :],
             0.5, ALU.mult)
        E.drain(E.vec)
        E.op(E.vec, "memset", ones_t[:, :], 1.0)
        E.drain(E.vec)
        E.op(E.vec, "memset",
             ones_t.rearrange("p (r c) -> p r c", c=SR)[:, :, 0:1], 0.0)
        E.op(E.vec, "memset", carry2[:, :], 0.0)
        E.op(E.vec, "memset",
             ext.rearrange("p (dc b t) -> p dc b t", dc=DC, b=BL)[:, :, :, 0:3],
             0.0, inc=E.s_v)
        E.drain(E.vec)
        if dbg_outs:
            for bfr in (obuf, xc, u8, abuf, btb, Ab):
                E.op(E.vec, "memset", bfr[:, :], 0.0)
            E.drain(E.vec)

        src_b, dst_b = phase_io("ab")
        m_xst = 0
        n_chunks = (Ll + BT - 1) // BT
        sp_prev = None
        cur_sr = [BT + 1]

        def set_sr(srn):
            if cur_sr[0] != srn:
                cur_sr[0] = srn
                E.op(E.vec, "memset", ones_t[:, 0:16 * srn], 1.0)
                E.drain(E.vec)
                E.op(E.vec, "memset", ones_t[:, 0:16 * srn].rearrange(
                    "p (r c) -> p r c", c=srn)[:, :, 0:1], 0.0)
                E.drain(E.vec)
        for ci in range(n_chunks):
            tc0 = ci * BT
            tcn = min(BT, Ll - tc0)
            TT = BL * tcn
            extv = ext.rearrange("p (dc b t) -> p dc b t", dc=DC, b=BL)
            if ci > 0:
                E.op(E.vec, "tensor_copy", extv[:, :, :, 0:3],
                     extv[:, :, :, prev_tcn:prev_tcn + 3])
                E.drain(E.vec)
            prev_tcn = tcn
            xrv = xbuf_b.rearrange("p (dc b t) -> p dc b t", dc=DC, b=BL)
            if m_xst:
                E.wait(E.sync, E.s_x, m_xst)
            for b in range(BL):
                E.op(E.sync, "dma_start", out=xrv[:, :, b, 0:tcn],
                     in_=src_b[:, :, b, tc0:tc0 + tcn], inc=E.s_x, inck=16)
            m_xld = E.s_x.n
            m_ln = emit_ln(tc0, tcn, None, ln3b, sqb3, toff=0,
                           f32view=extv[:, :, :, 3:3 + tcn],
                           pre_wait_t=(m_ms if ci > 0 else None),
                           xs=xrv[:, :, :, 0:tcn], wait_x=m_xld)

            xcv = xc.rearrange("p (dc b t) -> p dc b t", dc=DC, b=BL)[
                :, :, :, 0:tcn]
            tv = tmpb.rearrange("p (dc b t) -> p dc b t", dc=DC, b=BL)[
                :, :, :, 0:tcn]
            cwv = conv_w.rearrange("p (dc k) -> p dc k", dc=DC)
            E.wait(E.vec, E.s_v, m_ln)
            E.op(E.vec, "tensor_copy", xcv,
                 bview[:, 2, :, None, None].broadcast_to([128, DC, BL, tcn]))
            E.drain(E.vec)
            for k in range(4):
                E.op(E.vec, "tensor_mul", tv, extv[:, :, :, k:k + tcn],
                     cwv[:, :, k, None, None].broadcast_to([128, DC, BL, tcn]))
                E.drain(E.vec)
                E.op(E.vec, "tensor_add", xcv, xcv, tv)
                E.drain(E.vec)
            E.op(E.vec, "engine_nop", inc=E.s_v)
            m_xc = E.s_v.n
            E.wait(E.sca, E.s_v, m_xc)
            E.op(E.sca, "copy", xcb[:, 0:DC * TT] if TT == CW // 1 and False
                 else xcb.rearrange("p (dc b t) -> p dc b t", dc=DC, b=BL)[
                     :, :, :, 0:tcn],
                 xcv, inc=E.s_s)
            m_xcb = E.s_s.n

            xbv = xcb.rearrange("p (dc b t) -> p dc b t", dc=DC, b=BL)
            E.wait(E.ten, E.s_s, m_xcb)
            if ci == 0:
                E.wait(E.ten, E.s_w, m_wab)
            for wsb, bslot, dstb in ((wr_sb, 3, thr), (wi_sb, 4, thi)):
                wv = wsb.rearrange("p (kc m) -> p kc m", kc=DC)
                dv3 = dstb.rearrange("p (dc b t) -> p dc b t", dc=DC, b=BL)
                for oc in range(DC):
                    dst = pA[:, (oc % 2) * 512:(oc % 2) * 512 + TT].rearrange(
                        "p (b t) -> p b t", b=BL)
                    if sp_prev is not None:
                        E.wait(E.ten, E.s_s, sp_prev)
                    for kc in range(DC):
                        E.op(E.ten, "matmul", dst,
                             wv[:, kc, oc * 128:(oc + 1) * 128],
                             xbv[:, kc, :, 0:tcn],
                             start=(kc == 0), stop=(kc == DC - 1),
                             inc=(E.s_t if kc == DC - 1 else None))
                    m_mm = E.s_t.n
                    E.wait(E.sca, E.s_t, m_mm)
                    E.op(E.sca, "activation", dv3[:, oc, :, 0:tcn], dst,
                         ACT.Tanh, scale=0.5, bias=bview[:, bslot, oc:oc + 1],
                         inc=E.s_s)
                    if oc >= 1 or wsb is wi_sb:
                        sp_prev = E.s_s.n
                    else:
                        sp_prev = E.s_s.n
            m_ri = E.s_s.n

            thrv = thr.rearrange("p (dc b t) -> p dc b t", dc=DC, b=BL)[
                :, :, :, 0:tcn]
            u8v = u8.rearrange("p (dc b t) -> p dc b t", dc=DC, b=BL)[
                :, :, :, 0:tcn]
            E.wait(E.vec, E.s_s, m_ri)
            E.op(E.vec, "tensor_mul", u8v, thrv,
                 bview[:, 5, :, None, None].broadcast_to([128, DC, BL, tcn]))
            E.drain(E.vec)
            E.op(E.vec, "tensor_add", u8v, u8v,
                 bview[:, 5, :, None, None].broadcast_to([128, DC, BL, tcn]),
                 inc=E.s_v)
            E.drain(E.vec)
            m_u8 = E.s_v.n
            E.wait(E.sca, E.s_v, m_u8)
            av = abuf.rearrange("p (dc b t) -> p dc b t", dc=DC, b=BL)[
                :, :, :, 0:tcn]
            E.op(E.sca, "activation", av, u8v, ACT.Exp, inc=E.s_s)
            m_a = E.s_s.n
            lasv = la_s[:, 0:16 * (tcn + 1)].rearrange(
                "p (dc b c) -> p dc b c", dc=DC, b=BL)
            E.op(E.vec, "tensor_single_scalar", lasv[:, :, :, 1:1 + tcn],
                 u8v, LN17, ALU.max)
            E.drain(E.vec)
            c2v = carry2.rearrange("p (k dcb) -> p k dcb", k=2)
            E.op(E.vec, "tensor_copy",
                 lasv[:, :, :, 0].rearrange("p dc b -> p (dc b)"),
                 c2v[:, 0, :])
            E.drain(E.vec)
            set_sr(tcn + 1)
            E.op(E.vec, "tensor_tensor_scan", lacum[:, 0:16 * (tcn + 1)],
                 ones_t[:, 0:16 * (tcn + 1)],
                 la_s[:, 0:16 * (tcn + 1)], 0.0, ALU.mult, ALU.add)
            E.drain(E.vec)
            lcv = lacum[:, 0:16 * (tcn + 1)].rearrange(
                "p (dc b c) -> p dc b c", dc=DC, b=BL)
            E.op(E.vec, "tensor_copy", c2v[:, 0, :],
                 lcv[:, :, :, tcn].rearrange("p dc b -> p (dc b)"))
            E.drain(E.vec)
            E.op(E.vec, "tensor_single_scalar", lcv[:, :, :, 1:1 + tcn],
                 lcv[:, :, :, 1:1 + tcn], -80.0, ALU.max, inc=E.s_v)
            E.drain(E.vec)
            m_lc = E.s_v.n

            qv = tmpb.rearrange("p (dc b t) -> p dc b t", dc=DC, b=BL)[
                :, :, :, 0:tcn]
            E.wait(E.vec, E.s_s, m_a)
            E.op(E.vec, "tensor_mul", qv, av, av)
            E.drain(E.vec)
            E.op(E.vec, "tensor_scalar", out=qv, in0=qv, scalar1=-1.0,
                 scalar2=1.0, op0=ALU.mult, op1=ALU.add)
            E.drain(E.vec)
            E.op(E.vec, "tensor_scalar_max", qv, qv, 1e-6, inc=E.s_v)
            E.drain(E.vec)
            m_q = E.s_v.n
            E.wait(E.sca, E.s_v, m_q)
            E.op(E.sca, "activation", qv, qv, ACT.Ln)
            E.drain(E.sca)
            E.op(E.sca, "activation", qv, qv, ACT.Exp, scale=0.5, inc=E.s_s)
            m_sq = E.s_s.n
            Av = Ab.rearrange("p (dc b t) -> p dc b t", dc=DC, b=BL)[
                :, :, :, 0:tcn]
            E.wait(E.sca, E.s_v, m_lc)
            E.op(E.sca, "activation", Av, lcv[:, :, :, 1:1 + tcn], ACT.Exp)
            E.drain(E.sca)
            q2v = q2_s[:, 0:16 * (tcn + 1)].rearrange(
                "p (dc b c) -> p dc b c", dc=DC, b=BL)
            E.op(E.sca, "activation", q2v[:, :, :, 1:1 + tcn],
                 lcv[:, :, :, 1:1 + tcn], ACT.Exp, scale=-1.0, inc=E.s_s)
            m_enA = E.s_s.n

            btv = btb.rearrange("p (dc b t) -> p dc b t", dc=DC, b=BL)[
                :, :, :, 0:tcn]
            thiv = thi.rearrange("p (dc b t) -> p dc b t", dc=DC, b=BL)[
                :, :, :, 0:tcn]
            E.wait(E.vec, E.s_s, m_sq)
            E.op(E.vec, "tensor_scalar", out=thiv, in0=thiv, scalar1=0.5,
                 scalar2=0.5, op0=ALU.mult, op1=ALU.add)
            E.drain(E.vec)
            E.op(E.vec, "tensor_mul", btv, qv, thiv)
            E.drain(E.vec)
            E.op(E.vec, "tensor_mul", btv, btv, xcv)
            E.drain(E.vec)
            E.wait(E.vec, E.s_s, m_enA)
            E.op(E.vec, "tensor_mul", q2v[:, :, :, 1:1 + tcn],
                 q2v[:, :, :, 1:1 + tcn], btv)
            E.drain(E.vec)
            E.op(E.vec, "tensor_copy",
                 q2v[:, :, :, 0].rearrange("p dc b -> p (dc b)"), c2v[:, 1, :])
            E.drain(E.vec)
            E.op(E.vec, "tensor_tensor_scan", q2cum[:, 0:16 * (tcn + 1)],
                 ones_t[:, 0:16 * (tcn + 1)],
                 q2_s[:, 0:16 * (tcn + 1)], 0.0, ALU.mult, ALU.add)
            E.drain(E.vec)
            qcv = q2cum[:, 0:16 * (tcn + 1)].rearrange(
                "p (dc b c) -> p dc b c", dc=DC, b=BL)
            E.op(E.vec, "tensor_copy", c2v[:, 1, :],
                 qcv[:, :, :, tcn].rearrange("p dc b -> p (dc b)"))
            E.drain(E.vec)
            hv = hbf.rearrange("p (dc b t) -> p dc b t", dc=DC, b=BL)[
                :, :, :, 0:tcn]
            E.op(E.vec, "tensor_mul", hv, Av, qcv[:, :, :, 1:1 + tcn],
                 inc=E.s_v)
            E.drain(E.vec)
            m_h = E.s_v.n

            hbv = hbf.rearrange("p (dc b t) -> p dc b t", dc=DC, b=BL)
            ov3 = obuf.rearrange("p (dc b t) -> p dc b t", dc=DC, b=BL)
            sq3 = sqb3.rearrange("p (dc b t) -> p dc b t", dc=DC, b=BL)
            wov = wo_sb.rearrange("p (kc m) -> p kc m", kc=DC)
            E.wait(E.ten, E.s_v, m_h)
            for oc in range(DC):
                dst = pA[:, (oc % 2) * 512:(oc % 2) * 512 + TT].rearrange(
                    "p (b t) -> p b t", b=BL)
                if sp_prev is not None:
                    E.wait(E.ten, E.s_s, sp_prev)
                for kc in range(DC):
                    E.op(E.ten, "matmul", dst,
                         wov[:, kc, oc * 128:(oc + 1) * 128],
                         hbv[:, kc, :, 0:tcn],
                         start=(kc == 0), stop=(kc == DC - 1),
                         inc=(E.s_t if kc == DC - 1 else None))
                m_mm = E.s_t.n
                E.wait(E.sca, E.s_t, m_mm)
                E.op(E.sca, "copy", ov3[:, oc, :, 0:tcn], dst)
                E.drain(E.sca)
                E.op(E.sca, "activation", sq3[:, oc, :, 0:tcn], dst,
                     ACT.Square, inc=E.s_s)
                sp_prev = E.s_s.n
            m_oev = E.s_s.n

            E.wait(E.ten, E.s_s, m_oev)
            for b in range(BL):
                dst = pB[0:1, b * tcn:(b + 1) * tcn]
                for kc in range(DC):
                    E.op(E.ten, "matmul", dst, onescol[:, :],
                         sq3[:, kc, b, 0:tcn],
                         start=(kc == 0), stop=(kc == DC - 1),
                         inc=(E.s_t if (b == BL - 1 and kc == DC - 1)
                              else None))
            m_ms = E.s_t.n
            E.wait(E.vec, E.s_t, m_ms)
            msr = smalls[0:1, 0:TT]
            w1 = smalls[0:1, 512:512 + TT]
            w2 = smalls[0:1, 1024:1024 + TT]
            E.op(E.vec, "tensor_scalar", out=msr, in0=pB[0:1, 0:TT],
                 scalar1=1.0 / D, scalar2=1e-6, op0=ALU.mult, op1=ALU.add)
            E.drain(E.vec)
            emit_rsqrt(E, E.vec, msr, msr, w1, w2)
            rh = hlrows[0:1, 0:TT]
            rl = hlrows[0:1, 512:512 + TT]
            E.op(E.vec, "tensor_copy", rh, msr)
            E.drain(E.vec)
            E.op(E.vec, "tensor_sub", w1, msr, rh)
            E.drain(E.vec)
            E.op(E.vec, "tensor_copy", rl, w1, inc=E.s_v)
            m_rr = E.s_v.n
            E.wait(E.ten, E.s_v, m_rr)
            E.op(E.ten, "matmul", pC[:, 0:TT], ones1[:, :], rh,
                 start=True, stop=False)
            E.op(E.ten, "matmul", pC[:, 0:TT], ones1[:, :], rl,
                 start=False, stop=True, inc=E.s_t)
            m_rbc = E.s_t.n
            E.wait(E.vec, E.s_t, m_rbc)
            rbc = pC[:, 0:TT].rearrange("p (b t) -> p b t", b=BL)
            y3 = tmpb.rearrange("p (dc b t) -> p dc b t", dc=DC, b=BL)[
                :, :, :, 0:tcn]
            E.op(E.vec, "tensor_mul", y3, ov3[:, :, :, 0:tcn],
                 rbc[:, None, :, :].broadcast_to([128, DC, BL, tcn]))
            E.drain(E.vec)
            E.op(E.vec, "tensor_mul", y3, y3,
                 bview[:, 6, :, None, None].broadcast_to([128, DC, BL, tcn]))
            E.drain(E.vec)
            xs = xrv[:, :, :, 0:tcn]
            E.op(E.vec, "tensor_add", xs, xs, y3, inc=E.s_v)
            E.drain(E.vec)
            m_xres = E.s_v.n
            E.wait(E.sync, E.s_v, m_xres)
            for b in range(BL):
                E.op(E.sync, "dma_start", out=dst_b[:, :, b, tc0:tc0 + tcn],
                     in_=xrv[:, :, b, 0:tcn], inc=E.s_x, inck=16)
            m_xst = E.s_x.n

        if dbg_outs and "ab_o" in dbg_outs:
            E.wait(E.sync, E.s_v, E.s_v.n)
            E.op(E.sync, "dma_start", out=dbg_outs["ab_o"], in_=obuf[:, :],
                 inc=E.s_x, inck=16)
            E.op(E.sync, "dma_start", out=dbg_outs["ab_xc"], in_=xc[:, :],
                 inc=E.s_x, inck=16)
            E.wait(E.sync, E.s_x, E.s_x.n)
        E.barrier()
        ph.close()

    if "ab" in phases:
        emit_attn_b()

    if "f2" in phases:
        emit_ffn("f2_")


    if not ph_order:
        E.op(E.sync, "dma_start",
             out=x_out.rearrange("(dc p) b t -> p dc b t", p=128),
             in_=x_in.rearrange("(dc p) b t -> p dc b t", p=128),
             inc=E.s_x, inck=16)
    E.wait(E.sync, E.s_x, E.s_x.n)

    with ctx:
        blk = ctx.enter_context(nc.Block())
        blk.sync(lambda e: [f(e) for f in E.sync.q])
        blk.vector(lambda e: [f(e) for f in E.vec.q])
        blk.scalar(lambda e: [f(e) for f in E.sca.q])
        blk.tensor(lambda e: [f(e) for f in E.ten.q])
        blk.gpsimd(lambda e: [f(e) for f in E.gps.q])

    return nc, E, dbg_outs


def _prep_inputs(inp, core, Ll=2048):
    x = np.asarray(inp["L0_attn_a_x"], np.float32)
    xc = np.ascontiguousarray(
        np.transpose(x[2 * core:2 * core + 2, :Ll], (2, 0, 1)))

    def T(a):
        return np.ascontiguousarray(np.asarray(a, np.float32).T)

    def r128(v):
        return np.ascontiguousarray(np.asarray(v, np.float32).reshape(DC, 128).T)

    import ml_dtypes
    BFD = ml_dtypes.bfloat16
    sel = np.zeros((NST, NST * 128), BFD)
    for n in range(NST):
        sel[n, n * 128:(n + 1) * 128] = 1
    im = {
        "x_in": xc,
        "w_dtT": T(inp["L0_attn_a_W_dt"]),
        "w_BT": T(inp["L0_attn_a_W_B"]),
        "w_CT": T(inp["L0_attn_a_W_C"]),
        "b_dt_r": r128(inp["L0_attn_a_b_dt"]),
        "A_log_r": np.ascontiguousarray(
            np.asarray(inp["L0_attn_a_A_log"], np.float32)
            .reshape(DC, 128, NST).transpose(1, 0, 2).reshape(128, DC * NST)),
        "D_par_r": r128(inp["L0_attn_a_D_param"]),
        "conv_w_r": np.ascontiguousarray(
            np.asarray(inp["L0_attn_b_conv_weight"], np.float32)[:, 0, :]
            .reshape(DC, 128, 4).transpose(1, 0, 2).reshape(128, DC * 4)),
        "conv_b_r": r128(inp["L0_attn_b_conv_bias"]),
        "w_rT": T(inp["L0_attn_b_Wr_w"]),
        "w_iT": T(inp["L0_attn_b_Wi_w"]),
        "wr_b_r": r128(inp["L0_attn_b_Wr_b"]),
        "wi_b_r": r128(inp["L0_attn_b_Wi_b"]),
        "log_a_r": r128(inp["L0_attn_b_log_a"]),
        "w_oT": T(inp["L0_attn_b_out_w"]),
        "norm_w_r": r128(inp["L0_attn_b_norm_w"]),
        "gate_wT": T(inp["L0_ffn_gate_w"]),
        "e_bias_bc": np.ascontiguousarray(np.broadcast_to(
            np.asarray(inp["L0_ffn_e_bias"], np.float32)[None, :], (128, NE))),
        "gprojT": np.ascontiguousarray(
            np.asarray(inp["L0_ffn_gate_proj"], np.float32).transpose(0, 2, 1)),
        "uprojT": np.ascontiguousarray(
            np.asarray(inp["L0_ffn_up_proj"], np.float32).transpose(0, 2, 1)),
        "dprojT": np.ascontiguousarray(
            np.asarray(inp["L0_ffn_down_proj"], np.float32).transpose(0, 2, 1)),
        "sgT": T(inp["L0_ffn_shared_gate_w"]),
        "suT": T(inp["L0_ffn_shared_up_w"]),
        "sdT": T(inp["L0_ffn_shared_down_w"]),
        "c_ones1": np.ones((1, 128), BFD),
        "c_onescol": np.ones((128, 1), BFD),
        "c_ident": np.eye(128, dtype=BFD),
        "c_bias": np.ascontiguousarray(np.broadcast_to(np.array(
            [ATANH_BI, 1.0, LN17, 0.0], np.float32)[None, :], (128, 4))),
        "c_sel": sel,
    }
    return im


def kernel(**inputs):
    nc, E, _ = build_program()
    in_maps = [_prep_inputs(inputs, c) for c in range(8)]
    res = run_bass_kernel_spmd(nc, in_maps, core_ids=list(range(8)))
    outs = []
    for c in range(8):
        xc = res.results[c]["x_out"]
        outs.append(np.transpose(xc, (1, 2, 0)))
    return np.concatenate(outs, axis=0).astype(np.float32)


# version 10
# speedup vs baseline: 1.0721x; 1.0721x over previous
import numpy as np
from contextlib import ExitStack

import concourse.bass as bass
import concourse.mybir as mybir
from concourse.bass_utils import run_bass_kernel_spmd

F32 = mybir.dt.float32
BF16 = mybir.dt.bfloat16
I32 = mybir.dt.int32
ALU = mybir.AluOpType
ACT = mybir.ActivationFunctionType
AX = mybir.AxisListType

D = 1024
DC = 8
NST = 16
BL = 2
NE = 16
NI = 512
IC = 4
HC = 8

SP_C = (0.001 + 0.1) / 2.0
SP_R = (0.1 - 0.001) / 2.0
ATANH_SC = 0.5 / (SP_R + 1e-8)
ATANH_BI = float(-SP_C * 0.5 / (SP_R + 1e-8))
LN17 = float(np.log(np.float32(1e-7)))


class Stream:
    def __init__(self, name):
        self.name = name
        self.q = []


class Sem:
    def __init__(self, ctx, nc, name):
        self.h = ctx.enter_context(nc.semaphore(name))
        self.n = 0


class Emit:
    def __init__(self, ctx, nc):
        self.nc = nc
        self.sync = Stream("sync")
        self.vec = Stream("vec")
        self.sca = Stream("sca")
        self.ten = Stream("ten")
        self.gps = Stream("gps")
        self.s_x = Sem(ctx, nc, "s_x")
        self.s_w = Sem(ctx, nc, "s_w")
        self.s_v = Sem(ctx, nc, "s_v")
        self.s_s = Sem(ctx, nc, "s_s")
        self.s_t = Sem(ctx, nc, "s_t")
        self.s_g = Sem(ctx, nc, "s_g")
        self.s_b = Sem(ctx, nc, "s_b")
        self.s_wsl = [Sem(ctx, nc, f"s_wsl{i}") for i in range(4)]
        self.s_wsh = Sem(ctx, nc, "s_wsh")
        self.ninstr = 0

    def op(self, stream, method, *args, inc=None, inck=1, **kw):
        self.ninstr += 1

        def f(eng, method=method, args=args, kw=kw, inc=inc, inck=inck):
            ins = getattr(eng, method)(*args, **kw)
            if inc is not None:
                ins.then_inc(inc.h, inck)
        stream.q.append(f)
        if inc is not None:
            inc.n += inck
            return inc.n
        return None

    def wait(self, stream, sem, val):
        stream.q.append(lambda eng, s=sem, v=val: eng.wait_ge(s.h, v))

    def drain(self, stream):
        self.ninstr += 1
        stream.q.append(lambda eng: eng.drain())

    def barrier(self):
        streams = [self.sync, self.vec, self.sca, self.ten, self.gps]
        self.wait(self.sync, self.s_x, self.s_x.n)
        self.wait(self.gps, self.s_w, self.s_w.n)
        for sm in self.s_wsl:
            if sm.n:
                self.wait(self.gps, sm, sm.n)
        if self.s_wsh.n:
            self.wait(self.gps, self.s_wsh, self.s_wsh.n)
        for st in streams:
            if st not in (self.sync,):
                self.drain(st)
            st.q.append(lambda eng, h=self.s_b.h: eng.sem_inc(h, 1))
            self.s_b.n += 1
        tgt = self.s_b.n
        for st in streams:
            self.wait(st, self.s_b, tgt)


def split_ranges(total, step, first=None):
    out = []
    t = 0
    if first is not None and total > 0:
        f = min(first, total)
        out.append((0, f))
        t = f
    while t < total:
        out.append((t, min(step, total - t)))
        t += min(step, total - t)
    return out


def emit_rsqrt(E, st, out, in_, t1, t2, iters=3):
    ib, t1b = in_.bitcast(I32), t1.bitcast(I32)
    E.op(st, "tensor_single_scalar", t1b, ib, 1, ALU.arith_shift_right)
    E.drain(st)
    E.op(st, "tensor_single_scalar", t1b, t1b, -1, ALU.bitwise_xor)
    E.drain(st)
    E.op(st, "tensor_single_scalar", t1b, t1b, int(0x5F3759DF + 1), ALU.add)
    E.drain(st)
    for _ in range(iters):
        E.op(st, "tensor_mul", t2, t1, t1)
        E.drain(st)
        E.op(st, "tensor_mul", t2, t2, in_)
        E.drain(st)
        E.op(st, "tensor_scalar", out=t2, in0=t2, scalar1=-0.5, scalar2=1.5,
             op0=ALU.mult, op1=ALU.add)
        E.drain(st)
        E.op(st, "tensor_mul", t1, t1, t2)
        E.drain(st)
    E.op(st, "tensor_copy", out, t1)
    E.drain(st)


def attn_a_chunks(Ll):
    out = []
    t = 0
    first = True
    while t < Ll:
        if first:
            n = min(121, Ll - t)
            first = False
        else:
            n = min(120, Ll - t)
        out.append((t, n))
        t += n
    return out


def subchunks_of(tc0, tcn):
    subs = []
    t = tc0
    end = tc0 + tcn
    if tc0 == 0:
        subs.append((0, 1))
        t = 1
    while t < end:
        n = min(10, end - t)
        subs.append((t, n))
        t += n
    return subs


def build_program(Ll=2048, FF_T=256, B_TC=64,
                  phases=("aa", "f1", "ab", "f2"), dbg=()):
    nc = bass.Bass("TRN2", target_bir_lowering=False)

    def din(name, shape, dt=F32):
        return nc.dram_tensor(name, list(shape), dt, kind="ExternalInput").ap()

    x_in = din("x_in", [D, BL, Ll])
    x_out = nc.dram_tensor("x_out", [D, BL, Ll], F32, kind="ExternalOutput").ap()
    xs_dram = nc.dram_tensor("xs_dram", [D, BL, Ll], F32, kind="Internal").ap()
    ph_order = [p for p in ("aa", "f1", "ab", "f2") if p in phases]

    def phase_io(p):
        src = x_in if (ph_order and p == ph_order[0]) else xs_dram
        dst = x_out if (ph_order and p == ph_order[-1]) else xs_dram
        return (src.rearrange("(dc p) b t -> p dc b t", p=128),
                dst.rearrange("(dc p) b t -> p dc b t", p=128))
    w_dtT = din("w_dtT", [D, D])
    w_BT = din("w_BT", [D, NST])
    w_CT = din("w_CT", [D, NST])
    b_dt_r = din("b_dt_r", [128, DC])
    A_log_r = din("A_log_r", [128, DC * NST])
    D_par_r = din("D_par_r", [128, DC])
    conv_w_r = din("conv_w_r", [128, DC * 4])
    conv_b_r = din("conv_b_r", [128, DC])
    w_rT = din("w_rT", [D, D])
    w_iT = din("w_iT", [D, D])
    wr_b_r = din("wr_b_r", [128, DC])
    wi_b_r = din("wi_b_r", [128, DC])
    log_a_r = din("log_a_r", [128, DC])
    w_oT = din("w_oT", [D, D])
    norm_w_r = din("norm_w_r", [128, DC])
    gate_wT = din("gate_wT", [D, NE])
    e_bias_bc = din("e_bias_bc", [128, NE])
    gprojT = din("gprojT", [NE, D, NI])
    uprojT = din("uprojT", [NE, D, NI])
    dprojT = din("dprojT", [NE, NI, D])
    sgT = din("sgT", [D, NI])
    suT = din("suT", [D, NI])
    sdT = din("sdT", [NI, D])
    c_ones1 = din("c_ones1", [1, 128], BF16)
    c_onescol = din("c_onescol", [128, 1], BF16)
    c_ident = din("c_ident", [128, 128], BF16)
    c_sel = din("c_sel", [NST, NST * 128], BF16)
    c_bias = din("c_bias", [128, 4])

    dbg_outs = {}
    for name, shape in dbg:
        dbg_outs[name] = nc.dram_tensor(
            "dbg_" + name, list(shape), F32, kind="ExternalOutput").ap()

    ctx = ExitStack()

    def sb(name, shape, dt=F32):
        return ctx.enter_context(nc.sbuf_tensor(name, list(shape), dt))

    A_tile = sb("A_tile", [128, DC * NST])
    biases = sb("biases", [128, DC * 8])
    conv_w = sb("conv_w", [128, DC * 4])
    gw_sb = sb("gw_sb", [128, DC * NE], BF16)
    wB_sb = sb("wB_sb", [128, DC * NST], BF16)
    wC_sb = sb("wC_sb", [128, DC * NST], BF16)
    ebias_sb = sb("ebias_sb", [128, NE])
    ones1 = sb("ones1", [1, 128], BF16)
    onescol = sb("onescol", [128, 1], BF16)
    ident = sb("ident", [128, 128], BF16)
    sel16 = sb("sel16", [NST, NST * 128], BF16)
    smalls = sb("smalls", [128, 3072])
    cbias = sb("cbias", [128, 4])
    hlrows = sb("hlrows", [128, 2560], BF16)

    E = Emit(ctx, nc)

    def psum(name, shape):
        return ctx.enter_context(nc.psum_tensor(name, list(shape), F32))

    pA = psum("pA", [128, 1024])
    pB = psum("pB", [128, 1024])
    pC = psum("pC", [128, 1024])
    pD = psum("pD", [128, 1024])

    bview = biases.rearrange("p (k dc) -> p k dc", k=8)

    for dst, src in ((ones1[:, :], c_ones1), (onescol[:, :], c_onescol),
                     (ident[:, :], c_ident), (sel16[:, :], c_sel),
                     (ebias_sb[:, :], e_bias_bc), (A_tile[:, :], A_log_r),
                     (conv_w[:, :], conv_w_r), (cbias[:, :], c_bias)):
        E.op(E.sync, "dma_start", out=dst, in_=src, inc=E.s_x, inck=16)
    for i, src in enumerate((b_dt_r, D_par_r, conv_b_r, wr_b_r, wi_b_r,
                             log_a_r, norm_w_r)):
        E.op(E.sync, "dma_start", out=bview[:, i, :], in_=src,
             inc=E.s_x, inck=16)
    E.op(E.gps, "dma_start",
         out=gw_sb.rearrange("p (kc e) -> p kc e", kc=DC),
         in_=gate_wT.rearrange("(kc p) e -> p kc e", p=128), inc=E.s_w, inck=16)
    E.op(E.gps, "dma_start",
         out=wB_sb.rearrange("p (kc n) -> p kc n", kc=DC),
         in_=w_BT.rearrange("(kc p) n -> p kc n", p=128), inc=E.s_w, inck=16)
    E.op(E.gps, "dma_start",
         out=wC_sb.rearrange("p (kc n) -> p kc n", kc=DC),
         in_=w_CT.rearrange("(kc p) n -> p kc n", p=128), inc=E.s_w, inck=16)

    E.wait(E.sca, E.s_x, E.s_x.n)
    E.op(E.sca, "activation", A_tile[:, :], A_tile[:, :], ACT.Tanh, scale=0.1)
    E.drain(E.sca)
    E.op(E.sca, "activation", A_tile[:, :], A_tile[:, :], ACT.Exp, scale=10.0)
    E.drain(E.sca)
    la = bview[:, 5, :]
    t1i = smalls[:, 0:DC]
    t2i = smalls[:, DC:2 * DC]
    E.op(E.sca, "activation", t1i, la, ACT.Abs)
    E.drain(E.sca)
    E.op(E.sca, "activation", t1i, t1i, ACT.Exp, scale=-1.0)
    E.drain(E.sca)
    E.op(E.sca, "activation", t1i, t1i, ACT.Ln, bias=1.0)
    E.drain(E.sca)
    E.op(E.sca, "activation", t2i, la, ACT.Relu, scale=-1.0, inc=E.s_s)
    E.wait(E.vec, E.s_s, E.s_s.n)
    E.op(E.vec, "tensor_add", t1i, t1i, t2i)
    E.drain(E.vec)
    E.op(E.vec, "tensor_single_scalar", la, t1i, -8.0, ALU.mult, inc=E.s_v)
    E.barrier()

    def emit_ln(tc0, tcn, out_f32, out_bf, sq_bf, toff=0, obn=None,
                f32view=None, pre_wait_t=None, xs=None, wait_x=None):
        TT = BL * tcn
        assert TT <= 512
        obv = out_bf.rearrange("p (dc b t) -> p dc b t", dc=DC, b=BL,
                               t=obn if False else None) if False else             out_bf.rearrange("p (dc b t) -> p dc b t", dc=DC, b=BL)[
                :, :, :, toff:toff + tcn]
        sqv = sq_bf.rearrange("p (dc b t) -> p dc b t", dc=DC, b=BL)[
            :, :, :, 0:tcn]

        if pre_wait_t is not None:
            E.wait(E.sca, E.s_t, pre_wait_t)
        if wait_x is not None:
            E.wait(E.sca, E.s_x, wait_x)
            E.wait(E.vec, E.s_x, wait_x)
        E.op(E.sca, "activation", sqv, xs, ACT.Square, inc=E.s_s)
        m_sq = E.s_s.n
        E.op(E.vec, "tensor_copy", obv, xs, inc=E.s_v)
        m_xc = E.s_v.n

        E.wait(E.ten, E.s_s, m_sq)
        E.wait(E.ten, E.s_v, m_xc)
        last = None
        for q, src in ((0, obv), (1, sqv)):
            for b in range(BL):
                dst = pB[0:1, q * 512 + b * tcn: q * 512 + (b + 1) * tcn]
                for kc in range(DC):
                    last = E.op(E.ten, "matmul", dst, onescol[:, :],
                                src[:, kc, b, :], start=(kc == 0),
                                stop=(kc == DC - 1),
                                inc=(E.s_t if (q, b, kc) == (1, BL - 1, DC - 1)
                                     else None))
        m_st = E.s_t.n
        E.last_ln_mst = m_st

        E.wait(E.vec, E.s_t, m_st)
        m = smalls[0:1, 0:TT]
        v = smalls[0:1, 512:512 + TT]
        w1 = smalls[0:1, 1024:1024 + TT]
        w2 = smalls[0:1, 1536:1536 + TT]
        mm_ = [pB[0:1, 0 * 512 + b * tcn:0 * 512 + (b + 1) * tcn] for b in range(BL)]
        vv_ = [pB[0:1, 1 * 512 + b * tcn:1 * 512 + (b + 1) * tcn] for b in range(BL)]
        E.op(E.vec, "tensor_single_scalar", m,
             pB[0:1, 0:TT] if tcn * BL == TT else None, 1.0 / D, ALU.mult)
        E.op(E.vec, "tensor_single_scalar", v, pB[0:1, 512:512 + TT],
             1.0 / D, ALU.mult)
        E.drain(E.vec)
        E.op(E.vec, "tensor_mul", w1, m, m)
        E.drain(E.vec)
        E.op(E.vec, "tensor_sub", v, v, w1)
        E.drain(E.vec)
        E.op(E.vec, "tensor_scalar_add", v, v, 1e-5)
        E.drain(E.vec)
        emit_rsqrt(E, E.vec, v, v, w1, w2)
        mh, ml = hlrows[0:1, 0:TT], hlrows[0:1, 512:512 + TT]
        ih, il = hlrows[0:1, 1024:1024 + TT], hlrows[0:1, 1536:1536 + TT]
        E.op(E.vec, "tensor_copy", mh, m)
        E.op(E.vec, "tensor_copy", ih, v)
        E.drain(E.vec)
        E.op(E.vec, "tensor_sub", w1, m, mh)
        E.op(E.vec, "tensor_sub", w2, v, ih)
        E.drain(E.vec)
        E.op(E.vec, "tensor_copy", ml, w1)
        E.op(E.vec, "tensor_copy", il, w2, inc=E.s_v)
        m_rows = E.s_v.n

        E.wait(E.ten, E.s_v, m_rows)
        E.op(E.ten, "matmul", pC[:, 0:TT], ones1[:, :], mh, start=True, stop=False)
        E.op(E.ten, "matmul", pC[:, 0:TT], ones1[:, :], ml, start=False, stop=True)
        E.op(E.ten, "matmul", pD[:, 0:TT], ones1[:, :], ih, start=True, stop=False)
        E.op(E.ten, "matmul", pD[:, 0:TT], ones1[:, :], il, start=False, stop=True,
             inc=E.s_t)
        m_bc = E.s_t.n

        E.wait(E.vec, E.s_t, m_bc)
        mbc = pC[:, 0:TT].rearrange("p (b t) -> p b t", b=BL)
        ibc = pD[:, 0:TT].rearrange("p (b t) -> p b t", b=BL)
        if f32view is not None:
            ov = f32view
        else:
            ov = out_f32.rearrange("p (dc b t) -> p dc b t", dc=DC, b=BL)[
                :, :, :, toff:toff + tcn]
        E.op(E.vec, "tensor_sub", ov, xs,
             mbc[:, None, :, :].broadcast_to([128, DC, BL, tcn]))
        E.drain(E.vec)
        E.op(E.vec, "tensor_mul", ov, ov,
             ibc[:, None, :, :].broadcast_to([128, DC, BL, tcn]))
        E.drain(E.vec)
        E.op(E.vec, "tensor_copy", obv, ov, inc=E.s_v)
        return E.s_v.n

    if "aa" in phases:
        A_TCmax = 121
        ph = ExitStack()

        def psb(name, shape, dt=F32):
            return ph.enter_context(nc.sbuf_tensor(name, list(shape), dt))

        W = BL * A_TCmax
        wdt_sb = psb("wdt_sb", [128, DC * D], BF16)
        ln1 = psb("aa_ln1", [128, DC * W])
        ln1b = psb("aa_ln1b", [128, DC * W], BF16)
        sqb = psb("aa_sqb", [128, DC * W], BF16)
        dt_b = psb("aa_dt", [128, DC * W])
        wdtx = psb("aa_wdtx", [128, DC * W])
        ybuf = psb("aa_y", [128, DC * W])
        ubuf0 = psb("aa_u0", [128, 256 * 10])
        ubuf = [ubuf0, ubuf0]
        dabuf = [psb("aa_da0", [128, 256 * 11]), psb("aa_da1", [128, 256 * 11])]
        btbuf = psb("aa_bt", [128, 256 * 11])
        Hbuf = [psb("aa_H0", [128, 256 * 11]), psb("aa_H1", [128, 256 * 11])]
        ytmp = [psb("aa_yt0", [128, 256 * 10]), psb("aa_yt1", [128, 256 * 10])]
        sp1 = ytmp[0]
        Bp = psb("aa_Bp", [NST, W])
        Cp = psb("aa_Cp", [NST, W])
        Bph = psb("aa_Bph", [NST, 2 * W], BF16)
        Cph = psb("aa_Cph", [NST, 2 * W], BF16)
        nst_t = psb("aa_nst", [NST, W])
        xbuf_a = psb("aa_x", [128, DC * W])
        A_exp = psb("aa_Aexp", [128, DC * BL * NST])
        bc_sb = [psb("aa_bcsb0", [128, 2048]), psb("aa_bcsb1", [128, 2048])]
        scale_sb = [psb("aa_scale0", [128, 8]), psb("aa_scale1", [128, 8])]

        E.op(E.gps, "dma_start",
             out=wdt_sb.rearrange("p (kc m) -> p kc m", kc=DC),
             in_=w_dtT.rearrange("(kc p) m -> p kc m", p=128),
             inc=E.s_w, inck=16)
        m_wdt = E.s_w.n

        if dbg_outs:
            for buf in (ln1, dt_b, ybuf, sp1, wdtx):
                E.op(E.vec, "memset", buf[:, :], 0.0)
            E.drain(E.vec)

        aev = A_exp.rearrange("p (dc b n) -> p dc b n", dc=DC, b=BL)
        atv = A_tile.rearrange("p (dc n) -> p dc n", dc=DC)
        E.op(E.vec, "tensor_copy", aev,
             atv[:, :, None, :].broadcast_to([128, DC, BL, NST]), inc=E.s_v)
        E.drain(E.vec)
        m_aexp = E.s_v.n

        cur_S_par = [None, None]

        def set_S(S, par):
            if cur_S_par[par] != S:
                cur_S_par[par] = S
                E.op(E.vec, "memset",
                     dabuf[par][:, 0:256 * S].rearrange(
                         "p (r c) -> p r c", c=S)[:, :, 0:1], 0.0, inc=E.s_v)
                E.drain(E.vec)
                setS_marks[par] = E.s_v.n

        sp_marks_prev = None
        m_scale_sb = 0
        m_bt_free = 0
        first_sub_global = True
        sub_idx = [0]
        zda_marks = [0, 0]
        setS_marks = [0, 0]
        z_marks = [0, 0]
        tanh_marks = [0, 0]
        da_marks = [0, 0]
        ymul_marks = [0, 0]
        ymul_done = [0, 0]
        ycol_marks = [0, 0]
        m_rsq_prev = 0
        m_nmm_prev = 0
        m_scmm_prev = 0
        grp_last_ymul = {}
        prev_par = 0
        gidx = 0
        m_grp_evac_g = 0

        src_aa, dst_aa = phase_io("aa")
        m_xst = 0
        chunks = attn_a_chunks(Ll)
        for ci, (tc0, tcn) in enumerate(chunks):
            TT = BL * tcn
            xbv = xbuf_a.rearrange("p (dc b t) -> p dc b t", dc=DC, b=BL)
            if m_xst:
                E.wait(E.sync, E.s_x, m_xst)
            for b in range(BL):
                E.op(E.sync, "dma_start", out=xbv[:, :, b, 0:tcn],
                     in_=src_aa[:, :, b, tc0:tc0 + tcn], inc=E.s_x, inck=16)
            m_xld = E.s_x.n
            m_ln = emit_ln(tc0, tcn, ln1, ln1b, sqb,
                           xs=xbv[:, :, :, 0:tcn], wait_x=m_xld)

            lbv = ln1b.rearrange("p (dc b t) -> p dc b t", dc=DC, b=BL)
            lnv = ln1.rearrange("p (dc b t) -> p dc b t", dc=DC, b=BL)
            dtv = dt_b.rearrange("p (dc b t) -> p dc b t", dc=DC, b=BL)
            wdv = wdt_sb.rearrange("p (kc m) -> p kc m", kc=DC)
            wxv = wdtx.rearrange("p (dc b t) -> p dc b t", dc=DC, b=BL)
            yv = ybuf.rearrange("p (dc b t) -> p dc b t", dc=DC, b=BL)

            E.wait(E.ten, E.s_v, m_ln)
            if ci == 0:
                E.wait(E.ten, E.s_w, m_wdt)
            else:
                E.wait(E.sca, E.s_g, E.s_g.n)
            dt_mm = []
            sp_marks = []
            for oc in range(DC):
                dst = pA[:, (oc % 2) * 512:(oc % 2) * 512 + TT]
                if oc >= 2:
                    E.wait(E.ten, E.s_s, sp_marks[oc - 2])
                elif ci > 0:
                    E.wait(E.ten, E.s_s, sp_marks_prev[oc + DC - 2])
                dst3 = dst.rearrange("p (b t) -> p b t", b=BL)
                for kc in range(DC):
                    E.op(E.ten, "matmul", dst3,
                         wdv[:, kc, oc * 128:(oc + 1) * 128],
                         lbv[:, kc, :, 0:tcn],
                         start=(kc == 0), stop=(kc == DC - 1),
                         inc=(E.s_t if kc == DC - 1 else None))
                dt_mm.append(E.s_t.n)
                E.wait(E.sca, E.s_t, dt_mm[oc])
                src = dst3
                a1 = sp1.rearrange("p (dc b t) -> p dc b t", dc=DC, b=BL)[
                    :, oc, :, 0:tcn]
                a2 = dtv[:, oc, :, 0:tcn]
                E.op(E.sca, "activation", a1, src, ACT.Abs,
                     bias=bview[:, 0, oc:oc + 1])
                E.drain(E.sca)
                E.op(E.sca, "activation", a1, a1, ACT.Exp, scale=-1.0)
                E.drain(E.sca)
                E.op(E.sca, "activation", a1, a1, ACT.Ln, bias=1.0)
                E.drain(E.sca)
                E.op(E.sca, "activation", a2, src, ACT.Relu,
                     bias=bview[:, 0, oc:oc + 1], inc=E.s_s)
                sp_marks.append(E.s_s.n)

            wbv = wB_sb.rearrange("p (kc n) -> p kc n", kc=DC)
            wcv = wC_sb.rearrange("p (kc n) -> p kc n", kc=DC)
            for dstoff, wsrc in ((0, wbv), (512, wcv)):
                dst3 = pB[0:NST, dstoff:dstoff + TT].rearrange(
                    "p (b t) -> p b t", b=BL)
                for kc in range(DC):
                    E.op(E.ten, "matmul", dst3,
                         wsrc[:, kc, :],
                         lbv[:, kc, :, 0:tcn],
                         start=(kc == 0), stop=(kc == DC - 1),
                         inc=(E.s_t if (dstoff == 512 and kc == DC - 1) else None))
            m_bpc = E.s_t.n

            sp_marks_prev = sp_marks

            for oc in range(DC):
                E.wait(E.vec, E.s_s, sp_marks[oc])
                a1 = sp1.rearrange("p (dc b t) -> p dc b t", dc=DC, b=BL)[
                    :, oc, :, 0:tcn]
                a2 = dtv[:, oc, :, 0:tcn]
                E.op(E.vec, "tensor_add", a2, a2, a1,
                     inc=(E.s_v if oc == DC - 1 else None))
            E.drain(E.vec)
            m_spsum = E.s_v.n

            E.wait(E.sca, E.s_v, m_spsum)
            for oc in range(DC):
                sl = dtv[:, oc, :, 0:tcn]
                E.op(E.sca, "activation", sl, sl, ACT.Tanh,
                     scale=ATANH_SC, bias=cbias[:, 0:1],
                     inc=(E.s_s if oc == DC - 1 else None))
            m_dtt = E.s_s.n

            E.wait(E.vec, E.s_s, m_dtt)
            if ci > 0:
                E.wait(E.vec, E.s_g, E.s_g.n)
            sl = dtv[:, :, :, 0:tcn]
            E.op(E.vec, "tensor_scalar", out=sl, in0=sl,
                 scalar1=SP_R / 0.5, scalar2=SP_C, op0=ALU.mult, op1=ALU.add)
            E.drain(E.vec)
            E.op(E.vec, "tensor_mul", wxv[:, :, :, 0:tcn], sl,
                 lnv[:, :, :, 0:tcn], inc=E.s_v)
            E.drain(E.vec)
            m_wdtx = E.s_v.n

            E.wait(E.vec, E.s_t, m_bpc)
            E.op(E.vec, "tensor_copy", Bp[:, 0:TT], pB[0:NST, 0:TT])
            E.op(E.vec, "tensor_copy", Cp[:, 0:TT], pB[0:NST, 512:512 + TT])
            E.drain(E.vec)
            for src, dsthl in ((Bp, Bph), (Cp, Cph)):
                E.op(E.vec, "tensor_copy", dsthl[:, 0:TT], src[:, 0:TT])
                E.drain(E.vec)
                E.op(E.vec, "tensor_sub", nst_t[:, 0:TT], src[:, 0:TT],
                     dsthl[:, 0:TT])
                E.drain(E.vec)
                E.op(E.vec, "tensor_copy", dsthl[:, TT:2 * TT], nst_t[:, 0:TT])
                E.drain(E.vec)
            E.op(E.vec, "memset", smalls[0:1, 2047:2048], 0.0, inc=E.s_v)
            m_bpchl = E.s_v.n

            subs = subchunks_of(tc0, tcn)
            groups = []
            for i in range(0, len(subs), 3):
                gs = subs[i:i + 3]
                groups.append((gs[0][0], gs[-1][0] + gs[-1][1] - gs[0][0], gs))
            flat = []
            for (g0, gtn, gsubs) in groups:
                for (st0, sln) in gsubs:
                    flat.append((st0, sln, g0, gtn, gsubs[0][0]))

            Aslm = A_exp[:, :].rearrange("p (a n) -> p a n", n=NST)

            def emit_z(st0_, sln_, par_):
                S_ = sln_ + 1
                dslm = dtv[:, :, :, st0_ - tc0:st0_ - tc0 + sln_].rearrange(
                    "p dc b t -> p (dc b) t")
                z4 = dabuf[par_][:, 0:256 * S_].rearrange(
                    "p (a n t) -> p a n t", a=16, n=NST)[:, :, :, 1:1 + sln_]
                w = max(zda_marks[par_], m_wdtx, m_aexp)
                E.wait(E.gps, E.s_v, w)
                E.op(E.gps, "tensor_mul", z4,
                     dslm[:, :, None, :].broadcast_to([128, 16, NST, sln_]),
                     Aslm[:, :, :, None].broadcast_to([128, 16, NST, sln_]),
                     inc=E.s_g)
                z_marks[par_] = E.s_g.n

            def emit_tanh_exp(st0_, sln_, par_):
                S_ = sln_ + 1
                set_S(S_, par_)
                dar = dabuf[par_][:, 0:256 * S_].rearrange(
                    "p (r t) -> p r t", t=S_)
                E.wait(E.sca, E.s_g, z_marks[par_])
                E.op(E.sca, "activation",
                     ubuf[par_][:, 0:256 * sln_].rearrange(
                         "p (r t) -> p r t", t=sln_),
                     dar[:, :, 1:1 + sln_], ACT.Tanh, scale=1.0 / 16.0,
                     inc=E.s_s)
                tanh_marks[par_] = E.s_s.n
                E.drain(E.sca)
                if setS_marks[par_]:
                    E.wait(E.sca, E.s_v, setS_marks[par_])
                E.op(E.sca, "activation", dar[:, :, 1:1 + sln_],
                     ubuf[par_][:, 0:256 * sln_].rearrange(
                         "p (r t) -> p r t", t=sln_),
                     ACT.Exp, scale=-16.0, inc=E.s_s)
                da_marks[par_] = E.s_s.n

            def emit_ymul(p):
                E.wait(E.gps, E.s_v, max(p["m_tts"], ymul_done[p["par"]]))
                y4 = ytmp[p["par"]][:, 0:256 * p["sln"]].rearrange(
                    "p (dc bn t) -> p dc bn t", dc=DC, bn=32)
                H4 = Hbuf[p["par"]][:, 0:256 * (p["sln"] + 1)].rearrange(
                    "p (dc bn t) -> p dc bn t", dc=DC, bn=32)
                E.op(E.gps, "tensor_mul", y4, H4[:, :, :, 1:1 + p["sln"]],
                     p["bcC"][:, None, :, p["lo"]:p["lo"] + p["sln"]
                              ].broadcast_to([128, DC, 32, p["sln"]]),
                     inc=E.s_g)
                ymul_marks[p["par"]] = E.s_g.n
                grp_last_ymul[p["gidx"]] = E.s_g.n
                p["m_ymul"] = E.s_g.n

            def emit_yred(p):
                E.wait(E.vec, E.s_g, p["m_ymul"])
                yin = ytmp[p["par"]][:, 0:256 * p["sln"]].rearrange(
                    "p (a n t) -> p a t n", a=16, n=NST)
                yout = yv[:, :, :,
                          p["st0"] - tc0:p["st0"] - tc0 + p["sln"]].rearrange(
                    "p dc b t -> p (dc b) t")
                E.op(E.vec, "tensor_reduce", yout, yin, AX.X, ALU.add,
                     inc=E.s_v)
                ymul_done[p["par"]] = E.s_v.n
                E.drain(E.vec)
                if p["flag"]:
                    E.wait(E.vec, E.s_s, m_scale_sb)
                    ycol = yv[:, :, :, p["st0"] - tc0 + p["sln"] - 1]
                    E.op(E.vec, "tensor_mul", ycol, ycol,
                         scale_sb[p["par"]][:, None, 0:BL].broadcast_to(
                             [128, DC, BL]), inc=E.s_v)
                    ycol_marks[p["par"]] = E.s_v.n
                    E.drain(E.vec)

            st0_0, sln_0 = flat[0][0], flat[0][1]
            par0 = sub_idx[0] % 2
            emit_z(st0_0, sln_0, par0)
            emit_tanh_exp(st0_0, sln_0, par0)

            pend = None
            bcBv = bcCv = None
            for j, (st0, sln, g0, gtn, gfirst) in enumerate(flat):
                S = sln + 1
                lo = st0 - g0
                flag = (st0 + sln - 1) % 10 == 0
                par = sub_idx[0] % 2

                if st0 == gfirst:
                    gp = gidx % 2
                    assert BL * gtn <= 64
                    E.wait(E.ten, E.s_v, m_bpchl)
                    if m_grp_evac_g:
                        E.wait(E.ten, E.s_s, m_grp_evac_g)
                    for srchl, dstp in ((Bph, pC), (Cph, pD)):
                        hlv = srchl[:, 0:2 * TT].rearrange(
                            "q (h b t) -> q h b t", h=2, b=BL)
                        for n in range(NST):
                            dst = dstp[:, n * 64:n * 64 + BL * gtn].rearrange(
                                "p (b t) -> p b t", b=BL)
                            E.op(E.ten, "matmul", dst,
                                 sel16[:, n * 128:(n + 1) * 128],
                                 hlv[:, 0, :, g0 - tc0:g0 - tc0 + gtn],
                                 start=True, stop=False)
                            E.op(E.ten, "matmul", dst,
                                 sel16[:, n * 128:(n + 1) * 128],
                                 hlv[:, 1, :, g0 - tc0:g0 - tc0 + gtn],
                                 start=False, stop=True,
                                 inc=(E.s_t if (srchl is Cph and n == NST - 1)
                                      else None))
                    m_grp_fill = E.s_t.n
                    E.wait(E.sca, E.s_t, m_grp_fill)
                    if gidx - 2 in grp_last_ymul:
                        E.wait(E.sca, E.s_g, grp_last_ymul[gidx - 2])
                    for qi, srcp in ((0, pC), (1, pD)):
                        inap = srcp[:, 0:NST * 64].rearrange(
                            "p (n t) -> p n t", n=NST)[
                            :, :, 0:BL * gtn].rearrange(
                            "p n (b t) -> p b n t", b=BL)
                        outap = bc_sb[gp][:, qi * 1024:(qi + 1) * 1024
                                          ].rearrange(
                            "p (b n t) -> p b n t", b=BL, n=NST)[:, :, :, 0:gtn]
                        E.op(E.sca, "copy", outap, inap,
                             inc=(E.s_s if qi == 1 else None))
                    m_grp_evac_g = E.s_s.n
                    bcBv = bc_sb[gp][:, 0:1024].rearrange(
                        "p (b n t) -> p b n t", b=BL, n=NST)
                    bcCv = bc_sb[gp][:, 1024:2048].rearrange(
                        "p (bn t) -> p bn t", t=32)
                    cur_gidx = gidx
                    gidx += 1

                bt5 = btbuf[:, 0:256 * S].rearrange(
                    "p (dc b n t) -> p dc b n t", dc=DC, b=BL, n=NST)
                E.wait(E.gps, E.s_v, max(m_bt_free, m_wdtx))
                E.wait(E.gps, E.s_s, m_grp_evac_g)
                for b in range(BL):
                    E.op(E.gps, "tensor_mul", bt5[:, :, b, :, 1:1 + sln],
                         bcBv[:, b, :, lo:lo + sln][:, None, :, :
                              ].broadcast_to([128, DC, NST, sln]),
                         wxv[:, :, b, st0 - tc0:st0 - tc0 + sln][:, :, None, :
                              ].broadcast_to([128, DC, NST, sln]),
                         inc=(E.s_g if b == BL - 1 else None))
                m_btmul = E.s_g.n

                if j + 1 < len(flat):
                    nst0, nsln = flat[j + 1][0], flat[j + 1][1]
                    npar = (sub_idx[0] + 1) % 2
                    emit_z(nst0, nsln, npar)
                    emit_tanh_exp(nst0, nsln, npar)

                btc0 = bt5[:, :, :, :, 0]
                if first_sub_global:
                    E.op(E.vec, "memset", btc0, 0.0)
                else:
                    pvr = Hbuf[prev_par][:, 0:256 * prev_S].rearrange(
                        "p (dc b n t) -> p dc b n t",
                        dc=DC, b=BL, n=NST)[:, :, :, :, prev_S - 1]
                    if prev_flag:
                        E.wait(E.vec, E.s_s, m_scale_sb)
                        E.op(E.vec, "tensor_mul", btc0, pvr,
                             scale_sb[prev_par][:, None, 0:BL, None
                                  ].broadcast_to([128, DC, BL, NST]))
                    else:
                        E.op(E.vec, "tensor_copy", btc0, pvr)
                E.drain(E.vec)
                E.wait(E.vec, E.s_s, da_marks[par])
                E.wait(E.vec, E.s_g, max(m_btmul, ymul_marks[par]))
                E.op(E.vec, "tensor_tensor_scan",
                     Hbuf[par][:, 0:256 * S], dabuf[par][:, 0:256 * S],
                     btbuf[:, 0:256 * S], 0.0, ALU.mult, ALU.add,
                     inc=E.s_v)
                m_bt_free = E.s_v.n
                zda_marks[par] = E.s_v.n
                m_tts = E.s_v.n
                E.drain(E.vec)

                if flag:
                    Hc = Hbuf[par][:, 0:256 * S].rearrange(
                        "p (dc b n t) -> p dc b n t",
                        dc=DC, b=BL, n=NST)[:, :, :, :, S - 1]
                    outc = smalls[:, 2176:2176 + 256].rearrange(
                        "p (b dc n) -> p b dc n", b=BL, dc=DC)
                    E.wait(E.sca, E.s_v, m_tts)
                    E.op(E.sca, "activation", outc,
                         Hc.transpose([0, 2, 1, 3]), ACT.Square, inc=E.s_s)
                    m_sq = E.s_s.n
                    E.wait(E.vec, E.s_s, m_sq)
                    if m_nmm_prev:
                        E.wait(E.vec, E.s_t, max(m_nmm_prev, m_scmm_prev))
                    comb = smalls[:, 2176:2176 + 256].rearrange(
                        "p (b rest) -> p b rest", b=BL)
                    red = smalls[:, 2432:2432 + BL]
                    E.op(E.vec, "tensor_reduce", red, comb, AX.X, ALU.add)
                    E.drain(E.vec)
                    redb = smalls[:, 2440:2444].bitcast(BF16)[:, 0:BL]
                    E.op(E.vec, "tensor_copy", redb, red, inc=E.s_v)
                    m_hsq = E.s_v.n
                    E.drain(E.vec)
                    E.wait(E.ten, E.s_v, m_hsq)
                    if m_rsq_prev:
                        E.wait(E.ten, E.s_s, m_rsq_prev)
                    E.op(E.ten, "matmul", pB[0:1, 512:512 + BL],
                         onescol[:, :], redb, start=True, stop=True,
                         inc=E.s_t)
                    m_nmm = E.s_t.n
                    nr = smalls[0:1, 2448:2448 + BL]
                    E.wait(E.sca, E.s_t, m_nmm)
                    E.op(E.sca, "activation", nr, pB[0:1, 512:512 + BL],
                         ACT.Sqrt, inc=E.s_s)
                    m_rsq = E.s_s.n
                    g1 = smalls[0:1, 2456:2456 + BL]
                    g2 = smalls[0:1, 2464:2464 + BL]
                    E.wait(E.vec, E.s_s, m_rsq)
                    E.op(E.vec, "reciprocal", g2, nr)
                    E.drain(E.vec)
                    E.op(E.vec, "tensor_scalar", out=g1, in0=g2,
                         scalar1=20.0, scalar2=1.0, op0=ALU.mult,
                         op1=ALU.min)
                    E.drain(E.vec)
                    sch = hlrows[0:1, 2048:2048 + BL]
                    scl = hlrows[0:1, 2056:2056 + BL]
                    E.op(E.vec, "tensor_copy", sch, g1)
                    E.drain(E.vec)
                    E.op(E.vec, "tensor_sub", g2, g1, sch)
                    E.drain(E.vec)
                    E.op(E.vec, "tensor_copy", scl, g2, inc=E.s_v)
                    m_schl = E.s_v.n
                    E.wait(E.ten, E.s_v, m_schl)
                    E.wait(E.ten, E.s_s, m_scale_sb)
                    E.op(E.ten, "matmul", pB[:, 768:768 + BL], ones1[:, :],
                         sch, start=True, stop=False)
                    E.op(E.ten, "matmul", pB[:, 768:768 + BL], ones1[:, :],
                         scl, start=False, stop=True, inc=E.s_t)
                    m_scmm = E.s_t.n
                    E.wait(E.sca, E.s_t, m_scmm)
                    if ycol_marks[par]:
                        E.wait(E.sca, E.s_v, ycol_marks[par])
                    E.op(E.sca, "copy", scale_sb[par][:, 0:BL],
                         pB[:, 768:768 + BL], inc=E.s_s)
                    m_scale_sb = E.s_s.n
                    m_rsq_prev = m_rsq
                    m_nmm_prev = m_nmm
                    m_scmm_prev = m_scmm

                if pend is not None:
                    emit_ymul(pend)
                    emit_yred(pend)
                pend = {"par": par, "st0": st0, "sln": sln, "flag": flag,
                        "lo": lo, "bcC": bcCv, "gidx": cur_gidx,
                        "m_tts": m_tts}

                prev_par = par
                prev_S = S
                prev_flag = flag
                first_sub_global = False
                sub_idx[0] += 1

            if pend is not None:
                emit_ymul(pend)
                emit_yred(pend)
                pend = None

            xs = xbv[:, :, :, 0:tcn]
            Dbc = bview[:, 1, :]
            tmp = sp1.rearrange("p (dc b t) -> p dc b t", dc=DC, b=BL)[
                :, :, :, 0:tcn]
            E.op(E.vec, "tensor_mul", tmp, lnv[:, :, :, 0:tcn],
                 Dbc[:, :, None, None].broadcast_to([128, DC, BL, tcn]))
            E.drain(E.vec)
            E.op(E.vec, "tensor_add", xs, xs, yv[:, :, :, 0:tcn])
            E.drain(E.vec)
            E.op(E.vec, "tensor_add", xs, xs, tmp, inc=E.s_v)
            E.drain(E.vec)
            m_xres = E.s_v.n
            E.wait(E.sync, E.s_v, m_xres)
            for b in range(BL):
                E.op(E.sync, "dma_start", out=dst_aa[:, :, b, tc0:tc0 + tcn],
                     in_=xbv[:, :, b, 0:tcn], inc=E.s_x, inck=16)
            m_xst = E.s_x.n

        E.wait(E.sync, E.s_v, E.s_v.n)
        E.wait(E.sync, E.s_s, E.s_s.n)
        if "aa_ln1" in dbg_outs:
            E.op(E.sync, "dma_start", out=dbg_outs["aa_ln1"], in_=ln1[:, :],
                 inc=E.s_x, inck=16)
        if "aa_dt" in dbg_outs:
            E.op(E.sync, "dma_start", out=dbg_outs["aa_dt"], in_=dt_b[:, :],
                 inc=E.s_x, inck=16)
        if "aa_y" in dbg_outs:
            E.op(E.sync, "dma_start", out=dbg_outs["aa_y"], in_=ybuf[:, :],
                 inc=E.s_x, inck=16)
        E.barrier()
        ph.close()

    def emit_ffn(tag):
        FT = 512
        TOK = BL * FT
        ph = ExitStack()

        def psb(name, shape, dt=F32):
            return ph.enter_context(nc.sbuf_tensor(tag + name, list(shape), dt))

        wslot = [psb(f"w{i}", [128, 4096], BF16) for i in range(4)]
        shg = psb("shg", [128, 4096], BF16)
        shu = psb("shu", [128, 4096], BF16)
        shd = psb("shd", [128, 4096], BF16)
        ln2b = psb("ln2b", [128, DC * TOK], BF16)
        sqb2 = psb("sqb", [128, DC * BL * 256], BF16)
        ybuf = psb("y", [128, DC * TOK])
        interb = psb("inter", [128, IC * FT], BF16)
        sgbuf = psb("sg", [128, 2 * FT])
        uwbuf = psb("uw", [128, 2 * FT])
        wbcs = psb("wbc", [128, 2 * TOK])
        gbuf = psb("g", [128, 1536])
        wfm = psb("wfm", [NST, TOK])
        xbuf_f = psb("x", [128, DC * TOK])
        wfmh = psb("wfmh", [NST, 2 * TOK], BF16)
        whl = psb("whl", [128, 16 * NST], BF16)

        E.op(E.gps, "dma_start",
             out=shg.rearrange("p (kc m) -> p kc m", kc=DC),
             in_=sgT.rearrange("(kc p) m -> p kc m", p=128), inc=E.s_wsh, inck=16)
        E.op(E.gps, "dma_start",
             out=shu.rearrange("p (kc m) -> p kc m", kc=DC),
             in_=suT.rearrange("(kc p) m -> p kc m", p=128), inc=E.s_wsh, inck=16)
        E.op(E.gps, "dma_start",
             out=shd.rearrange("p (kc m) -> p kc m", kc=IC),
             in_=sdT.rearrange("(kc p) m -> p kc m", p=128), inc=E.s_wsh, inck=16)
        m_shw = E.s_wsh.n

        n_chunks = (Ll + FT - 1) // FT
        slot_free = [0, 0, 0, 0]
        wm = {}
        sidx = [0]

        def stream_mat(ci, e, j):
            src = (gprojT, uprojT, dprojT)[j][e]
            sl = sidx[0] % 4
            sidx[0] += 1
            E.wait(E.gps, E.s_t, slot_free[sl])
            kcn = DC if j < 2 else IC
            E.op(E.gps, "dma_start",
                 out=wslot[sl].rearrange("p (kc m) -> p kc m", kc=kcn),
                 in_=src.rearrange("(kc p) m -> p kc m", p=128),
                 inc=E.s_wsl[sl], inck=16)
            wm[(ci, e, j)] = (sl, E.s_wsl[sl].n)

        if dbg_outs:
            E.op(E.vec, "memset", ln2b[:, :].bitcast(I32), 0)
            E.op(E.vec, "memset", ybuf[:, :], 0.0)
            E.op(E.vec, "memset", gbuf[:, :], 0.0)
            E.drain(E.vec)

        for j in range(3):
            stream_mat(0, 0, j)

        src_f, dst_f = phase_io(tag[:2])
        m_xst = 0
        for ci in range(n_chunks):
            tc0 = ci * FT
            tcn = min(FT, Ll - tc0)
            TTOK = BL * tcn

            xbv = xbuf_f.rearrange("p (dc b t) -> p dc b t", dc=DC, b=BL)
            if m_xst:
                E.wait(E.sync, E.s_x, m_xst)
            for b in range(BL):
                E.op(E.sync, "dma_start", out=xbv[:, :, b, 0:tcn],
                     in_=src_f[:, :, b, tc0:tc0 + tcn], inc=E.s_x, inck=16)
            m_xld = E.s_x.n
            for h0 in range(0, tcn, 256):
                hn = min(256, tcn - h0)
                m_ln = emit_ln(tc0 + h0, hn, ybuf, ln2b, sqb2, toff=h0,
                               xs=xbv[:, :, :, h0:h0 + hn], wait_x=m_xld,
                               pre_wait_t=(E.last_ln_mst if h0 > 0 else None))
            lbv = ln2b.rearrange("p (dc b t) -> p dc b t", dc=DC, b=BL)

            E.wait(E.vec, E.s_v, m_ln)
            E.op(E.vec, "memset", ybuf[:, :], 0.0, inc=E.s_v)
            E.drain(E.vec)
            m_y0 = E.s_v.n

            tpb = (tcn + 127) // 128
            ntile = BL * tpb
            E.op(E.vec, "memset", pB[:, 0:128], 0.0, inc=E.s_v)
            E.drain(E.vec)
            E.wait(E.ten, E.s_v, E.s_v.n)
            E.wait(E.ten, E.s_v, m_y0)
            for ti in range(ntile):
                b, tq = ti // tpb, ti % tpb
                tl = min(128, tcn - tq * 128)
                dst = pB[0:tl, ti * NST:(ti + 1) * NST]
                for kc in range(DC):
                    E.op(E.ten, "matmul", dst,
                         lbv[:, kc, b, tq * 128:tq * 128 + tl],
                         gw_sb.rearrange("p (kc e) -> p kc e", kc=DC)[:, kc, :],
                         start=(kc == 0), stop=(kc == DC - 1),
                         inc=(E.s_t if (ti == ntile - 1 and kc == DC - 1)
                              else None))
            m_glog = E.s_t.n

            gat = gbuf[:, 0:128]
            E.wait(E.sca, E.s_t, m_glog)
            E.op(E.sca, "activation", gat, pB[:, 0:128], ACT.Tanh, scale=0.5,
                 inc=E.s_s)
            m_gth = E.s_s.n
            E.wait(E.vec, E.s_s, m_gth)
            E.op(E.vec, "tensor_scalar", out=gat, in0=gat, scalar1=0.5,
                 scalar2=0.5, op0=ALU.mult, op1=ALU.add)
            E.drain(E.vec)
            sc = gbuf[:, 128:256]
            E.op(E.vec, "tensor_tensor",
                 sc.rearrange("p (ti e) -> p ti e", e=NST),
                 gat.rearrange("p (ti e) -> p ti e", e=NST),
                 ebias_sb[:, None, :].broadcast_to([128, 8, NST]), ALU.add)
            E.drain(E.vec)
            scg = sc.rearrange("p (ti g j) -> p ti g j", g=4, j=4)
            Mx = gbuf[:, 256:320].rearrange("p (ti g j) -> p ti g j", g=4, j=2)
            mn = gbuf[:, 320:384].rearrange("p (ti g j) -> p ti g j", g=4, j=2)
            E.op(E.vec, "tensor_max", Mx, scg[:, :, :, 0:2], scg[:, :, :, 2:4])
            E.op(E.vec, "tensor_tensor", mn, scg[:, :, :, 0:2],
                 scg[:, :, :, 2:4], ALU.min)
            E.drain(E.vec)
            gsv = gbuf[:, 384:416].rearrange("p (ti g) -> p ti g", g=4)
            t1g = gbuf[:, 416:448].rearrange("p (ti g) -> p ti g", g=4)
            t2g = gbuf[:, 448:480].rearrange("p (ti g) -> p ti g", g=4)
            E.op(E.vec, "tensor_max", gsv, Mx[:, :, :, 0], Mx[:, :, :, 1])
            E.op(E.vec, "tensor_tensor", t1g, Mx[:, :, :, 0], Mx[:, :, :, 1],
                 ALU.min)
            E.op(E.vec, "tensor_max", t2g, mn[:, :, :, 0], mn[:, :, :, 1])
            E.drain(E.vec)
            E.op(E.vec, "tensor_max", t1g, t1g, t2g)
            E.drain(E.vec)
            E.op(E.vec, "tensor_add", gsv, gsv, t1g)
            E.drain(E.vec)
            gm1 = gbuf[:, 480:488]
            hit = gbuf[:, 488:520].rearrange("p (ti g) -> p ti g", g=4)
            cums = gbuf[:, 520:552].rearrange("p (ti g) -> p ti g", g=4)
            c2 = gbuf[:, 552:584].rearrange("p (ti g) -> p ti g", g=4)
            gmask = t2g
            E.op(E.vec, "memset", gmask, 0.0)
            E.drain(E.vec)
            for _ in range(2):
                E.op(E.vec, "tensor_reduce", gm1, gsv, AX.X, ALU.max)
                E.drain(E.vec)
                E.op(E.vec, "tensor_tensor", hit, gsv,
                     gm1.rearrange("p ti -> p ti")[:, :, None].broadcast_to(
                         [128, 8, 4]), ALU.is_equal)
                E.drain(E.vec)
                E.op(E.vec, "tensor_copy", cums[:, :, 0:1], hit[:, :, 0:1])
                E.op(E.vec, "tensor_add", cums[:, :, 1:4], hit[:, :, 1:4],
                     hit[:, :, 0:3])
                E.drain(E.vec)
                E.op(E.vec, "tensor_copy", c2[:, :, 0:2], cums[:, :, 0:2])
                E.op(E.vec, "tensor_add", c2[:, :, 2:4], cums[:, :, 2:4],
                     cums[:, :, 0:2])
                E.drain(E.vec)
                E.op(E.vec, "tensor_single_scalar", c2, c2, 1.0, ALU.is_equal)
                E.drain(E.vec)
                E.op(E.vec, "tensor_mul", hit, hit, c2)
                E.drain(E.vec)
                E.op(E.vec, "tensor_add", gmask, gmask, hit)
                E.drain(E.vec)
                E.op(E.vec, "scalar_tensor_tensor", out=gsv, in0=hit,
                     scalar=-1e30, in1=gsv, op0=ALU.mult, op1=ALU.add)
                E.drain(E.vec)
            tmp = gbuf[:, 584:712].rearrange("p (ti g j) -> p ti g j", g=4, j=4)
            E.op(E.vec, "tensor_mul", tmp, scg,
                 gmask[:, :, :, None].broadcast_to([128, 8, 4, 4]))
            E.drain(E.vec)
            tmpv = gbuf[:, 584:712].rearrange("p (ti e) -> p ti e", e=NST)
            sel = gbuf[:, 712:840].rearrange("p (ti e) -> p ti e", e=NST)
            m16 = gbuf[:, 840:848]
            h16 = gbuf[:, 848:976].rearrange("p (ti e) -> p ti e", e=NST)
            c16a = gbuf[:, 976:1104].rearrange("p (ti e) -> p ti e", e=NST)
            c16b = gbuf[:, 1104:1232].rearrange("p (ti e) -> p ti e", e=NST)
            E.op(E.vec, "memset", sel, 0.0)
            E.drain(E.vec)
            for _ in range(4):
                E.op(E.vec, "tensor_reduce", m16, tmpv, AX.X, ALU.max)
                E.drain(E.vec)
                E.op(E.vec, "tensor_tensor", h16, tmpv,
                     m16[:, :, None].broadcast_to([128, 8, NST]), ALU.is_equal)
                E.drain(E.vec)
                srcc, dstc = h16, c16a
                for sh in (1, 2, 4, 8):
                    E.op(E.vec, "tensor_copy", dstc[:, :, 0:sh],
                         srcc[:, :, 0:sh])
                    E.op(E.vec, "tensor_add", dstc[:, :, sh:NST],
                         srcc[:, :, sh:NST], srcc[:, :, 0:NST - sh])
                    E.drain(E.vec)
                    srcc, dstc = dstc, (c16b if dstc is c16a else c16a)
                E.op(E.vec, "tensor_single_scalar", srcc, srcc, 1.0,
                     ALU.is_equal)
                E.drain(E.vec)
                E.op(E.vec, "tensor_mul", h16, h16, srcc)
                E.drain(E.vec)
                E.op(E.vec, "tensor_add", sel, sel, h16)
                E.drain(E.vec)
                E.op(E.vec, "scalar_tensor_tensor", out=tmpv, in0=h16,
                     scalar=-1e30, in1=tmpv, op0=ALU.mult, op1=ALU.add)
                E.drain(E.vec)
            wre = gbuf[:, 1232:1360].rearrange("p (ti e) -> p ti e", e=NST)
            rs = gbuf[:, 1360:1368]
            E.op(E.vec, "tensor_mul", wre, sel,
                 gat.rearrange("p (ti e) -> p ti e", e=NST))
            E.drain(E.vec)
            E.op(E.vec, "tensor_reduce", rs, wre, AX.X, ALU.add)
            E.drain(E.vec)
            E.op(E.vec, "tensor_scalar_add", rs, rs, 1e-20)
            E.drain(E.vec)
            E.op(E.vec, "reciprocal", rs, rs)
            E.drain(E.vec)
            E.op(E.vec, "scalar_tensor_tensor", out=wre, in0=wre, scalar=2.5,
                 in1=rs.rearrange("p ti -> p ti")[:, :, None].broadcast_to(
                     [128, 8, NST]), op0=ALU.mult, op1=ALU.mult)
            E.drain(E.vec)

            for ti in range(ntile):
                hi = whl[:, 2 * ti * NST:(2 * ti + 1) * NST]
                lo = whl[:, (2 * ti + 1) * NST:(2 * ti + 2) * NST]
                E.op(E.vec, "tensor_copy", hi, wre[:, ti, :])
                E.drain(E.vec)
                E.op(E.vec, "tensor_sub", gbuf[:, 1368:1384],
                     wre[:, ti, :], hi)
                E.drain(E.vec)
                E.op(E.vec, "tensor_copy", lo,
                     gbuf[:, 1368:1384], inc=E.s_v)
                m_whl = E.s_v.n
                E.wait(E.ten, E.s_v, m_whl)
                E.wait(E.ten, E.s_s, m_gth)
                dst = pB[0:NST, ti * 128:(ti + 1) * 128]
                E.op(E.ten, "matmul", dst, hi, ident[:, :],
                     start=True, stop=False)
                E.op(E.ten, "matmul", dst, lo, ident[:, :],
                     start=False, stop=True,
                     inc=(E.s_t if ti == ntile - 1 else None))
            m_wtr = E.s_t.n
            E.wait(E.vec, E.s_t, m_wtr)
            for ti in range(ntile):
                b, tq = ti // tpb, ti % tpb
                tl = min(128, tcn - tq * 128)
                E.op(E.vec, "tensor_copy",
                     wfm[:, b * tcn + tq * 128:b * tcn + tq * 128 + tl],
                     pB[0:NST, ti * 128:ti * 128 + tl])
            E.drain(E.vec)
            E.op(E.vec, "tensor_copy", wfmh[:, 0:TTOK], wfm[:, 0:TTOK])
            E.drain(E.vec)
            E.op(E.vec, "tensor_sub", wfm[:, 0:TTOK], wfm[:, 0:TTOK],
                 wfmh[:, 0:TTOK])
            E.drain(E.vec)
            E.op(E.vec, "tensor_copy", wfmh[:, TOK:TOK + TTOK], wfm[:, 0:TTOK],
                 inc=E.s_v)
            E.drain(E.vec)
            m_wfm = E.s_v.n

            wbc_used = [0, 0]
            m_int_done = 0
            sg_marks = [0, 0]
            uw_marks = [m_wfm, m_wfm]
            int_marks = [0, 0]
            m_yadd = m_y0
            iv = interb.rearrange("p (ic t) -> p ic t", ic=IC)
            for e in range(NE + 1):
                shared = (e == NE)
                if not shared:
                    epar = e % 2
                    E.wait(E.ten, E.s_v, m_wfm)
                    if wbc_used[epar]:
                        E.wait(E.ten, E.s_v, wbc_used[epar])
                    nq = (TTOK + 511) // 512
                    for q in range(nq):
                        qn = min(512, TTOK - q * 512)
                        dst = pD[:, q * 512:q * 512 + qn]
                        E.op(E.ten, "matmul", dst,
                             sel16[:, e * 128:(e + 1) * 128],
                             wfmh[:, q * 512:q * 512 + qn],
                             start=True, stop=False)
                        E.op(E.ten, "matmul", dst,
                             sel16[:, e * 128:(e + 1) * 128],
                             wfmh[:, TOK + q * 512:TOK + q * 512 + qn],
                             start=False, stop=True,
                             inc=(E.s_t if q == nq - 1 else None))
                    m_wbc = E.s_t.n
                    E.wait(E.vec, E.s_t, m_wbc)
                    E.op(E.vec, "tensor_copy", wbcs[:, epar * TOK:epar * TOK + TTOK],
                         pD[:, 0:TTOK], inc=E.s_v)
                    E.drain(E.vec)
                    m_wbcs_evac = E.s_v.n
                    slg, mg = wm[(ci, e, 0)]
                    slu, mu = wm[(ci, e, 1)]
                    sld, md = wm[(ci, e, 2)]
                    gv = wslot[slg].rearrange("p (kc m) -> p kc m", kc=DC)
                    uv = wslot[slu].rearrange("p (kc m) -> p kc m", kc=DC)
                    dv = wslot[sld].rearrange("p (kc m) -> p kc m", kc=IC)
                else:
                    E.wait(E.ten, E.s_wsh, m_shw)
                    gv = shg.rearrange("p (kc m) -> p kc m", kc=DC)
                    uv = shu.rearrange("p (kc m) -> p kc m", kc=DC)
                    dv = shd.rearrange("p (kc m) -> p kc m", kc=IC)

                for b in range(BL):
                    for icq in range(IC):
                        gp = pA[:, (icq % 2) * 512:(icq % 2) * 512 + tcn]
                        up = pB[:, (icq % 2) * 512:(icq % 2) * 512 + tcn]
                        if not shared:
                            E.wait(E.ten, E.s_wsl[slg], mg)
                            E.wait(E.ten, E.s_wsl[slu], mu)
                        if sg_marks[icq % 2]:
                            E.wait(E.ten, E.s_s, sg_marks[icq % 2])
                        for kc in range(DC):
                            E.op(E.ten, "matmul", gp,
                                 gv[:, kc, icq * 128:(icq + 1) * 128],
                                 lbv[:, kc, b, 0:tcn],
                                 start=(kc == 0), stop=(kc == DC - 1),
                                 inc=(E.s_t if kc == DC - 1 else None))
                        m_gmm = E.s_t.n
                        if uw_marks[icq % 2]:
                            E.wait(E.ten, E.s_v, uw_marks[icq % 2])
                        for kc in range(DC):
                            E.op(E.ten, "matmul", up,
                                 uv[:, kc, icq * 128:(icq + 1) * 128],
                                 lbv[:, kc, b, 0:tcn],
                                 start=(kc == 0), stop=(kc == DC - 1),
                                 inc=(E.s_t if kc == DC - 1 else None))
                        m_umm = E.s_t.n
                        sgsl = sgbuf[:, (icq % 2) * FT:(icq % 2) * FT + tcn]
                        uwsl = uwbuf[:, (icq % 2) * FT:(icq % 2) * FT + tcn]
                        E.wait(E.sca, E.s_t, m_gmm)
                        if int_marks[icq % 2]:
                            E.wait(E.sca, E.s_v, int_marks[icq % 2])
                        E.op(E.sca, "activation", sgsl, gp, ACT.Silu,
                             inc=E.s_s)
                        sg_marks[icq % 2] = E.s_s.n
                        E.wait(E.vec, E.s_t, m_umm)
                        if not shared:
                            E.op(E.vec, "tensor_mul", uwsl, up,
                                 wbcs[:, epar * TOK + b * tcn:
                                      epar * TOK + b * tcn + tcn], inc=E.s_v)
                        else:
                            E.op(E.vec, "tensor_copy", uwsl, up, inc=E.s_v)
                        uw_marks[icq % 2] = E.s_v.n
                        E.drain(E.vec)
                        E.wait(E.vec, E.s_s, sg_marks[icq % 2])
                        E.op(E.vec, "tensor_mul", iv[:, icq, 0:tcn],
                             sgsl, uwsl, inc=E.s_v)
                        E.drain(E.vec)
                        int_marks[icq % 2] = E.s_v.n
                    m_int_done = E.s_v.n

                    if not shared:
                        E.wait(E.ten, E.s_wsl[sld], md)
                    E.wait(E.ten, E.s_v, m_int_done)
                    yv = ybuf.rearrange("p (hc b t) -> p hc b t", hc=HC, b=BL)
                    for hc in range(HC):
                        off = (hc % 2) * 512
                        if hc >= 2:
                            E.wait(E.ten, E.s_v, yadd_marks[hc - 2])
                        elif m_yadd:
                            E.wait(E.ten, E.s_v, m_yadd)
                        if hc == 0:
                            yadd_marks = [0] * HC
                        for kci in range(IC):
                            E.op(E.ten, "matmul", pC[:, off:off + tcn],
                                 dv[:, kci, hc * 128:(hc + 1) * 128],
                                 iv[:, kci, 0:tcn],
                                 start=(kci == 0), stop=(kci == IC - 1),
                                 inc=(E.s_t if kci == IC - 1 else None))
                        m_dmm = E.s_t.n
                        E.wait(E.vec, E.s_t, m_dmm)
                        E.op(E.vec, "tensor_add", yv[:, hc, b, 0:tcn],
                             yv[:, hc, b, 0:tcn], pC[:, off:off + tcn],
                             inc=E.s_v)
                        E.drain(E.vec)
                        yadd_marks[hc] = E.s_v.n
                    m_yadd = E.s_v.n

                if not shared:
                    wbc_used[epar] = uw_marks[(IC - 1) % 2]
                    slot_free[slg] = m_gmm
                    slot_free[slu] = m_umm
                    slot_free[sld] = m_dmm
                    if e + 1 < NE:
                        for j in range(3):
                            stream_mat(ci, e + 1, j)
                    elif ci + 1 < n_chunks:
                        for j in range(3):
                            stream_mat(ci + 1, 0, j)

            if ci == 0 and ("f_w" in dbg_outs):
                E.wait(E.sync, E.s_v, E.s_v.n)
                E.op(E.sync, "dma_start", out=dbg_outs["f_w"],
                     in_=gbuf[:, 1232:1360], inc=E.s_x, inck=16)
                E.op(E.sync, "dma_start", out=dbg_outs["f_y"],
                     in_=ybuf[:, 0:DC * TOK], inc=E.s_x, inck=16)
                E.op(E.sync, "dma_start", out=dbg_outs["f_ln"],
                     in_=ln2b[:, :].bitcast(I32).bitcast(F32)[:, 0:DC * TOK // 2],
                     inc=E.s_x, inck=16)
                E.wait(E.sync, E.s_x, E.s_x.n)
            xs = xbv[:, :, :, 0:tcn]
            yvv = ybuf.rearrange("p (hc b t) -> p hc b t", hc=HC, b=BL)
            E.wait(E.vec, E.s_v, m_yadd)
            E.op(E.vec, "tensor_add", xs, xs, yvv[:, :, :, 0:tcn], inc=E.s_v)
            E.drain(E.vec)
            m_xres = E.s_v.n
            E.wait(E.sync, E.s_v, m_xres)
            for b in range(BL):
                E.op(E.sync, "dma_start", out=dst_f[:, :, b, tc0:tc0 + tcn],
                     in_=xbv[:, :, b, 0:tcn], inc=E.s_x, inck=16)
            m_xst = E.s_x.n

        E.barrier()
        ph.close()
    if "f1" in phases:
        emit_ffn("f1_")

    def emit_attn_b():
        BT = 96
        ph = ExitStack()

        def psb(name, shape, dt=F32):
            return ph.enter_context(nc.sbuf_tensor("ab" + name, list(shape), dt))

        CW = BL * BT
        SR = BT + 1
        wr_sb = psb("wr", [128, DC * D], BF16)
        wi_sb = psb("wi", [128, DC * D], BF16)
        wo_sb = psb("wo", [128, DC * D], BF16)
        ext = psb("ext", [128, DC * BL * (BT + 3)])
        xbuf_b = psb("x", [128, DC * CW])
        ln3b = psb("ln3b", [128, DC * CW], BF16)
        sqb3 = psb("sqb3", [128, DC * CW], BF16)
        xc = psb("xc", [128, DC * CW])
        xcb = psb("xcb", [128, DC * CW], BF16)
        thr = psb("thr", [128, DC * CW])
        thi = psb("thi", [128, DC * CW])
        u8 = psb("u8", [128, DC * CW])
        abuf = psb("a", [128, DC * CW])
        btb = psb("bt", [128, DC * CW])
        tmpb = psb("tmp", [128, DC * CW])
        Ab = psb("A", [128, DC * CW])
        hbf = psb("hbf", [128, DC * CW], BF16)
        obuf = psb("o", [128, DC * CW])
        la_s = psb("las", [128, 16 * SR])
        q2_s = psb("q2s", [128, 16 * SR])
        lacum = psb("lacum", [128, 16 * SR])
        q2cum = psb("q2cum", [128, 16 * SR])
        ones_t = psb("onest", [128, 16 * SR])
        carry2 = psb("carry2", [128, 32])

        for dst, src in ((wr_sb, w_rT), (wi_sb, w_iT), (wo_sb, w_oT)):
            E.op(E.gps, "dma_start",
                 out=dst.rearrange("p (kc m) -> p kc m", kc=DC),
                 in_=src.rearrange("(kc p) m -> p kc m", p=128),
                 inc=E.s_w, inck=16)
        m_wab = E.s_w.n

        E.op(E.vec, "tensor_single_scalar", bview[:, 3, :], bview[:, 3, :],
             0.5, ALU.mult)
        E.op(E.vec, "tensor_single_scalar", bview[:, 4, :], bview[:, 4, :],
             0.5, ALU.mult)
        E.op(E.vec, "tensor_single_scalar", bview[:, 5, :], bview[:, 5, :],
             0.5, ALU.mult)
        E.drain(E.vec)
        E.op(E.vec, "memset", ones_t[:, :], 1.0)
        E.drain(E.vec)
        E.op(E.vec, "memset",
             ones_t.rearrange("p (r c) -> p r c", c=SR)[:, :, 0:1], 0.0)
        E.op(E.vec, "memset", carry2[:, :], 0.0)
        E.op(E.vec, "memset",
             ext.rearrange("p (dc b t) -> p dc b t", dc=DC, b=BL)[:, :, :, 0:3],
             0.0, inc=E.s_v)
        E.drain(E.vec)
        if dbg_outs:
            for bfr in (obuf, xc, u8, abuf, btb, Ab):
                E.op(E.vec, "memset", bfr[:, :], 0.0)
            E.drain(E.vec)

        src_b, dst_b = phase_io("ab")
        m_xst = 0
        n_chunks = (Ll + BT - 1) // BT
        sp_prev = None
        cur_sr = [BT + 1]

        def set_sr(srn):
            if cur_sr[0] != srn:
                cur_sr[0] = srn
                E.op(E.vec, "memset", ones_t[:, 0:16 * srn], 1.0)
                E.drain(E.vec)
                E.op(E.vec, "memset", ones_t[:, 0:16 * srn].rearrange(
                    "p (r c) -> p r c", c=srn)[:, :, 0:1], 0.0)
                E.drain(E.vec)
        for ci in range(n_chunks):
            tc0 = ci * BT
            tcn = min(BT, Ll - tc0)
            TT = BL * tcn
            extv = ext.rearrange("p (dc b t) -> p dc b t", dc=DC, b=BL)
            if ci > 0:
                E.op(E.vec, "tensor_copy", extv[:, :, :, 0:3],
                     extv[:, :, :, prev_tcn:prev_tcn + 3])
                E.drain(E.vec)
            prev_tcn = tcn
            xrv = xbuf_b.rearrange("p (dc b t) -> p dc b t", dc=DC, b=BL)
            if m_xst:
                E.wait(E.sync, E.s_x, m_xst)
            for b in range(BL):
                E.op(E.sync, "dma_start", out=xrv[:, :, b, 0:tcn],
                     in_=src_b[:, :, b, tc0:tc0 + tcn], inc=E.s_x, inck=16)
            m_xld = E.s_x.n
            m_ln = emit_ln(tc0, tcn, None, ln3b, sqb3, toff=0,
                           f32view=extv[:, :, :, 3:3 + tcn],
                           pre_wait_t=(m_ms if ci > 0 else None),
                           xs=xrv[:, :, :, 0:tcn], wait_x=m_xld)

            xcv = xc.rearrange("p (dc b t) -> p dc b t", dc=DC, b=BL)[
                :, :, :, 0:tcn]
            tv = tmpb.rearrange("p (dc b t) -> p dc b t", dc=DC, b=BL)[
                :, :, :, 0:tcn]
            cwv = conv_w.rearrange("p (dc k) -> p dc k", dc=DC)
            E.wait(E.vec, E.s_v, m_ln)
            E.op(E.vec, "tensor_copy", xcv,
                 bview[:, 2, :, None, None].broadcast_to([128, DC, BL, tcn]))
            E.drain(E.vec)
            for k in range(4):
                E.op(E.vec, "tensor_mul", tv, extv[:, :, :, k:k + tcn],
                     cwv[:, :, k, None, None].broadcast_to([128, DC, BL, tcn]))
                E.drain(E.vec)
                E.op(E.vec, "tensor_add", xcv, xcv, tv)
                E.drain(E.vec)
            E.op(E.vec, "engine_nop", inc=E.s_v)
            m_xc = E.s_v.n
            E.wait(E.sca, E.s_v, m_xc)
            E.op(E.sca, "copy", xcb[:, 0:DC * TT] if TT == CW // 1 and False
                 else xcb.rearrange("p (dc b t) -> p dc b t", dc=DC, b=BL)[
                     :, :, :, 0:tcn],
                 xcv, inc=E.s_s)
            m_xcb = E.s_s.n

            xbv = xcb.rearrange("p (dc b t) -> p dc b t", dc=DC, b=BL)
            E.wait(E.ten, E.s_s, m_xcb)
            if ci == 0:
                E.wait(E.ten, E.s_w, m_wab)
            for wsb, bslot, dstb in ((wr_sb, 3, thr), (wi_sb, 4, thi)):
                wv = wsb.rearrange("p (kc m) -> p kc m", kc=DC)
                dv3 = dstb.rearrange("p (dc b t) -> p dc b t", dc=DC, b=BL)
                for oc in range(DC):
                    dst = pA[:, (oc % 2) * 512:(oc % 2) * 512 + TT].rearrange(
                        "p (b t) -> p b t", b=BL)
                    if sp_prev is not None:
                        E.wait(E.ten, E.s_s, sp_prev)
                    for kc in range(DC):
                        E.op(E.ten, "matmul", dst,
                             wv[:, kc, oc * 128:(oc + 1) * 128],
                             xbv[:, kc, :, 0:tcn],
                             start=(kc == 0), stop=(kc == DC - 1),
                             inc=(E.s_t if kc == DC - 1 else None))
                    m_mm = E.s_t.n
                    E.wait(E.sca, E.s_t, m_mm)
                    E.op(E.sca, "activation", dv3[:, oc, :, 0:tcn], dst,
                         ACT.Tanh, scale=0.5, bias=bview[:, bslot, oc:oc + 1],
                         inc=E.s_s)
                    if oc >= 1 or wsb is wi_sb:
                        sp_prev = E.s_s.n
                    else:
                        sp_prev = E.s_s.n
            m_ri = E.s_s.n

            thrv = thr.rearrange("p (dc b t) -> p dc b t", dc=DC, b=BL)[
                :, :, :, 0:tcn]
            u8v = u8.rearrange("p (dc b t) -> p dc b t", dc=DC, b=BL)[
                :, :, :, 0:tcn]
            E.wait(E.vec, E.s_s, m_ri)
            E.op(E.vec, "tensor_mul", u8v, thrv,
                 bview[:, 5, :, None, None].broadcast_to([128, DC, BL, tcn]))
            E.drain(E.vec)
            E.op(E.vec, "tensor_add", u8v, u8v,
                 bview[:, 5, :, None, None].broadcast_to([128, DC, BL, tcn]),
                 inc=E.s_v)
            E.drain(E.vec)
            m_u8 = E.s_v.n
            E.wait(E.sca, E.s_v, m_u8)
            av = abuf.rearrange("p (dc b t) -> p dc b t", dc=DC, b=BL)[
                :, :, :, 0:tcn]
            E.op(E.sca, "activation", av, u8v, ACT.Exp, inc=E.s_s)
            m_a = E.s_s.n
            lasv = la_s[:, 0:16 * (tcn + 1)].rearrange(
                "p (dc b c) -> p dc b c", dc=DC, b=BL)
            E.op(E.vec, "tensor_single_scalar", lasv[:, :, :, 1:1 + tcn],
                 u8v, LN17, ALU.max)
            E.drain(E.vec)
            c2v = carry2.rearrange("p (k dcb) -> p k dcb", k=2)
            E.op(E.vec, "tensor_copy",
                 lasv[:, :, :, 0].rearrange("p dc b -> p (dc b)"),
                 c2v[:, 0, :])
            E.drain(E.vec)
            set_sr(tcn + 1)
            E.op(E.vec, "tensor_tensor_scan", lacum[:, 0:16 * (tcn + 1)],
                 ones_t[:, 0:16 * (tcn + 1)],
                 la_s[:, 0:16 * (tcn + 1)], 0.0, ALU.mult, ALU.add)
            E.drain(E.vec)
            lcv = lacum[:, 0:16 * (tcn + 1)].rearrange(
                "p (dc b c) -> p dc b c", dc=DC, b=BL)
            E.op(E.vec, "tensor_copy", c2v[:, 0, :],
                 lcv[:, :, :, tcn].rearrange("p dc b -> p (dc b)"))
            E.drain(E.vec)
            E.op(E.vec, "tensor_single_scalar", lcv[:, :, :, 1:1 + tcn],
                 lcv[:, :, :, 1:1 + tcn], -80.0, ALU.max, inc=E.s_v)
            E.drain(E.vec)
            m_lc = E.s_v.n

            qv = tmpb.rearrange("p (dc b t) -> p dc b t", dc=DC, b=BL)[
                :, :, :, 0:tcn]
            E.wait(E.vec, E.s_s, m_a)
            E.op(E.vec, "tensor_mul", qv, av, av)
            E.drain(E.vec)
            E.op(E.vec, "tensor_scalar", out=qv, in0=qv, scalar1=-1.0,
                 scalar2=1.0, op0=ALU.mult, op1=ALU.add)
            E.drain(E.vec)
            E.op(E.vec, "tensor_scalar_max", qv, qv, 1e-6, inc=E.s_v)
            E.drain(E.vec)
            m_q = E.s_v.n
            E.wait(E.sca, E.s_v, m_q)
            E.op(E.sca, "activation", qv, qv, ACT.Ln)
            E.drain(E.sca)
            E.op(E.sca, "activation", qv, qv, ACT.Exp, scale=0.5, inc=E.s_s)
            m_sq = E.s_s.n
            Av = Ab.rearrange("p (dc b t) -> p dc b t", dc=DC, b=BL)[
                :, :, :, 0:tcn]
            E.wait(E.sca, E.s_v, m_lc)
            E.op(E.sca, "activation", Av, lcv[:, :, :, 1:1 + tcn], ACT.Exp)
            E.drain(E.sca)
            q2v = q2_s[:, 0:16 * (tcn + 1)].rearrange(
                "p (dc b c) -> p dc b c", dc=DC, b=BL)
            E.op(E.sca, "activation", q2v[:, :, :, 1:1 + tcn],
                 lcv[:, :, :, 1:1 + tcn], ACT.Exp, scale=-1.0, inc=E.s_s)
            m_enA = E.s_s.n

            btv = btb.rearrange("p (dc b t) -> p dc b t", dc=DC, b=BL)[
                :, :, :, 0:tcn]
            thiv = thi.rearrange("p (dc b t) -> p dc b t", dc=DC, b=BL)[
                :, :, :, 0:tcn]
            E.wait(E.vec, E.s_s, m_sq)
            E.op(E.vec, "tensor_scalar", out=thiv, in0=thiv, scalar1=0.5,
                 scalar2=0.5, op0=ALU.mult, op1=ALU.add)
            E.drain(E.vec)
            E.op(E.vec, "tensor_mul", btv, qv, thiv)
            E.drain(E.vec)
            E.op(E.vec, "tensor_mul", btv, btv, xcv)
            E.drain(E.vec)
            E.wait(E.vec, E.s_s, m_enA)
            E.op(E.vec, "tensor_mul", q2v[:, :, :, 1:1 + tcn],
                 q2v[:, :, :, 1:1 + tcn], btv)
            E.drain(E.vec)
            E.op(E.vec, "tensor_copy",
                 q2v[:, :, :, 0].rearrange("p dc b -> p (dc b)"), c2v[:, 1, :])
            E.drain(E.vec)
            E.op(E.vec, "tensor_tensor_scan", q2cum[:, 0:16 * (tcn + 1)],
                 ones_t[:, 0:16 * (tcn + 1)],
                 q2_s[:, 0:16 * (tcn + 1)], 0.0, ALU.mult, ALU.add)
            E.drain(E.vec)
            qcv = q2cum[:, 0:16 * (tcn + 1)].rearrange(
                "p (dc b c) -> p dc b c", dc=DC, b=BL)
            E.op(E.vec, "tensor_copy", c2v[:, 1, :],
                 qcv[:, :, :, tcn].rearrange("p dc b -> p (dc b)"))
            E.drain(E.vec)
            hv = hbf.rearrange("p (dc b t) -> p dc b t", dc=DC, b=BL)[
                :, :, :, 0:tcn]
            E.op(E.vec, "tensor_mul", hv, Av, qcv[:, :, :, 1:1 + tcn],
                 inc=E.s_v)
            E.drain(E.vec)
            m_h = E.s_v.n

            hbv = hbf.rearrange("p (dc b t) -> p dc b t", dc=DC, b=BL)
            ov3 = obuf.rearrange("p (dc b t) -> p dc b t", dc=DC, b=BL)
            sq3 = sqb3.rearrange("p (dc b t) -> p dc b t", dc=DC, b=BL)
            wov = wo_sb.rearrange("p (kc m) -> p kc m", kc=DC)
            E.wait(E.ten, E.s_v, m_h)
            for oc in range(DC):
                dst = pA[:, (oc % 2) * 512:(oc % 2) * 512 + TT].rearrange(
                    "p (b t) -> p b t", b=BL)
                if sp_prev is not None:
                    E.wait(E.ten, E.s_s, sp_prev)
                for kc in range(DC):
                    E.op(E.ten, "matmul", dst,
                         wov[:, kc, oc * 128:(oc + 1) * 128],
                         hbv[:, kc, :, 0:tcn],
                         start=(kc == 0), stop=(kc == DC - 1),
                         inc=(E.s_t if kc == DC - 1 else None))
                m_mm = E.s_t.n
                E.wait(E.sca, E.s_t, m_mm)
                E.op(E.sca, "copy", ov3[:, oc, :, 0:tcn], dst)
                E.drain(E.sca)
                E.op(E.sca, "activation", sq3[:, oc, :, 0:tcn], dst,
                     ACT.Square, inc=E.s_s)
                sp_prev = E.s_s.n
            m_oev = E.s_s.n

            E.wait(E.ten, E.s_s, m_oev)
            for b in range(BL):
                dst = pB[0:1, b * tcn:(b + 1) * tcn]
                for kc in range(DC):
                    E.op(E.ten, "matmul", dst, onescol[:, :],
                         sq3[:, kc, b, 0:tcn],
                         start=(kc == 0), stop=(kc == DC - 1),
                         inc=(E.s_t if (b == BL - 1 and kc == DC - 1)
                              else None))
            m_ms = E.s_t.n
            E.wait(E.vec, E.s_t, m_ms)
            msr = smalls[0:1, 0:TT]
            w1 = smalls[0:1, 512:512 + TT]
            w2 = smalls[0:1, 1024:1024 + TT]
            E.op(E.vec, "tensor_scalar", out=msr, in0=pB[0:1, 0:TT],
                 scalar1=1.0 / D, scalar2=1e-6, op0=ALU.mult, op1=ALU.add)
            E.drain(E.vec)
            emit_rsqrt(E, E.vec, msr, msr, w1, w2)
            rh = hlrows[0:1, 0:TT]
            rl = hlrows[0:1, 512:512 + TT]
            E.op(E.vec, "tensor_copy", rh, msr)
            E.drain(E.vec)
            E.op(E.vec, "tensor_sub", w1, msr, rh)
            E.drain(E.vec)
            E.op(E.vec, "tensor_copy", rl, w1, inc=E.s_v)
            m_rr = E.s_v.n
            E.wait(E.ten, E.s_v, m_rr)
            E.op(E.ten, "matmul", pC[:, 0:TT], ones1[:, :], rh,
                 start=True, stop=False)
            E.op(E.ten, "matmul", pC[:, 0:TT], ones1[:, :], rl,
                 start=False, stop=True, inc=E.s_t)
            m_rbc = E.s_t.n
            E.wait(E.vec, E.s_t, m_rbc)
            rbc = pC[:, 0:TT].rearrange("p (b t) -> p b t", b=BL)
            y3 = tmpb.rearrange("p (dc b t) -> p dc b t", dc=DC, b=BL)[
                :, :, :, 0:tcn]
            E.op(E.vec, "tensor_mul", y3, ov3[:, :, :, 0:tcn],
                 rbc[:, None, :, :].broadcast_to([128, DC, BL, tcn]))
            E.drain(E.vec)
            E.op(E.vec, "tensor_mul", y3, y3,
                 bview[:, 6, :, None, None].broadcast_to([128, DC, BL, tcn]))
            E.drain(E.vec)
            xs = xrv[:, :, :, 0:tcn]
            E.op(E.vec, "tensor_add", xs, xs, y3, inc=E.s_v)
            E.drain(E.vec)
            m_xres = E.s_v.n
            E.wait(E.sync, E.s_v, m_xres)
            for b in range(BL):
                E.op(E.sync, "dma_start", out=dst_b[:, :, b, tc0:tc0 + tcn],
                     in_=xrv[:, :, b, 0:tcn], inc=E.s_x, inck=16)
            m_xst = E.s_x.n

        if dbg_outs and "ab_o" in dbg_outs:
            E.wait(E.sync, E.s_v, E.s_v.n)
            E.op(E.sync, "dma_start", out=dbg_outs["ab_o"], in_=obuf[:, :],
                 inc=E.s_x, inck=16)
            E.op(E.sync, "dma_start", out=dbg_outs["ab_xc"], in_=xc[:, :],
                 inc=E.s_x, inck=16)
            E.wait(E.sync, E.s_x, E.s_x.n)
        E.barrier()
        ph.close()

    if "ab" in phases:
        emit_attn_b()

    if "f2" in phases:
        emit_ffn("f2_")


    if not ph_order:
        E.op(E.sync, "dma_start",
             out=x_out.rearrange("(dc p) b t -> p dc b t", p=128),
             in_=x_in.rearrange("(dc p) b t -> p dc b t", p=128),
             inc=E.s_x, inck=16)
    E.wait(E.sync, E.s_x, E.s_x.n)

    with ctx:
        blk = ctx.enter_context(nc.Block())
        blk.sync(lambda e: [f(e) for f in E.sync.q])
        blk.vector(lambda e: [f(e) for f in E.vec.q])
        blk.scalar(lambda e: [f(e) for f in E.sca.q])
        blk.tensor(lambda e: [f(e) for f in E.ten.q])
        blk.gpsimd(lambda e: [f(e) for f in E.gps.q])

    return nc, E, dbg_outs


def _prep_inputs(inp, core, Ll=2048):
    x = np.asarray(inp["L0_attn_a_x"], np.float32)
    xc = np.ascontiguousarray(
        np.transpose(x[2 * core:2 * core + 2, :Ll], (2, 0, 1)))

    def T(a):
        return np.ascontiguousarray(np.asarray(a, np.float32).T)

    def r128(v):
        return np.ascontiguousarray(np.asarray(v, np.float32).reshape(DC, 128).T)

    import ml_dtypes
    BFD = ml_dtypes.bfloat16
    sel = np.zeros((NST, NST * 128), BFD)
    for n in range(NST):
        sel[n, n * 128:(n + 1) * 128] = 1
    im = {
        "x_in": xc,
        "w_dtT": T(inp["L0_attn_a_W_dt"]),
        "w_BT": T(inp["L0_attn_a_W_B"]),
        "w_CT": T(inp["L0_attn_a_W_C"]),
        "b_dt_r": r128(inp["L0_attn_a_b_dt"]),
        "A_log_r": np.ascontiguousarray(
            np.asarray(inp["L0_attn_a_A_log"], np.float32)
            .reshape(DC, 128, NST).transpose(1, 0, 2).reshape(128, DC * NST)),
        "D_par_r": r128(inp["L0_attn_a_D_param"]),
        "conv_w_r": np.ascontiguousarray(
            np.asarray(inp["L0_attn_b_conv_weight"], np.float32)[:, 0, :]
            .reshape(DC, 128, 4).transpose(1, 0, 2).reshape(128, DC * 4)),
        "conv_b_r": r128(inp["L0_attn_b_conv_bias"]),
        "w_rT": T(inp["L0_attn_b_Wr_w"]),
        "w_iT": T(inp["L0_attn_b_Wi_w"]),
        "wr_b_r": r128(inp["L0_attn_b_Wr_b"]),
        "wi_b_r": r128(inp["L0_attn_b_Wi_b"]),
        "log_a_r": r128(inp["L0_attn_b_log_a"]),
        "w_oT": T(inp["L0_attn_b_out_w"]),
        "norm_w_r": r128(inp["L0_attn_b_norm_w"]),
        "gate_wT": T(inp["L0_ffn_gate_w"]),
        "e_bias_bc": np.ascontiguousarray(np.broadcast_to(
            np.asarray(inp["L0_ffn_e_bias"], np.float32)[None, :], (128, NE))),
        "gprojT": np.ascontiguousarray(
            np.asarray(inp["L0_ffn_gate_proj"], np.float32).transpose(0, 2, 1)),
        "uprojT": np.ascontiguousarray(
            np.asarray(inp["L0_ffn_up_proj"], np.float32).transpose(0, 2, 1)),
        "dprojT": np.ascontiguousarray(
            np.asarray(inp["L0_ffn_down_proj"], np.float32).transpose(0, 2, 1)),
        "sgT": T(inp["L0_ffn_shared_gate_w"]),
        "suT": T(inp["L0_ffn_shared_up_w"]),
        "sdT": T(inp["L0_ffn_shared_down_w"]),
        "c_ones1": np.ones((1, 128), BFD),
        "c_onescol": np.ones((128, 1), BFD),
        "c_ident": np.eye(128, dtype=BFD),
        "c_bias": np.ascontiguousarray(np.broadcast_to(np.array(
            [ATANH_BI, 1.0, LN17, 0.0], np.float32)[None, :], (128, 4))),
        "c_sel": sel,
    }
    return im


def kernel(**inputs):
    nc, E, _ = build_program()
    in_maps = [_prep_inputs(inputs, c) for c in range(8)]
    res = run_bass_kernel_spmd(nc, in_maps, core_ids=list(range(8)))
    outs = []
    for c in range(8):
        xc = res.results[c]["x_out"]
        outs.append(np.transpose(xc, (1, 2, 0)))
    return np.concatenate(outs, axis=0).astype(np.float32)


# version 12
# speedup vs baseline: 1.1081x; 1.0336x over previous
import numpy as np
from contextlib import ExitStack

import concourse.bass as bass
import concourse.mybir as mybir
from concourse.bass_utils import run_bass_kernel_spmd

F32 = mybir.dt.float32
BF16 = mybir.dt.bfloat16
I32 = mybir.dt.int32
ALU = mybir.AluOpType
ACT = mybir.ActivationFunctionType
AX = mybir.AxisListType

D = 1024
DC = 8
NST = 16
BL = 2
NE = 16
NI = 512
IC = 4
HC = 8

SP_C = (0.001 + 0.1) / 2.0
SP_R = (0.1 - 0.001) / 2.0
ATANH_SC = 0.5 / (SP_R + 1e-8)
ATANH_BI = float(-SP_C * 0.5 / (SP_R + 1e-8))
LN17 = float(np.log(np.float32(1e-7)))


class Stream:
    def __init__(self, name):
        self.name = name
        self.q = []


class Sem:
    def __init__(self, ctx, nc, name):
        self.h = ctx.enter_context(nc.semaphore(name))
        self.n = 0


class Emit:
    def __init__(self, ctx, nc):
        self.nc = nc
        self.sync = Stream("sync")
        self.vec = Stream("vec")
        self.sca = Stream("sca")
        self.ten = Stream("ten")
        self.gps = Stream("gps")
        self.s_x = Sem(ctx, nc, "s_x")
        self.s_w = Sem(ctx, nc, "s_w")
        self.s_v = Sem(ctx, nc, "s_v")
        self.s_s = Sem(ctx, nc, "s_s")
        self.s_t = Sem(ctx, nc, "s_t")
        self.s_g = Sem(ctx, nc, "s_g")
        self.s_b = Sem(ctx, nc, "s_b")
        self.s_wsl = [Sem(ctx, nc, f"s_wsl{i}") for i in range(4)]
        self.s_wsh = Sem(ctx, nc, "s_wsh")
        self.ninstr = 0

    def op(self, stream, method, *args, inc=None, inck=1, **kw):
        self.ninstr += 1

        def f(eng, method=method, args=args, kw=kw, inc=inc, inck=inck):
            ins = getattr(eng, method)(*args, **kw)
            if inc is not None:
                ins.then_inc(inc.h, inck)
        stream.q.append(f)
        if inc is not None:
            inc.n += inck
            return inc.n
        return None

    def wait(self, stream, sem, val):
        stream.q.append(lambda eng, s=sem, v=val: eng.wait_ge(s.h, v))

    def drain(self, stream):
        self.ninstr += 1
        stream.q.append(lambda eng: eng.drain())

    def barrier(self):
        streams = [self.sync, self.vec, self.sca, self.ten, self.gps]
        self.wait(self.sync, self.s_x, self.s_x.n)
        self.wait(self.gps, self.s_w, self.s_w.n)
        for sm in self.s_wsl:
            if sm.n:
                self.wait(self.gps, sm, sm.n)
        if self.s_wsh.n:
            self.wait(self.gps, self.s_wsh, self.s_wsh.n)
        for st in streams:
            if st not in (self.sync,):
                self.drain(st)
            st.q.append(lambda eng, h=self.s_b.h: eng.sem_inc(h, 1))
            self.s_b.n += 1
        tgt = self.s_b.n
        for st in streams:
            self.wait(st, self.s_b, tgt)


def split_ranges(total, step, first=None):
    out = []
    t = 0
    if first is not None and total > 0:
        f = min(first, total)
        out.append((0, f))
        t = f
    while t < total:
        out.append((t, min(step, total - t)))
        t += min(step, total - t)
    return out


def emit_rsqrt(E, st, out, in_, t1, t2, iters=3):
    ib, t1b = in_.bitcast(I32), t1.bitcast(I32)
    E.op(st, "tensor_single_scalar", t1b, ib, 1, ALU.arith_shift_right)
    E.drain(st)
    E.op(st, "tensor_single_scalar", t1b, t1b, -1, ALU.bitwise_xor)
    E.drain(st)
    E.op(st, "tensor_single_scalar", t1b, t1b, int(0x5F3759DF + 1), ALU.add)
    E.drain(st)
    for _ in range(iters):
        E.op(st, "tensor_mul", t2, t1, t1)
        E.drain(st)
        E.op(st, "tensor_mul", t2, t2, in_)
        E.drain(st)
        E.op(st, "tensor_scalar", out=t2, in0=t2, scalar1=-0.5, scalar2=1.5,
             op0=ALU.mult, op1=ALU.add)
        E.drain(st)
        E.op(st, "tensor_mul", t1, t1, t2)
        E.drain(st)
    E.op(st, "tensor_copy", out, t1)
    E.drain(st)


def attn_a_chunks(Ll):
    out = []
    t = 0
    first = True
    while t < Ll:
        if first:
            n = min(121, Ll - t)
            first = False
        else:
            n = min(120, Ll - t)
        out.append((t, n))
        t += n
    return out


def subchunks_of(tc0, tcn):
    subs = []
    t = tc0
    end = tc0 + tcn
    if tc0 == 0:
        subs.append((0, 1))
        t = 1
    while t < end:
        n = min(10, end - t)
        subs.append((t, n))
        t += n
    return subs


def build_program(Ll=2048, FF_T=256, B_TC=64,
                  phases=("aa", "f1", "ab", "f2"), dbg=()):
    nc = bass.Bass("TRN2", target_bir_lowering=False)

    def din(name, shape, dt=F32):
        return nc.dram_tensor(name, list(shape), dt, kind="ExternalInput").ap()

    x_in = din("x_in", [D, BL, Ll])
    x_out = nc.dram_tensor("x_out", [D, BL, Ll], F32, kind="ExternalOutput").ap()
    xs_dram = nc.dram_tensor("xs_dram", [D, BL, Ll], F32, kind="Internal").ap()
    ph_order = [p for p in ("aa", "f1", "ab", "f2") if p in phases]

    def phase_io(p):
        src = x_in if (ph_order and p == ph_order[0]) else xs_dram
        dst = x_out if (ph_order and p == ph_order[-1]) else xs_dram
        return (src.rearrange("(dc p) b t -> p dc b t", p=128),
                dst.rearrange("(dc p) b t -> p dc b t", p=128))
    w_dtT = din("w_dtT", [D, D])
    w_BT = din("w_BT", [D, NST])
    w_CT = din("w_CT", [D, NST])
    b_dt_r = din("b_dt_r", [128, DC])
    A_log_r = din("A_log_r", [128, DC * NST])
    D_par_r = din("D_par_r", [128, DC])
    conv_w_r = din("conv_w_r", [128, DC * 4])
    conv_b_r = din("conv_b_r", [128, DC])
    w_rT = din("w_rT", [D, D])
    w_iT = din("w_iT", [D, D])
    wr_b_r = din("wr_b_r", [128, DC])
    wi_b_r = din("wi_b_r", [128, DC])
    log_a_r = din("log_a_r", [128, DC])
    w_oT = din("w_oT", [D, D])
    norm_w_r = din("norm_w_r", [128, DC])
    gate_wT = din("gate_wT", [D, NE])
    e_bias_bc = din("e_bias_bc", [128, NE])
    gprojT = din("gprojT", [NE, D, NI])
    uprojT = din("uprojT", [NE, D, NI])
    dprojT = din("dprojT", [NE, NI, D])
    sgT = din("sgT", [D, NI])
    suT = din("suT", [D, NI])
    sdT = din("sdT", [NI, D])
    c_ones1 = din("c_ones1", [1, 128], BF16)
    c_onescol = din("c_onescol", [128, 1], BF16)
    c_ones128 = din("c_ones128", [128, 128], BF16)
    c_ident = din("c_ident", [128, 128], BF16)
    c_sel = din("c_sel", [NST, NST * 128], BF16)
    c_bias = din("c_bias", [128, 4])

    dbg_outs = {}
    for name, shape in dbg:
        dbg_outs[name] = nc.dram_tensor(
            "dbg_" + name, list(shape), F32, kind="ExternalOutput").ap()

    ctx = ExitStack()

    def sb(name, shape, dt=F32):
        return ctx.enter_context(nc.sbuf_tensor(name, list(shape), dt))

    A_tile = sb("A_tile", [128, DC * NST])
    biases = sb("biases", [128, DC * 8])
    conv_w = sb("conv_w", [128, DC * 4])
    gw_sb = sb("gw_sb", [128, DC * NE], BF16)
    wB_sb = sb("wB_sb", [128, DC * NST], BF16)
    wC_sb = sb("wC_sb", [128, DC * NST], BF16)
    ebias_sb = sb("ebias_sb", [128, NE])
    ones1 = sb("ones1", [1, 128], BF16)
    onescol = sb("onescol", [128, 1], BF16)
    ones128 = sb("ones128", [128, 128], BF16)
    ident = sb("ident", [128, 128], BF16)
    sel16 = sb("sel16", [NST, NST * 128], BF16)
    smalls = sb("smalls", [128, 3072])
    cbias = sb("cbias", [128, 4])
    hlrows = sb("hlrows", [128, 2560], BF16)

    E = Emit(ctx, nc)

    def psum(name, shape):
        return ctx.enter_context(nc.psum_tensor(name, list(shape), F32))

    pA = psum("pA", [128, 1024])
    pB = psum("pB", [128, 1024])
    pC = psum("pC", [128, 1024])
    pD = psum("pD", [128, 1024])

    bview = biases.rearrange("p (k dc) -> p k dc", k=8)

    for dst, src in ((ones1[:, :], c_ones1), (onescol[:, :], c_onescol),
                     (ones128[:, :], c_ones128),
                     (ident[:, :], c_ident), (sel16[:, :], c_sel),
                     (ebias_sb[:, :], e_bias_bc), (A_tile[:, :], A_log_r),
                     (conv_w[:, :], conv_w_r), (cbias[:, :], c_bias)):
        E.op(E.sync, "dma_start", out=dst, in_=src, inc=E.s_x, inck=16)
    for i, src in enumerate((b_dt_r, D_par_r, conv_b_r, wr_b_r, wi_b_r,
                             log_a_r, norm_w_r)):
        E.op(E.sync, "dma_start", out=bview[:, i, :], in_=src,
             inc=E.s_x, inck=16)
    E.op(E.gps, "dma_start",
         out=gw_sb.rearrange("p (kc e) -> p kc e", kc=DC),
         in_=gate_wT.rearrange("(kc p) e -> p kc e", p=128), inc=E.s_w, inck=16)
    E.op(E.gps, "dma_start",
         out=wB_sb.rearrange("p (kc n) -> p kc n", kc=DC),
         in_=w_BT.rearrange("(kc p) n -> p kc n", p=128), inc=E.s_w, inck=16)
    E.op(E.gps, "dma_start",
         out=wC_sb.rearrange("p (kc n) -> p kc n", kc=DC),
         in_=w_CT.rearrange("(kc p) n -> p kc n", p=128), inc=E.s_w, inck=16)

    E.wait(E.sca, E.s_x, E.s_x.n)
    E.op(E.sca, "activation", A_tile[:, :], A_tile[:, :], ACT.Tanh, scale=0.1)
    E.drain(E.sca)
    E.op(E.sca, "activation", A_tile[:, :], A_tile[:, :], ACT.Exp, scale=10.0)
    E.drain(E.sca)
    la = bview[:, 5, :]
    t1i = smalls[:, 0:DC]
    t2i = smalls[:, DC:2 * DC]
    E.op(E.sca, "activation", t1i, la, ACT.Abs)
    E.drain(E.sca)
    E.op(E.sca, "activation", t1i, t1i, ACT.Exp, scale=-1.0)
    E.drain(E.sca)
    E.op(E.sca, "activation", t1i, t1i, ACT.Ln, bias=1.0)
    E.drain(E.sca)
    E.op(E.sca, "activation", t2i, la, ACT.Relu, scale=-1.0, inc=E.s_s)
    E.wait(E.vec, E.s_s, E.s_s.n)
    E.op(E.vec, "tensor_add", t1i, t1i, t2i)
    E.drain(E.vec)
    E.op(E.vec, "tensor_single_scalar", la, t1i, -8.0, ALU.mult, inc=E.s_v)
    E.barrier()

    def emit_ln(tc0, tcn, out_f32, out_bf, sq_bf, toff=0, obn=None,
                f32view=None, pre_wait_t=None, xs=None, wait_x=None):
        TT = BL * tcn
        assert TT <= 512
        obv = out_bf.rearrange("p (dc b t) -> p dc b t", dc=DC, b=BL,
                               t=obn if False else None) if False else             out_bf.rearrange("p (dc b t) -> p dc b t", dc=DC, b=BL)[
                :, :, :, toff:toff + tcn]
        sqv = sq_bf.rearrange("p (dc b t) -> p dc b t", dc=DC, b=BL)[
            :, :, :, 0:tcn]

        if pre_wait_t is not None:
            E.wait(E.sca, E.s_t, pre_wait_t)
        if wait_x is not None:
            E.wait(E.sca, E.s_x, wait_x)
            E.wait(E.vec, E.s_x, wait_x)
        E.op(E.sca, "activation", sqv, xs, ACT.Square, inc=E.s_s)
        m_sq = E.s_s.n
        E.op(E.vec, "tensor_copy", obv, xs, inc=E.s_v)
        m_xc = E.s_v.n

        E.wait(E.ten, E.s_s, m_sq)
        E.wait(E.ten, E.s_v, m_xc)
        last = None
        for q, src in ((0, obv), (1, sqv)):
            for b in range(BL):
                dst = pB[0:1, q * 512 + b * tcn: q * 512 + (b + 1) * tcn]
                for kc in range(DC):
                    last = E.op(E.ten, "matmul", dst, onescol[:, :],
                                src[:, kc, b, :], start=(kc == 0),
                                stop=(kc == DC - 1),
                                inc=(E.s_t if (q, b, kc) == (1, BL - 1, DC - 1)
                                     else None))
        m_st = E.s_t.n
        E.last_ln_mst = m_st

        E.wait(E.vec, E.s_t, m_st)
        m = smalls[0:1, 0:TT]
        v = smalls[0:1, 512:512 + TT]
        w1 = smalls[0:1, 1024:1024 + TT]
        w2 = smalls[0:1, 1536:1536 + TT]
        mm_ = [pB[0:1, 0 * 512 + b * tcn:0 * 512 + (b + 1) * tcn] for b in range(BL)]
        vv_ = [pB[0:1, 1 * 512 + b * tcn:1 * 512 + (b + 1) * tcn] for b in range(BL)]
        E.op(E.vec, "tensor_single_scalar", m,
             pB[0:1, 0:TT] if tcn * BL == TT else None, 1.0 / D, ALU.mult)
        E.op(E.vec, "tensor_single_scalar", v, pB[0:1, 512:512 + TT],
             1.0 / D, ALU.mult)
        E.drain(E.vec)
        E.op(E.vec, "tensor_mul", w1, m, m)
        E.drain(E.vec)
        E.op(E.vec, "tensor_sub", v, v, w1)
        E.drain(E.vec)
        E.op(E.vec, "tensor_scalar_add", v, v, 1e-5)
        E.drain(E.vec)
        emit_rsqrt(E, E.vec, v, v, w1, w2)
        mh, ml = hlrows[0:1, 0:TT], hlrows[0:1, 512:512 + TT]
        ih, il = hlrows[0:1, 1024:1024 + TT], hlrows[0:1, 1536:1536 + TT]
        E.op(E.vec, "tensor_copy", mh, m)
        E.op(E.vec, "tensor_copy", ih, v)
        E.drain(E.vec)
        E.op(E.vec, "tensor_sub", w1, m, mh)
        E.op(E.vec, "tensor_sub", w2, v, ih)
        E.drain(E.vec)
        E.op(E.vec, "tensor_copy", ml, w1)
        E.op(E.vec, "tensor_copy", il, w2, inc=E.s_v)
        m_rows = E.s_v.n

        E.wait(E.ten, E.s_v, m_rows)
        E.op(E.ten, "matmul", pC[:, 0:TT], ones1[:, :], mh, start=True, stop=False)
        E.op(E.ten, "matmul", pC[:, 0:TT], ones1[:, :], ml, start=False, stop=True)
        E.op(E.ten, "matmul", pD[:, 0:TT], ones1[:, :], ih, start=True, stop=False)
        E.op(E.ten, "matmul", pD[:, 0:TT], ones1[:, :], il, start=False, stop=True,
             inc=E.s_t)
        m_bc = E.s_t.n

        E.wait(E.vec, E.s_t, m_bc)
        mbc = pC[:, 0:TT].rearrange("p (b t) -> p b t", b=BL)
        ibc = pD[:, 0:TT].rearrange("p (b t) -> p b t", b=BL)
        if f32view is not None:
            ov = f32view
        else:
            ov = out_f32.rearrange("p (dc b t) -> p dc b t", dc=DC, b=BL)[
                :, :, :, toff:toff + tcn]
        E.op(E.vec, "tensor_sub", ov, xs,
             mbc[:, None, :, :].broadcast_to([128, DC, BL, tcn]))
        E.drain(E.vec)
        E.op(E.vec, "tensor_mul", ov, ov,
             ibc[:, None, :, :].broadcast_to([128, DC, BL, tcn]))
        E.drain(E.vec)
        E.op(E.vec, "tensor_copy", obv, ov, inc=E.s_v)
        return E.s_v.n

    if "aa" in phases:
        A_TCmax = 121
        ph = ExitStack()

        def psb(name, shape, dt=F32):
            return ph.enter_context(nc.sbuf_tensor(name, list(shape), dt))

        W = BL * A_TCmax
        wdt_sb = psb("wdt_sb", [128, DC * D], BF16)
        ln1 = psb("aa_ln1", [128, DC * W])
        ln1b = psb("aa_ln1b", [128, DC * W], BF16)
        sqb = psb("aa_sqb", [128, DC * W], BF16)
        dt_b = psb("aa_dt", [128, DC * W])
        wdtx = psb("aa_wdtx", [128, DC * W])
        ybuf = psb("aa_y", [128, DC * W])
        ubuf0 = psb("aa_u0", [128, 256 * 10])
        ubuf = [ubuf0, ubuf0]
        dabuf = [psb("aa_da0", [128, 256 * 11]), psb("aa_da1", [128, 256 * 11])]
        btbuf = psb("aa_bt", [128, 256 * 11])
        Hbuf = [psb("aa_H0", [128, 256 * 11]), psb("aa_H1", [128, 256 * 11])]
        ytmp = [psb("aa_yt0", [128, 256 * 10]), psb("aa_yt1", [128, 256 * 10])]
        sp1 = ytmp[0]
        Bp = psb("aa_Bp", [NST, W])
        Cp = psb("aa_Cp", [NST, W])
        Bph = psb("aa_Bph", [NST, 2 * W], BF16)
        Cph = psb("aa_Cph", [NST, 2 * W], BF16)
        nst_t = psb("aa_nst", [NST, W])
        xbuf_a = psb("aa_x", [128, DC * W])
        A_exp = psb("aa_Aexp", [128, DC * BL * NST])
        bc_sb = [psb("aa_bcsb0", [128, 2048]), psb("aa_bcsb1", [128, 2048])]
        scale_sb = [psb("aa_scale0", [128, 8]), psb("aa_scale1", [128, 8])]

        E.op(E.gps, "dma_start",
             out=wdt_sb.rearrange("p (kc m) -> p kc m", kc=DC),
             in_=w_dtT.rearrange("(kc p) m -> p kc m", p=128),
             inc=E.s_w, inck=16)
        m_wdt = E.s_w.n

        if dbg_outs:
            for buf in (ln1, dt_b, ybuf, sp1, wdtx):
                E.op(E.vec, "memset", buf[:, :], 0.0)
            E.drain(E.vec)

        aev = A_exp.rearrange("p (dc b n) -> p dc b n", dc=DC, b=BL)
        atv = A_tile.rearrange("p (dc n) -> p dc n", dc=DC)
        E.op(E.vec, "tensor_copy", aev,
             atv[:, :, None, :].broadcast_to([128, DC, BL, NST]), inc=E.s_v)
        E.drain(E.vec)
        m_aexp = E.s_v.n

        cur_S_par = [None, None]

        def set_S(S, par):
            if cur_S_par[par] != S:
                cur_S_par[par] = S
                E.op(E.vec, "memset",
                     dabuf[par][:, 0:256 * S].rearrange(
                         "p (r c) -> p r c", c=S)[:, :, 0:1], 0.0, inc=E.s_v)
                E.drain(E.vec)
                setS_marks[par] = E.s_v.n

        sp_marks_prev = None
        first_sub_global = True
        sub_idx = [0]
        zda_v = [0, 0]
        zda_g = [0, 0]
        setS_marks = [0, 0]
        z_marks = [0, 0]
        tanh_marks = [0, 0]
        da_marks = [0, 0]
        yred_v = [0, 0]
        m_tts_v = 0
        m_tts_g = 0
        m_scale_v = 0
        m_lnexp_prev = 0
        m_vmm_prev = 0
        grp_last_ymul = {}
        prev_par = 0
        gidx = 0
        m_grp_evac_g = 0
        DCH = DC // 2

        src_aa, dst_aa = phase_io("aa")
        m_xst = 0
        chunks = attn_a_chunks(Ll)
        for ci, (tc0, tcn) in enumerate(chunks):
            TT = BL * tcn
            xbv = xbuf_a.rearrange("p (dc b t) -> p dc b t", dc=DC, b=BL)
            if m_xst:
                E.wait(E.sync, E.s_x, m_xst)
            for b in range(BL):
                E.op(E.sync, "dma_start", out=xbv[:, :, b, 0:tcn],
                     in_=src_aa[:, :, b, tc0:tc0 + tcn], inc=E.s_x, inck=16)
            m_xld = E.s_x.n
            m_ln = emit_ln(tc0, tcn, ln1, ln1b, sqb,
                           xs=xbv[:, :, :, 0:tcn], wait_x=m_xld)

            lbv = ln1b.rearrange("p (dc b t) -> p dc b t", dc=DC, b=BL)
            lnv = ln1.rearrange("p (dc b t) -> p dc b t", dc=DC, b=BL)
            dtv = dt_b.rearrange("p (dc b t) -> p dc b t", dc=DC, b=BL)
            wdv = wdt_sb.rearrange("p (kc m) -> p kc m", kc=DC)
            wxv = wdtx.rearrange("p (dc b t) -> p dc b t", dc=DC, b=BL)
            yv = ybuf.rearrange("p (dc b t) -> p dc b t", dc=DC, b=BL)

            E.wait(E.ten, E.s_v, m_ln)
            if ci == 0:
                E.wait(E.ten, E.s_w, m_wdt)
            else:
                E.wait(E.sca, E.s_g, E.s_g.n)
            dt_mm = []
            sp_marks = []
            for oc in range(DC):
                dst = pA[:, (oc % 2) * 512:(oc % 2) * 512 + TT]
                if oc >= 2:
                    E.wait(E.ten, E.s_s, sp_marks[oc - 2])
                elif ci > 0:
                    E.wait(E.ten, E.s_s, sp_marks_prev[oc + DC - 2])
                dst3 = dst.rearrange("p (b t) -> p b t", b=BL)
                for kc in range(DC):
                    E.op(E.ten, "matmul", dst3,
                         wdv[:, kc, oc * 128:(oc + 1) * 128],
                         lbv[:, kc, :, 0:tcn],
                         start=(kc == 0), stop=(kc == DC - 1),
                         inc=(E.s_t if kc == DC - 1 else None))
                dt_mm.append(E.s_t.n)
                E.wait(E.sca, E.s_t, dt_mm[oc])
                src = dst3
                a1 = sp1.rearrange("p (dc b t) -> p dc b t", dc=DC, b=BL)[
                    :, oc, :, 0:tcn]
                a2 = dtv[:, oc, :, 0:tcn]
                E.op(E.sca, "activation", a1, src, ACT.Abs,
                     bias=bview[:, 0, oc:oc + 1])
                E.drain(E.sca)
                E.op(E.sca, "activation", a1, a1, ACT.Exp, scale=-1.0)
                E.drain(E.sca)
                E.op(E.sca, "activation", a1, a1, ACT.Ln, bias=1.0)
                E.drain(E.sca)
                E.op(E.sca, "activation", a2, src, ACT.Relu,
                     bias=bview[:, 0, oc:oc + 1], inc=E.s_s)
                sp_marks.append(E.s_s.n)

            wbv = wB_sb.rearrange("p (kc n) -> p kc n", kc=DC)
            wcv = wC_sb.rearrange("p (kc n) -> p kc n", kc=DC)
            for dstoff, wsrc in ((0, wbv), (512, wcv)):
                dst3 = pB[0:NST, dstoff:dstoff + TT].rearrange(
                    "p (b t) -> p b t", b=BL)
                for kc in range(DC):
                    E.op(E.ten, "matmul", dst3,
                         wsrc[:, kc, :],
                         lbv[:, kc, :, 0:tcn],
                         start=(kc == 0), stop=(kc == DC - 1),
                         inc=(E.s_t if (dstoff == 512 and kc == DC - 1) else None))
            m_bpc = E.s_t.n

            sp_marks_prev = sp_marks

            for oc in range(DC):
                E.wait(E.vec, E.s_s, sp_marks[oc])
                a1 = sp1.rearrange("p (dc b t) -> p dc b t", dc=DC, b=BL)[
                    :, oc, :, 0:tcn]
                a2 = dtv[:, oc, :, 0:tcn]
                E.op(E.vec, "tensor_add", a2, a2, a1,
                     inc=(E.s_v if oc == DC - 1 else None))
            E.drain(E.vec)
            m_spsum = E.s_v.n

            E.wait(E.sca, E.s_v, m_spsum)
            for oc in range(DC):
                sl = dtv[:, oc, :, 0:tcn]
                E.op(E.sca, "activation", sl, sl, ACT.Tanh,
                     scale=ATANH_SC, bias=cbias[:, 0:1],
                     inc=(E.s_s if oc == DC - 1 else None))
            m_dtt = E.s_s.n

            E.wait(E.vec, E.s_s, m_dtt)
            if ci > 0:
                E.wait(E.vec, E.s_g, E.s_g.n)
            sl = dtv[:, :, :, 0:tcn]
            E.op(E.vec, "tensor_scalar", out=sl, in0=sl,
                 scalar1=SP_R / 0.5, scalar2=SP_C, op0=ALU.mult, op1=ALU.add)
            E.drain(E.vec)
            E.op(E.vec, "tensor_mul", wxv[:, :, :, 0:tcn], sl,
                 lnv[:, :, :, 0:tcn], inc=E.s_v)
            E.drain(E.vec)
            m_wdtx = E.s_v.n

            E.wait(E.vec, E.s_t, m_bpc)
            E.op(E.vec, "tensor_copy", Bp[:, 0:TT], pB[0:NST, 0:TT])
            E.op(E.vec, "tensor_copy", Cp[:, 0:TT], pB[0:NST, 512:512 + TT])
            E.drain(E.vec)
            for src, dsthl in ((Bp, Bph), (Cp, Cph)):
                E.op(E.vec, "tensor_copy", dsthl[:, 0:TT], src[:, 0:TT])
                E.drain(E.vec)
                E.op(E.vec, "tensor_sub", nst_t[:, 0:TT], src[:, 0:TT],
                     dsthl[:, 0:TT])
                E.drain(E.vec)
                E.op(E.vec, "tensor_copy", dsthl[:, TT:2 * TT], nst_t[:, 0:TT])
                E.drain(E.vec)
            E.op(E.vec, "memset", smalls[0:1, 2047:2048], 0.0, inc=E.s_v)
            m_bpchl = E.s_v.n

            subs = subchunks_of(tc0, tcn)
            groups = []
            for i in range(0, len(subs), 3):
                gs = subs[i:i + 3]
                groups.append((gs[0][0], gs[-1][0] + gs[-1][1] - gs[0][0], gs))
            flat = []
            for (g0, gtn, gsubs) in groups:
                for (st0, sln) in gsubs:
                    flat.append((st0, sln, g0, gtn, gsubs[0][0]))

            Aslm = A_exp[:, :].rearrange("p (a n) -> p a n", n=NST)

            def emit_z(st0_, sln_, par_):
                S_ = sln_ + 1
                dslm = dtv[:, :, :, st0_ - tc0:st0_ - tc0 + sln_].rearrange(
                    "p dc b t -> p (dc b) t")
                z4 = dabuf[par_][:, 0:256 * S_].rearrange(
                    "p (a n t) -> p a n t", a=16, n=NST)[:, :, :, 1:1 + sln_]
                w = max(zda_v[par_], m_wdtx, m_aexp)
                E.wait(E.gps, E.s_v, w)
                E.op(E.gps, "tensor_mul", z4,
                     dslm[:, :, None, :].broadcast_to([128, 16, NST, sln_]),
                     Aslm[:, :, :, None].broadcast_to([128, 16, NST, sln_]),
                     inc=E.s_g)
                z_marks[par_] = E.s_g.n

            def emit_tanh_exp(st0_, sln_, par_):
                S_ = sln_ + 1
                set_S(S_, par_)
                dar = dabuf[par_][:, 0:256 * S_].rearrange(
                    "p (r t) -> p r t", t=S_)
                E.wait(E.sca, E.s_g, z_marks[par_])
                E.op(E.sca, "activation",
                     ubuf[par_][:, 0:256 * sln_].rearrange(
                         "p (r t) -> p r t", t=sln_),
                     dar[:, :, 1:1 + sln_], ACT.Tanh, scale=1.0 / 16.0,
                     inc=E.s_s)
                tanh_marks[par_] = E.s_s.n
                E.drain(E.sca)
                w = max(zda_v[par_], setS_marks[par_])
                if w:
                    E.wait(E.sca, E.s_v, w)
                E.op(E.sca, "activation", dar[:, :, 1:1 + sln_],
                     ubuf[par_][:, 0:256 * sln_].rearrange(
                         "p (r t) -> p r t", t=sln_),
                     ACT.Exp, scale=-16.0, inc=E.s_s)
                da_marks[par_] = E.s_s.n

            def emit_ymul(p):
                E.wait(E.gps, E.s_v, max(p["m_tts_v"], yred_v[p["par"]]))
                y4 = ytmp[p["par"]][:, 0:256 * p["sln"]].rearrange(
                    "p (dc bn t) -> p dc bn t", dc=DC, bn=32)
                H4 = Hbuf[p["par"]][:, 0:256 * (p["sln"] + 1)].rearrange(
                    "p (dc bn t) -> p dc bn t", dc=DC, bn=32)
                E.op(E.gps, "tensor_mul", y4, H4[:, :, :, 1:1 + p["sln"]],
                     p["bcC"][:, None, :, p["lo"]:p["lo"] + p["sln"]
                              ].broadcast_to([128, DC, 32, p["sln"]]))
                y4n = ytmp[p["par"]][:, 0:256 * p["sln"]].rearrange(
                    "p (a n t) -> p a n t", a=16, n=NST)
                E.op(E.gps, "tensor_add", y4n[:, :, 0:8, :],
                     y4n[:, :, 0:8, :], y4n[:, :, 8:16, :], inc=E.s_g)
                grp_last_ymul[p["gidx"]] = E.s_g.n
                p["m_ymul"] = E.s_g.n

            def emit_yred(p):
                E.wait(E.vec, E.s_g, p["m_ymul"])
                yin = ytmp[p["par"]][:, 0:256 * p["sln"]].rearrange(
                    "p (a n t) -> p a t n", a=16, n=NST)[:, :, :, 0:8]
                yout = yv[:, :, :,
                          p["st0"] - tc0:p["st0"] - tc0 + p["sln"]].rearrange(
                    "p dc b t -> p (dc b) t")
                E.op(E.vec, "tensor_reduce", yout, yin, AX.X, ALU.add,
                     inc=E.s_v)
                yred_v[p["par"]] = E.s_v.n
                E.drain(E.vec)
                if p["flag"]:
                    ycol = yv[:, :, :, p["st0"] - tc0 + p["sln"] - 1]
                    E.op(E.vec, "tensor_mul", ycol, ycol,
                         scale_sb[p["par"]][:, None, 0:BL].broadcast_to(
                             [128, DC, BL]), inc=E.s_v)
                    E.drain(E.vec)

            st0_0, sln_0 = flat[0][0], flat[0][1]
            par0 = sub_idx[0] % 2
            emit_z(st0_0, sln_0, par0)
            emit_tanh_exp(st0_0, sln_0, par0)

            pend = None
            bcBv = bcCv = None
            for j, (st0, sln, g0, gtn, gfirst) in enumerate(flat):
                S = sln + 1
                lo = st0 - g0
                flag = (st0 + sln - 1) % 10 == 0
                par = sub_idx[0] % 2

                if st0 == gfirst:
                    gp = gidx % 2
                    assert BL * gtn <= 64
                    E.wait(E.ten, E.s_v, m_bpchl)
                    if m_grp_evac_g:
                        E.wait(E.ten, E.s_s, m_grp_evac_g)
                    for srchl, dstp in ((Bph, pC), (Cph, pD)):
                        hlv = srchl[:, 0:2 * TT].rearrange(
                            "q (h b t) -> q h b t", h=2, b=BL)
                        for n in range(NST):
                            dst = dstp[:, n * 64:n * 64 + BL * gtn].rearrange(
                                "p (b t) -> p b t", b=BL)
                            E.op(E.ten, "matmul", dst,
                                 sel16[:, n * 128:(n + 1) * 128],
                                 hlv[:, 0, :, g0 - tc0:g0 - tc0 + gtn],
                                 start=True, stop=False)
                            E.op(E.ten, "matmul", dst,
                                 sel16[:, n * 128:(n + 1) * 128],
                                 hlv[:, 1, :, g0 - tc0:g0 - tc0 + gtn],
                                 start=False, stop=True,
                                 inc=(E.s_t if (srchl is Cph and n == NST - 1)
                                      else None))
                    m_grp_fill = E.s_t.n
                    E.wait(E.sca, E.s_t, m_grp_fill)
                    if gidx - 2 in grp_last_ymul:
                        E.wait(E.sca, E.s_g, grp_last_ymul[gidx - 2])
                    for qi, srcp in ((0, pC), (1, pD)):
                        inap = srcp[:, 0:NST * 64].rearrange(
                            "p (n t) -> p n t", n=NST)[
                            :, :, 0:BL * gtn].rearrange(
                            "p n (b t) -> p b n t", b=BL)
                        outap = bc_sb[gp][:, qi * 1024:(qi + 1) * 1024
                                          ].rearrange(
                            "p (b n t) -> p b n t", b=BL, n=NST)[:, :, :, 0:gtn]
                        E.op(E.sca, "copy", outap, inap,
                             inc=(E.s_s if qi == 1 else None))
                    m_grp_evac_g = E.s_s.n
                    bcBv = bc_sb[gp][:, 0:1024].rearrange(
                        "p (b n t) -> p b n t", b=BL, n=NST)
                    bcCv = bc_sb[gp][:, 1024:2048].rearrange(
                        "p (bn t) -> p bn t", t=32)
                    cur_gidx = gidx
                    gidx += 1

                bt5 = btbuf[:, 0:256 * S].rearrange(
                    "p (dc b n t) -> p dc b n t", dc=DC, b=BL, n=NST)
                E.wait(E.gps, E.s_v, max(m_tts_v, m_wdtx))
                E.wait(E.gps, E.s_s, m_grp_evac_g)
                for b in range(BL):
                    E.op(E.gps, "tensor_mul", bt5[:, :, b, :, 1:1 + sln],
                         bcBv[:, b, :, lo:lo + sln][:, None, :, :
                              ].broadcast_to([128, DC, NST, sln]),
                         wxv[:, :, b, st0 - tc0:st0 - tc0 + sln][
                             :, :, None, :].broadcast_to(
                             [128, DC, NST, sln]),
                         inc=(E.s_g if b == BL - 1 else None))
                m_btmul_g = E.s_g.n

                if j + 1 < len(flat):
                    nst0, nsln = flat[j + 1][0], flat[j + 1][1]
                    npar = (sub_idx[0] + 1) % 2
                    emit_z(nst0, nsln, npar)
                    emit_tanh_exp(nst0, nsln, npar)

                btc0 = bt5[:, :, :, :, 0]
                if first_sub_global:
                    E.op(E.vec, "memset", btc0, 0.0)
                else:
                    pvr = Hbuf[prev_par][:, 0:256 * prev_S].rearrange(
                        "p (dc b n t) -> p dc b n t",
                        dc=DC, b=BL, n=NST)[:, :, :, :, prev_S - 1]
                    if prev_flag:
                        E.op(E.vec, "tensor_mul", btc0, pvr,
                             scale_sb[prev_par][:, None, 0:BL, None
                                  ].broadcast_to([128, DC, BL, NST]))
                    else:
                        E.op(E.vec, "tensor_copy", btc0, pvr)
                E.drain(E.vec)
                E.wait(E.vec, E.s_s, da_marks[par])
                E.wait(E.vec, E.s_g, m_btmul_g)
                E.op(E.vec, "tensor_tensor_scan",
                     Hbuf[par][:, 0:256 * S], dabuf[par][:, 0:256 * S],
                     btbuf[:, 0:256 * S], 0.0, ALU.mult, ALU.add,
                     inc=E.s_v)
                m_tts_v = E.s_v.n
                zda_v[par] = m_tts_v
                E.drain(E.vec)

                if flag:
                    Hc = Hbuf[par][:, 0:256 * S].rearrange(
                        "p (dc b n t) -> p dc b n t",
                        dc=DC, b=BL, n=NST)[:, :, :, :, S - 1]
                    outc = smalls[:, 2176:2176 + 256].rearrange(
                        "p (b dc n) -> p b dc n", b=BL, dc=DC)
                    if m_vmm_prev:
                        E.wait(E.vec, E.s_t, m_vmm_prev)
                    E.op(E.vec, "tensor_mul", outc,
                         Hc.transpose([0, 2, 1, 3]),
                         Hc.transpose([0, 2, 1, 3]))
                    E.drain(E.vec)
                    comb = smalls[:, 2176:2176 + 256].rearrange(
                        "p (b rest) -> p b rest", b=BL)
                    red = smalls[:, 2432:2432 + BL]
                    E.op(E.vec, "tensor_reduce", red, comb, AX.X, ALU.add)
                    E.drain(E.vec)
                    redb_h = hlrows[:, 2048:2048 + BL]
                    redb_l = hlrows[:, 2056:2056 + BL]
                    g2 = smalls[:, 2464:2464 + BL]
                    E.op(E.vec, "tensor_copy", redb_h, red)
                    E.drain(E.vec)
                    E.op(E.vec, "tensor_sub", g2, red, redb_h)
                    E.drain(E.vec)
                    E.op(E.vec, "tensor_copy", redb_l, g2, inc=E.s_v)
                    m_hsq = E.s_v.n
                    E.drain(E.vec)
                    E.wait(E.ten, E.s_v, m_hsq)
                    if m_lnexp_prev:
                        E.wait(E.ten, E.s_s, m_lnexp_prev)
                    E.op(E.ten, "matmul", pB[:, 768:768 + BL],
                         ones128[:, :], redb_h, start=True, stop=False)
                    E.op(E.ten, "matmul", pB[:, 768:768 + BL],
                         ones128[:, :], redb_l, start=False, stop=True,
                         inc=E.s_t)
                    m_vmm = E.s_t.n
                    vln = smalls[:, 2448:2448 + BL]
                    vex = smalls[:, 2452:2452 + BL]
                    E.wait(E.sca, E.s_t, m_vmm)
                    E.op(E.sca, "activation", vln, pB[:, 768:768 + BL],
                         ACT.Ln)
                    E.drain(E.sca)
                    E.op(E.sca, "activation", vex, vln, ACT.Exp, scale=-0.5,
                         inc=E.s_s)
                    m_lnexp = E.s_s.n
                    E.wait(E.vec, E.s_s, m_lnexp)
                    E.op(E.vec, "tensor_scalar",
                         out=scale_sb[par][:, 0:BL], in0=vex,
                         scalar1=20.0, scalar2=1.0, op0=ALU.mult,
                         op1=ALU.min, inc=E.s_v)
                    m_scale_v = E.s_v.n
                    E.drain(E.vec)
                    m_lnexp_prev = m_lnexp
                    m_vmm_prev = m_vmm

                if pend is not None:
                    emit_ymul(pend)
                    emit_yred(pend)
                pend = {"par": par, "st0": st0, "sln": sln, "flag": flag,
                        "lo": lo, "bcC": bcCv, "gidx": cur_gidx,
                        "m_tts_v": m_tts_v, "m_scale_v": m_scale_v}

                prev_par = par
                prev_S = S
                prev_flag = flag
                first_sub_global = False
                sub_idx[0] += 1

            if pend is not None:
                emit_ymul(pend)
                emit_yred(pend)
                pend = None

            E.wait(E.vec, E.s_g, E.s_g.n)
            xs = xbv[:, :, :, 0:tcn]
            Dbc = bview[:, 1, :]
            tmp = sp1.rearrange("p (dc b t) -> p dc b t", dc=DC, b=BL)[
                :, :, :, 0:tcn]
            E.op(E.vec, "tensor_mul", tmp, lnv[:, :, :, 0:tcn],
                 Dbc[:, :, None, None].broadcast_to([128, DC, BL, tcn]))
            E.drain(E.vec)
            E.op(E.vec, "tensor_add", xs, xs, yv[:, :, :, 0:tcn])
            E.drain(E.vec)
            E.op(E.vec, "tensor_add", xs, xs, tmp, inc=E.s_v)
            E.drain(E.vec)
            m_xres = E.s_v.n
            E.wait(E.sync, E.s_v, m_xres)
            for b in range(BL):
                E.op(E.sync, "dma_start", out=dst_aa[:, :, b, tc0:tc0 + tcn],
                     in_=xbv[:, :, b, 0:tcn], inc=E.s_x, inck=16)
            m_xst = E.s_x.n

        E.wait(E.sync, E.s_v, E.s_v.n)
        E.wait(E.sync, E.s_s, E.s_s.n)
        if "aa_ln1" in dbg_outs:
            E.op(E.sync, "dma_start", out=dbg_outs["aa_ln1"], in_=ln1[:, :],
                 inc=E.s_x, inck=16)
        if "aa_dt" in dbg_outs:
            E.op(E.sync, "dma_start", out=dbg_outs["aa_dt"], in_=dt_b[:, :],
                 inc=E.s_x, inck=16)
        if "aa_y" in dbg_outs:
            E.op(E.sync, "dma_start", out=dbg_outs["aa_y"], in_=ybuf[:, :],
                 inc=E.s_x, inck=16)
        E.barrier()
        ph.close()

    def emit_ffn(tag):
        FT = 512
        TOK = BL * FT
        ph = ExitStack()

        def psb(name, shape, dt=F32):
            return ph.enter_context(nc.sbuf_tensor(tag + name, list(shape), dt))

        wslot = [psb(f"w{i}", [128, 4096], BF16) for i in range(4)]
        shg = psb("shg", [128, 4096], BF16)
        shu = psb("shu", [128, 4096], BF16)
        shd = psb("shd", [128, 4096], BF16)
        ln2b = psb("ln2b", [128, DC * TOK], BF16)
        sqb2 = psb("sqb", [128, DC * BL * 256], BF16)
        ybuf = psb("y", [128, DC * TOK])
        interb = psb("inter", [128, IC * FT], BF16)
        sgbuf = psb("sg", [128, 2 * FT])
        uwbuf = psb("uw", [128, 2 * FT])
        wbcs = psb("wbc", [128, 2 * TOK])
        gbuf = psb("g", [128, 1536])
        wfm = psb("wfm", [NST, TOK])
        xbuf_f = psb("x", [128, DC * TOK])
        wfmh = psb("wfmh", [NST, 2 * TOK], BF16)
        whl = psb("whl", [128, 16 * NST], BF16)

        E.op(E.gps, "dma_start",
             out=shg.rearrange("p (kc m) -> p kc m", kc=DC),
             in_=sgT.rearrange("(kc p) m -> p kc m", p=128), inc=E.s_wsh, inck=16)
        E.op(E.gps, "dma_start",
             out=shu.rearrange("p (kc m) -> p kc m", kc=DC),
             in_=suT.rearrange("(kc p) m -> p kc m", p=128), inc=E.s_wsh, inck=16)
        E.op(E.gps, "dma_start",
             out=shd.rearrange("p (kc m) -> p kc m", kc=IC),
             in_=sdT.rearrange("(kc p) m -> p kc m", p=128), inc=E.s_wsh, inck=16)
        m_shw = E.s_wsh.n

        n_chunks = (Ll + FT - 1) // FT
        slot_free = [0, 0, 0, 0]
        wm = {}
        sidx = [0]

        def stream_mat(ci, e, j):
            src = (gprojT, uprojT, dprojT)[j][e]
            sl = sidx[0] % 4
            sidx[0] += 1
            E.wait(E.gps, E.s_t, slot_free[sl])
            kcn = DC if j < 2 else IC
            E.op(E.gps, "dma_start",
                 out=wslot[sl].rearrange("p (kc m) -> p kc m", kc=kcn),
                 in_=src.rearrange("(kc p) m -> p kc m", p=128),
                 inc=E.s_wsl[sl], inck=16)
            wm[(ci, e, j)] = (sl, E.s_wsl[sl].n)

        if dbg_outs:
            E.op(E.vec, "memset", ln2b[:, :].bitcast(I32), 0)
            E.op(E.vec, "memset", ybuf[:, :], 0.0)
            E.op(E.vec, "memset", gbuf[:, :], 0.0)
            E.drain(E.vec)

        for j in range(3):
            stream_mat(0, 0, j)

        src_f, dst_f = phase_io(tag[:2])
        m_xst = 0
        for ci in range(n_chunks):
            tc0 = ci * FT
            tcn = min(FT, Ll - tc0)
            TTOK = BL * tcn

            xbv = xbuf_f.rearrange("p (dc b t) -> p dc b t", dc=DC, b=BL)
            if m_xst:
                E.wait(E.sync, E.s_x, m_xst)
            for b in range(BL):
                E.op(E.sync, "dma_start", out=xbv[:, :, b, 0:tcn],
                     in_=src_f[:, :, b, tc0:tc0 + tcn], inc=E.s_x, inck=16)
            m_xld = E.s_x.n
            for h0 in range(0, tcn, 256):
                hn = min(256, tcn - h0)
                m_ln = emit_ln(tc0 + h0, hn, ybuf, ln2b, sqb2, toff=h0,
                               xs=xbv[:, :, :, h0:h0 + hn], wait_x=m_xld,
                               pre_wait_t=(E.last_ln_mst if h0 > 0 else None))
            lbv = ln2b.rearrange("p (dc b t) -> p dc b t", dc=DC, b=BL)

            E.wait(E.vec, E.s_v, m_ln)
            E.op(E.vec, "memset", ybuf[:, :], 0.0, inc=E.s_v)
            E.drain(E.vec)
            m_y0 = E.s_v.n

            tpb = (tcn + 127) // 128
            ntile = BL * tpb
            E.op(E.vec, "memset", pB[:, 0:128], 0.0, inc=E.s_v)
            E.drain(E.vec)
            E.wait(E.ten, E.s_v, E.s_v.n)
            E.wait(E.ten, E.s_v, m_y0)
            for ti in range(ntile):
                b, tq = ti // tpb, ti % tpb
                tl = min(128, tcn - tq * 128)
                dst = pB[0:tl, ti * NST:(ti + 1) * NST]
                for kc in range(DC):
                    E.op(E.ten, "matmul", dst,
                         lbv[:, kc, b, tq * 128:tq * 128 + tl],
                         gw_sb.rearrange("p (kc e) -> p kc e", kc=DC)[:, kc, :],
                         start=(kc == 0), stop=(kc == DC - 1),
                         inc=(E.s_t if (ti == ntile - 1 and kc == DC - 1)
                              else None))
            m_glog = E.s_t.n

            gat = gbuf[:, 0:128]
            E.wait(E.sca, E.s_t, m_glog)
            E.op(E.sca, "activation", gat, pB[:, 0:128], ACT.Tanh, scale=0.5,
                 inc=E.s_s)
            m_gth = E.s_s.n
            E.wait(E.vec, E.s_s, m_gth)
            E.op(E.vec, "tensor_scalar", out=gat, in0=gat, scalar1=0.5,
                 scalar2=0.5, op0=ALU.mult, op1=ALU.add)
            E.drain(E.vec)
            sc = gbuf[:, 128:256]
            E.op(E.vec, "tensor_tensor",
                 sc.rearrange("p (ti e) -> p ti e", e=NST),
                 gat.rearrange("p (ti e) -> p ti e", e=NST),
                 ebias_sb[:, None, :].broadcast_to([128, 8, NST]), ALU.add)
            E.drain(E.vec)
            scg = sc.rearrange("p (ti g j) -> p ti g j", g=4, j=4)
            Mx = gbuf[:, 256:320].rearrange("p (ti g j) -> p ti g j", g=4, j=2)
            mn = gbuf[:, 320:384].rearrange("p (ti g j) -> p ti g j", g=4, j=2)
            E.op(E.vec, "tensor_max", Mx, scg[:, :, :, 0:2], scg[:, :, :, 2:4])
            E.op(E.vec, "tensor_tensor", mn, scg[:, :, :, 0:2],
                 scg[:, :, :, 2:4], ALU.min)
            E.drain(E.vec)
            gsv = gbuf[:, 384:416].rearrange("p (ti g) -> p ti g", g=4)
            t1g = gbuf[:, 416:448].rearrange("p (ti g) -> p ti g", g=4)
            t2g = gbuf[:, 448:480].rearrange("p (ti g) -> p ti g", g=4)
            E.op(E.vec, "tensor_max", gsv, Mx[:, :, :, 0], Mx[:, :, :, 1])
            E.op(E.vec, "tensor_tensor", t1g, Mx[:, :, :, 0], Mx[:, :, :, 1],
                 ALU.min)
            E.op(E.vec, "tensor_max", t2g, mn[:, :, :, 0], mn[:, :, :, 1])
            E.drain(E.vec)
            E.op(E.vec, "tensor_max", t1g, t1g, t2g)
            E.drain(E.vec)
            E.op(E.vec, "tensor_add", gsv, gsv, t1g)
            E.drain(E.vec)
            gm1 = gbuf[:, 480:488]
            hit = gbuf[:, 488:520].rearrange("p (ti g) -> p ti g", g=4)
            cums = gbuf[:, 520:552].rearrange("p (ti g) -> p ti g", g=4)
            c2 = gbuf[:, 552:584].rearrange("p (ti g) -> p ti g", g=4)
            gmask = t2g
            E.op(E.vec, "memset", gmask, 0.0)
            E.drain(E.vec)
            for _ in range(2):
                E.op(E.vec, "tensor_reduce", gm1, gsv, AX.X, ALU.max)
                E.drain(E.vec)
                E.op(E.vec, "tensor_tensor", hit, gsv,
                     gm1.rearrange("p ti -> p ti")[:, :, None].broadcast_to(
                         [128, 8, 4]), ALU.is_equal)
                E.drain(E.vec)
                E.op(E.vec, "tensor_copy", cums[:, :, 0:1], hit[:, :, 0:1])
                E.op(E.vec, "tensor_add", cums[:, :, 1:4], hit[:, :, 1:4],
                     hit[:, :, 0:3])
                E.drain(E.vec)
                E.op(E.vec, "tensor_copy", c2[:, :, 0:2], cums[:, :, 0:2])
                E.op(E.vec, "tensor_add", c2[:, :, 2:4], cums[:, :, 2:4],
                     cums[:, :, 0:2])
                E.drain(E.vec)
                E.op(E.vec, "tensor_single_scalar", c2, c2, 1.0, ALU.is_equal)
                E.drain(E.vec)
                E.op(E.vec, "tensor_mul", hit, hit, c2)
                E.drain(E.vec)
                E.op(E.vec, "tensor_add", gmask, gmask, hit)
                E.drain(E.vec)
                E.op(E.vec, "scalar_tensor_tensor", out=gsv, in0=hit,
                     scalar=-1e30, in1=gsv, op0=ALU.mult, op1=ALU.add)
                E.drain(E.vec)
            tmp = gbuf[:, 584:712].rearrange("p (ti g j) -> p ti g j", g=4, j=4)
            E.op(E.vec, "tensor_mul", tmp, scg,
                 gmask[:, :, :, None].broadcast_to([128, 8, 4, 4]))
            E.drain(E.vec)
            tmpv = gbuf[:, 584:712].rearrange("p (ti e) -> p ti e", e=NST)
            sel = gbuf[:, 712:840].rearrange("p (ti e) -> p ti e", e=NST)
            m16 = gbuf[:, 840:848]
            h16 = gbuf[:, 848:976].rearrange("p (ti e) -> p ti e", e=NST)
            c16a = gbuf[:, 976:1104].rearrange("p (ti e) -> p ti e", e=NST)
            c16b = gbuf[:, 1104:1232].rearrange("p (ti e) -> p ti e", e=NST)
            E.op(E.vec, "memset", sel, 0.0)
            E.drain(E.vec)
            for _ in range(4):
                E.op(E.vec, "tensor_reduce", m16, tmpv, AX.X, ALU.max)
                E.drain(E.vec)
                E.op(E.vec, "tensor_tensor", h16, tmpv,
                     m16[:, :, None].broadcast_to([128, 8, NST]), ALU.is_equal)
                E.drain(E.vec)
                srcc, dstc = h16, c16a
                for sh in (1, 2, 4, 8):
                    E.op(E.vec, "tensor_copy", dstc[:, :, 0:sh],
                         srcc[:, :, 0:sh])
                    E.op(E.vec, "tensor_add", dstc[:, :, sh:NST],
                         srcc[:, :, sh:NST], srcc[:, :, 0:NST - sh])
                    E.drain(E.vec)
                    srcc, dstc = dstc, (c16b if dstc is c16a else c16a)
                E.op(E.vec, "tensor_single_scalar", srcc, srcc, 1.0,
                     ALU.is_equal)
                E.drain(E.vec)
                E.op(E.vec, "tensor_mul", h16, h16, srcc)
                E.drain(E.vec)
                E.op(E.vec, "tensor_add", sel, sel, h16)
                E.drain(E.vec)
                E.op(E.vec, "scalar_tensor_tensor", out=tmpv, in0=h16,
                     scalar=-1e30, in1=tmpv, op0=ALU.mult, op1=ALU.add)
                E.drain(E.vec)
            wre = gbuf[:, 1232:1360].rearrange("p (ti e) -> p ti e", e=NST)
            rs = gbuf[:, 1360:1368]
            E.op(E.vec, "tensor_mul", wre, sel,
                 gat.rearrange("p (ti e) -> p ti e", e=NST))
            E.drain(E.vec)
            E.op(E.vec, "tensor_reduce", rs, wre, AX.X, ALU.add)
            E.drain(E.vec)
            E.op(E.vec, "tensor_scalar_add", rs, rs, 1e-20)
            E.drain(E.vec)
            E.op(E.vec, "reciprocal", rs, rs)
            E.drain(E.vec)
            E.op(E.vec, "scalar_tensor_tensor", out=wre, in0=wre, scalar=2.5,
                 in1=rs.rearrange("p ti -> p ti")[:, :, None].broadcast_to(
                     [128, 8, NST]), op0=ALU.mult, op1=ALU.mult)
            E.drain(E.vec)

            for ti in range(ntile):
                hi = whl[:, 2 * ti * NST:(2 * ti + 1) * NST]
                lo = whl[:, (2 * ti + 1) * NST:(2 * ti + 2) * NST]
                E.op(E.vec, "tensor_copy", hi, wre[:, ti, :])
                E.drain(E.vec)
                E.op(E.vec, "tensor_sub", gbuf[:, 1368:1384],
                     wre[:, ti, :], hi)
                E.drain(E.vec)
                E.op(E.vec, "tensor_copy", lo,
                     gbuf[:, 1368:1384], inc=E.s_v)
                m_whl = E.s_v.n
                E.wait(E.ten, E.s_v, m_whl)
                E.wait(E.ten, E.s_s, m_gth)
                dst = pB[0:NST, ti * 128:(ti + 1) * 128]
                E.op(E.ten, "matmul", dst, hi, ident[:, :],
                     start=True, stop=False)
                E.op(E.ten, "matmul", dst, lo, ident[:, :],
                     start=False, stop=True,
                     inc=(E.s_t if ti == ntile - 1 else None))
            m_wtr = E.s_t.n
            E.wait(E.vec, E.s_t, m_wtr)
            for ti in range(ntile):
                b, tq = ti // tpb, ti % tpb
                tl = min(128, tcn - tq * 128)
                E.op(E.vec, "tensor_copy",
                     wfm[:, b * tcn + tq * 128:b * tcn + tq * 128 + tl],
                     pB[0:NST, ti * 128:ti * 128 + tl])
            E.drain(E.vec)
            E.op(E.vec, "tensor_copy", wfmh[:, 0:TTOK], wfm[:, 0:TTOK])
            E.drain(E.vec)
            E.op(E.vec, "tensor_sub", wfm[:, 0:TTOK], wfm[:, 0:TTOK],
                 wfmh[:, 0:TTOK])
            E.drain(E.vec)
            E.op(E.vec, "tensor_copy", wfmh[:, TOK:TOK + TTOK], wfm[:, 0:TTOK],
                 inc=E.s_v)
            E.drain(E.vec)
            m_wfm = E.s_v.n

            wbc_used = [0, 0]
            m_int_done = 0
            sg_marks = [0, 0]
            uw_marks = [m_wfm, m_wfm]
            int_marks = [0, 0]
            m_yadd = m_y0
            iv = interb.rearrange("p (ic t) -> p ic t", ic=IC)
            for e in range(NE + 1):
                shared = (e == NE)
                if not shared:
                    epar = e % 2
                    E.wait(E.ten, E.s_v, m_wfm)
                    if wbc_used[epar]:
                        E.wait(E.ten, E.s_v, wbc_used[epar])
                    nq = (TTOK + 511) // 512
                    for q in range(nq):
                        qn = min(512, TTOK - q * 512)
                        dst = pD[:, q * 512:q * 512 + qn]
                        E.op(E.ten, "matmul", dst,
                             sel16[:, e * 128:(e + 1) * 128],
                             wfmh[:, q * 512:q * 512 + qn],
                             start=True, stop=False)
                        E.op(E.ten, "matmul", dst,
                             sel16[:, e * 128:(e + 1) * 128],
                             wfmh[:, TOK + q * 512:TOK + q * 512 + qn],
                             start=False, stop=True,
                             inc=(E.s_t if q == nq - 1 else None))
                    m_wbc = E.s_t.n
                    E.wait(E.vec, E.s_t, m_wbc)
                    E.op(E.vec, "tensor_copy", wbcs[:, epar * TOK:epar * TOK + TTOK],
                         pD[:, 0:TTOK], inc=E.s_v)
                    E.drain(E.vec)
                    m_wbcs_evac = E.s_v.n
                    slg, mg = wm[(ci, e, 0)]
                    slu, mu = wm[(ci, e, 1)]
                    sld, md = wm[(ci, e, 2)]
                    gv = wslot[slg].rearrange("p (kc m) -> p kc m", kc=DC)
                    uv = wslot[slu].rearrange("p (kc m) -> p kc m", kc=DC)
                    dv = wslot[sld].rearrange("p (kc m) -> p kc m", kc=IC)
                else:
                    E.wait(E.ten, E.s_wsh, m_shw)
                    gv = shg.rearrange("p (kc m) -> p kc m", kc=DC)
                    uv = shu.rearrange("p (kc m) -> p kc m", kc=DC)
                    dv = shd.rearrange("p (kc m) -> p kc m", kc=IC)

                for b in range(BL):
                    for icq in range(IC):
                        gp = pA[:, (icq % 2) * 512:(icq % 2) * 512 + tcn]
                        up = pB[:, (icq % 2) * 512:(icq % 2) * 512 + tcn]
                        if not shared:
                            E.wait(E.ten, E.s_wsl[slg], mg)
                            E.wait(E.ten, E.s_wsl[slu], mu)
                        if sg_marks[icq % 2]:
                            E.wait(E.ten, E.s_s, sg_marks[icq % 2])
                        for kc in range(DC):
                            E.op(E.ten, "matmul", gp,
                                 gv[:, kc, icq * 128:(icq + 1) * 128],
                                 lbv[:, kc, b, 0:tcn],
                                 start=(kc == 0), stop=(kc == DC - 1),
                                 inc=(E.s_t if kc == DC - 1 else None))
                        m_gmm = E.s_t.n
                        if uw_marks[icq % 2]:
                            E.wait(E.ten, E.s_v, uw_marks[icq % 2])
                        for kc in range(DC):
                            E.op(E.ten, "matmul", up,
                                 uv[:, kc, icq * 128:(icq + 1) * 128],
                                 lbv[:, kc, b, 0:tcn],
                                 start=(kc == 0), stop=(kc == DC - 1),
                                 inc=(E.s_t if kc == DC - 1 else None))
                        m_umm = E.s_t.n
                        sgsl = sgbuf[:, (icq % 2) * FT:(icq % 2) * FT + tcn]
                        uwsl = uwbuf[:, (icq % 2) * FT:(icq % 2) * FT + tcn]
                        E.wait(E.sca, E.s_t, m_gmm)
                        if int_marks[icq % 2]:
                            E.wait(E.sca, E.s_v, int_marks[icq % 2])
                        E.op(E.sca, "activation", sgsl, gp, ACT.Silu,
                             inc=E.s_s)
                        sg_marks[icq % 2] = E.s_s.n
                        E.wait(E.vec, E.s_t, m_umm)
                        if not shared:
                            E.op(E.vec, "tensor_mul", uwsl, up,
                                 wbcs[:, epar * TOK + b * tcn:
                                      epar * TOK + b * tcn + tcn], inc=E.s_v)
                        else:
                            E.op(E.vec, "tensor_copy", uwsl, up, inc=E.s_v)
                        uw_marks[icq % 2] = E.s_v.n
                        E.drain(E.vec)
                        E.wait(E.vec, E.s_s, sg_marks[icq % 2])
                        E.op(E.vec, "tensor_mul", iv[:, icq, 0:tcn],
                             sgsl, uwsl, inc=E.s_v)
                        E.drain(E.vec)
                        int_marks[icq % 2] = E.s_v.n
                    m_int_done = E.s_v.n

                    if not shared:
                        E.wait(E.ten, E.s_wsl[sld], md)
                    E.wait(E.ten, E.s_v, m_int_done)
                    yv = ybuf.rearrange("p (hc b t) -> p hc b t", hc=HC, b=BL)
                    for hc in range(HC):
                        off = (hc % 2) * 512
                        if hc >= 2:
                            E.wait(E.ten, E.s_v, yadd_marks[hc - 2])
                        elif m_yadd:
                            E.wait(E.ten, E.s_v, m_yadd)
                        if hc == 0:
                            yadd_marks = [0] * HC
                        for kci in range(IC):
                            E.op(E.ten, "matmul", pC[:, off:off + tcn],
                                 dv[:, kci, hc * 128:(hc + 1) * 128],
                                 iv[:, kci, 0:tcn],
                                 start=(kci == 0), stop=(kci == IC - 1),
                                 inc=(E.s_t if kci == IC - 1 else None))
                        m_dmm = E.s_t.n
                        E.wait(E.vec, E.s_t, m_dmm)
                        E.op(E.vec, "tensor_add", yv[:, hc, b, 0:tcn],
                             yv[:, hc, b, 0:tcn], pC[:, off:off + tcn],
                             inc=E.s_v)
                        E.drain(E.vec)
                        yadd_marks[hc] = E.s_v.n
                    m_yadd = E.s_v.n

                if not shared:
                    wbc_used[epar] = uw_marks[(IC - 1) % 2]
                    slot_free[slg] = m_gmm
                    slot_free[slu] = m_umm
                    slot_free[sld] = m_dmm
                    if e + 1 < NE:
                        for j in range(3):
                            stream_mat(ci, e + 1, j)
                    elif ci + 1 < n_chunks:
                        for j in range(3):
                            stream_mat(ci + 1, 0, j)

            if ci == 0 and ("f_w" in dbg_outs):
                E.wait(E.sync, E.s_v, E.s_v.n)
                E.op(E.sync, "dma_start", out=dbg_outs["f_w"],
                     in_=gbuf[:, 1232:1360], inc=E.s_x, inck=16)
                E.op(E.sync, "dma_start", out=dbg_outs["f_y"],
                     in_=ybuf[:, 0:DC * TOK], inc=E.s_x, inck=16)
                E.op(E.sync, "dma_start", out=dbg_outs["f_ln"],
                     in_=ln2b[:, :].bitcast(I32).bitcast(F32)[:, 0:DC * TOK // 2],
                     inc=E.s_x, inck=16)
                E.wait(E.sync, E.s_x, E.s_x.n)
            xs = xbv[:, :, :, 0:tcn]
            yvv = ybuf.rearrange("p (hc b t) -> p hc b t", hc=HC, b=BL)
            E.wait(E.vec, E.s_v, m_yadd)
            E.op(E.vec, "tensor_add", xs, xs, yvv[:, :, :, 0:tcn], inc=E.s_v)
            E.drain(E.vec)
            m_xres = E.s_v.n
            E.wait(E.sync, E.s_v, m_xres)
            for b in range(BL):
                E.op(E.sync, "dma_start", out=dst_f[:, :, b, tc0:tc0 + tcn],
                     in_=xbv[:, :, b, 0:tcn], inc=E.s_x, inck=16)
            m_xst = E.s_x.n

        E.barrier()
        ph.close()
    if "f1" in phases:
        emit_ffn("f1_")

    def emit_attn_b():
        BT = 96
        ph = ExitStack()

        def psb(name, shape, dt=F32):
            return ph.enter_context(nc.sbuf_tensor("ab" + name, list(shape), dt))

        CW = BL * BT
        SR = BT + 1
        wr_sb = psb("wr", [128, DC * D], BF16)
        wi_sb = psb("wi", [128, DC * D], BF16)
        wo_sb = psb("wo", [128, DC * D], BF16)
        ext = psb("ext", [128, DC * BL * (BT + 3)])
        xbuf_b = psb("x", [128, DC * CW])
        ln3b = psb("ln3b", [128, DC * CW], BF16)
        sqb3 = psb("sqb3", [128, DC * CW], BF16)
        xc = psb("xc", [128, DC * CW])
        xcb = psb("xcb", [128, DC * CW], BF16)
        thr = psb("thr", [128, DC * CW])
        thi = psb("thi", [128, DC * CW])
        u8 = psb("u8", [128, DC * CW])
        abuf = psb("a", [128, DC * CW])
        btb = psb("bt", [128, DC * CW])
        tmpb = psb("tmp", [128, DC * CW])
        Ab = psb("A", [128, DC * CW])
        hbf = psb("hbf", [128, DC * CW], BF16)
        obuf = psb("o", [128, DC * CW])
        la_s = psb("las", [128, 16 * SR])
        q2_s = psb("q2s", [128, 16 * SR])
        lacum = psb("lacum", [128, 16 * SR])
        q2cum = psb("q2cum", [128, 16 * SR])
        ones_t = psb("onest", [128, 16 * SR])
        carry2 = psb("carry2", [128, 32])

        for dst, src in ((wr_sb, w_rT), (wi_sb, w_iT), (wo_sb, w_oT)):
            E.op(E.gps, "dma_start",
                 out=dst.rearrange("p (kc m) -> p kc m", kc=DC),
                 in_=src.rearrange("(kc p) m -> p kc m", p=128),
                 inc=E.s_w, inck=16)
        m_wab = E.s_w.n

        E.op(E.vec, "tensor_single_scalar", bview[:, 3, :], bview[:, 3, :],
             0.5, ALU.mult)
        E.op(E.vec, "tensor_single_scalar", bview[:, 4, :], bview[:, 4, :],
             0.5, ALU.mult)
        E.op(E.vec, "tensor_single_scalar", bview[:, 5, :], bview[:, 5, :],
             0.5, ALU.mult)
        E.drain(E.vec)
        E.op(E.vec, "memset", ones_t[:, :], 1.0)
        E.drain(E.vec)
        E.op(E.vec, "memset",
             ones_t.rearrange("p (r c) -> p r c", c=SR)[:, :, 0:1], 0.0)
        E.op(E.vec, "memset", carry2[:, :], 0.0)
        E.op(E.vec, "memset",
             ext.rearrange("p (dc b t) -> p dc b t", dc=DC, b=BL)[:, :, :, 0:3],
             0.0, inc=E.s_v)
        E.drain(E.vec)
        if dbg_outs:
            for bfr in (obuf, xc, u8, abuf, btb, Ab):
                E.op(E.vec, "memset", bfr[:, :], 0.0)
            E.drain(E.vec)

        src_b, dst_b = phase_io("ab")
        m_xst = 0
        n_chunks = (Ll + BT - 1) // BT
        sp_prev = None
        cur_sr = [BT + 1]

        def set_sr(srn):
            if cur_sr[0] != srn:
                cur_sr[0] = srn
                E.op(E.vec, "memset", ones_t[:, 0:16 * srn], 1.0)
                E.drain(E.vec)
                E.op(E.vec, "memset", ones_t[:, 0:16 * srn].rearrange(
                    "p (r c) -> p r c", c=srn)[:, :, 0:1], 0.0)
                E.drain(E.vec)
        for ci in range(n_chunks):
            tc0 = ci * BT
            tcn = min(BT, Ll - tc0)
            TT = BL * tcn
            extv = ext.rearrange("p (dc b t) -> p dc b t", dc=DC, b=BL)
            if ci > 0:
                E.op(E.vec, "tensor_copy", extv[:, :, :, 0:3],
                     extv[:, :, :, prev_tcn:prev_tcn + 3])
                E.drain(E.vec)
            prev_tcn = tcn
            xrv = xbuf_b.rearrange("p (dc b t) -> p dc b t", dc=DC, b=BL)
            if m_xst:
                E.wait(E.sync, E.s_x, m_xst)
            for b in range(BL):
                E.op(E.sync, "dma_start", out=xrv[:, :, b, 0:tcn],
                     in_=src_b[:, :, b, tc0:tc0 + tcn], inc=E.s_x, inck=16)
            m_xld = E.s_x.n
            m_ln = emit_ln(tc0, tcn, None, ln3b, sqb3, toff=0,
                           f32view=extv[:, :, :, 3:3 + tcn],
                           pre_wait_t=(m_ms if ci > 0 else None),
                           xs=xrv[:, :, :, 0:tcn], wait_x=m_xld)

            xcv = xc.rearrange("p (dc b t) -> p dc b t", dc=DC, b=BL)[
                :, :, :, 0:tcn]
            tv = tmpb.rearrange("p (dc b t) -> p dc b t", dc=DC, b=BL)[
                :, :, :, 0:tcn]
            cwv = conv_w.rearrange("p (dc k) -> p dc k", dc=DC)
            E.wait(E.vec, E.s_v, m_ln)
            E.op(E.vec, "tensor_copy", xcv,
                 bview[:, 2, :, None, None].broadcast_to([128, DC, BL, tcn]))
            E.drain(E.vec)
            for k in range(4):
                E.op(E.vec, "tensor_mul", tv, extv[:, :, :, k:k + tcn],
                     cwv[:, :, k, None, None].broadcast_to([128, DC, BL, tcn]))
                E.drain(E.vec)
                E.op(E.vec, "tensor_add", xcv, xcv, tv)
                E.drain(E.vec)
            E.op(E.vec, "engine_nop", inc=E.s_v)
            m_xc = E.s_v.n
            E.wait(E.sca, E.s_v, m_xc)
            E.op(E.sca, "copy", xcb[:, 0:DC * TT] if TT == CW // 1 and False
                 else xcb.rearrange("p (dc b t) -> p dc b t", dc=DC, b=BL)[
                     :, :, :, 0:tcn],
                 xcv, inc=E.s_s)
            m_xcb = E.s_s.n

            xbv = xcb.rearrange("p (dc b t) -> p dc b t", dc=DC, b=BL)
            E.wait(E.ten, E.s_s, m_xcb)
            if ci == 0:
                E.wait(E.ten, E.s_w, m_wab)
            for wsb, bslot, dstb in ((wr_sb, 3, thr), (wi_sb, 4, thi)):
                wv = wsb.rearrange("p (kc m) -> p kc m", kc=DC)
                dv3 = dstb.rearrange("p (dc b t) -> p dc b t", dc=DC, b=BL)
                for oc in range(DC):
                    dst = pA[:, (oc % 2) * 512:(oc % 2) * 512 + TT].rearrange(
                        "p (b t) -> p b t", b=BL)
                    if sp_prev is not None:
                        E.wait(E.ten, E.s_s, sp_prev)
                    for kc in range(DC):
                        E.op(E.ten, "matmul", dst,
                             wv[:, kc, oc * 128:(oc + 1) * 128],
                             xbv[:, kc, :, 0:tcn],
                             start=(kc == 0), stop=(kc == DC - 1),
                             inc=(E.s_t if kc == DC - 1 else None))
                    m_mm = E.s_t.n
                    E.wait(E.sca, E.s_t, m_mm)
                    E.op(E.sca, "activation", dv3[:, oc, :, 0:tcn], dst,
                         ACT.Tanh, scale=0.5, bias=bview[:, bslot, oc:oc + 1],
                         inc=E.s_s)
                    if oc >= 1 or wsb is wi_sb:
                        sp_prev = E.s_s.n
                    else:
                        sp_prev = E.s_s.n
            m_ri = E.s_s.n

            thrv = thr.rearrange("p (dc b t) -> p dc b t", dc=DC, b=BL)[
                :, :, :, 0:tcn]
            u8v = u8.rearrange("p (dc b t) -> p dc b t", dc=DC, b=BL)[
                :, :, :, 0:tcn]
            E.wait(E.vec, E.s_s, m_ri)
            E.op(E.vec, "tensor_mul", u8v, thrv,
                 bview[:, 5, :, None, None].broadcast_to([128, DC, BL, tcn]))
            E.drain(E.vec)
            E.op(E.vec, "tensor_add", u8v, u8v,
                 bview[:, 5, :, None, None].broadcast_to([128, DC, BL, tcn]),
                 inc=E.s_v)
            E.drain(E.vec)
            m_u8 = E.s_v.n
            E.wait(E.sca, E.s_v, m_u8)
            av = abuf.rearrange("p (dc b t) -> p dc b t", dc=DC, b=BL)[
                :, :, :, 0:tcn]
            E.op(E.sca, "activation", av, u8v, ACT.Exp, inc=E.s_s)
            m_a = E.s_s.n
            lasv = la_s[:, 0:16 * (tcn + 1)].rearrange(
                "p (dc b c) -> p dc b c", dc=DC, b=BL)
            E.op(E.vec, "tensor_single_scalar", lasv[:, :, :, 1:1 + tcn],
                 u8v, LN17, ALU.max)
            E.drain(E.vec)
            c2v = carry2.rearrange("p (k dcb) -> p k dcb", k=2)
            E.op(E.vec, "tensor_copy",
                 lasv[:, :, :, 0].rearrange("p dc b -> p (dc b)"),
                 c2v[:, 0, :])
            E.drain(E.vec)
            set_sr(tcn + 1)
            E.op(E.vec, "tensor_tensor_scan", lacum[:, 0:16 * (tcn + 1)],
                 ones_t[:, 0:16 * (tcn + 1)],
                 la_s[:, 0:16 * (tcn + 1)], 0.0, ALU.mult, ALU.add)
            E.drain(E.vec)
            lcv = lacum[:, 0:16 * (tcn + 1)].rearrange(
                "p (dc b c) -> p dc b c", dc=DC, b=BL)
            E.op(E.vec, "tensor_copy", c2v[:, 0, :],
                 lcv[:, :, :, tcn].rearrange("p dc b -> p (dc b)"))
            E.drain(E.vec)
            E.op(E.vec, "tensor_single_scalar", lcv[:, :, :, 1:1 + tcn],
                 lcv[:, :, :, 1:1 + tcn], -80.0, ALU.max, inc=E.s_v)
            E.drain(E.vec)
            m_lc = E.s_v.n

            qv = tmpb.rearrange("p (dc b t) -> p dc b t", dc=DC, b=BL)[
                :, :, :, 0:tcn]
            E.wait(E.vec, E.s_s, m_a)
            E.op(E.vec, "tensor_mul", qv, av, av)
            E.drain(E.vec)
            E.op(E.vec, "tensor_scalar", out=qv, in0=qv, scalar1=-1.0,
                 scalar2=1.0, op0=ALU.mult, op1=ALU.add)
            E.drain(E.vec)
            E.op(E.vec, "tensor_scalar_max", qv, qv, 1e-6, inc=E.s_v)
            E.drain(E.vec)
            m_q = E.s_v.n
            E.wait(E.sca, E.s_v, m_q)
            E.op(E.sca, "activation", qv, qv, ACT.Ln)
            E.drain(E.sca)
            E.op(E.sca, "activation", qv, qv, ACT.Exp, scale=0.5, inc=E.s_s)
            m_sq = E.s_s.n
            Av = Ab.rearrange("p (dc b t) -> p dc b t", dc=DC, b=BL)[
                :, :, :, 0:tcn]
            E.wait(E.sca, E.s_v, m_lc)
            E.op(E.sca, "activation", Av, lcv[:, :, :, 1:1 + tcn], ACT.Exp)
            E.drain(E.sca)
            q2v = q2_s[:, 0:16 * (tcn + 1)].rearrange(
                "p (dc b c) -> p dc b c", dc=DC, b=BL)
            E.op(E.sca, "activation", q2v[:, :, :, 1:1 + tcn],
                 lcv[:, :, :, 1:1 + tcn], ACT.Exp, scale=-1.0, inc=E.s_s)
            m_enA = E.s_s.n

            btv = btb.rearrange("p (dc b t) -> p dc b t", dc=DC, b=BL)[
                :, :, :, 0:tcn]
            thiv = thi.rearrange("p (dc b t) -> p dc b t", dc=DC, b=BL)[
                :, :, :, 0:tcn]
            E.wait(E.vec, E.s_s, m_sq)
            E.op(E.vec, "tensor_scalar", out=thiv, in0=thiv, scalar1=0.5,
                 scalar2=0.5, op0=ALU.mult, op1=ALU.add)
            E.drain(E.vec)
            E.op(E.vec, "tensor_mul", btv, qv, thiv)
            E.drain(E.vec)
            E.op(E.vec, "tensor_mul", btv, btv, xcv)
            E.drain(E.vec)
            E.wait(E.vec, E.s_s, m_enA)
            E.op(E.vec, "tensor_mul", q2v[:, :, :, 1:1 + tcn],
                 q2v[:, :, :, 1:1 + tcn], btv)
            E.drain(E.vec)
            E.op(E.vec, "tensor_copy",
                 q2v[:, :, :, 0].rearrange("p dc b -> p (dc b)"), c2v[:, 1, :])
            E.drain(E.vec)
            E.op(E.vec, "tensor_tensor_scan", q2cum[:, 0:16 * (tcn + 1)],
                 ones_t[:, 0:16 * (tcn + 1)],
                 q2_s[:, 0:16 * (tcn + 1)], 0.0, ALU.mult, ALU.add)
            E.drain(E.vec)
            qcv = q2cum[:, 0:16 * (tcn + 1)].rearrange(
                "p (dc b c) -> p dc b c", dc=DC, b=BL)
            E.op(E.vec, "tensor_copy", c2v[:, 1, :],
                 qcv[:, :, :, tcn].rearrange("p dc b -> p (dc b)"))
            E.drain(E.vec)
            hv = hbf.rearrange("p (dc b t) -> p dc b t", dc=DC, b=BL)[
                :, :, :, 0:tcn]
            E.op(E.vec, "tensor_mul", hv, Av, qcv[:, :, :, 1:1 + tcn],
                 inc=E.s_v)
            E.drain(E.vec)
            m_h = E.s_v.n

            hbv = hbf.rearrange("p (dc b t) -> p dc b t", dc=DC, b=BL)
            ov3 = obuf.rearrange("p (dc b t) -> p dc b t", dc=DC, b=BL)
            sq3 = sqb3.rearrange("p (dc b t) -> p dc b t", dc=DC, b=BL)
            wov = wo_sb.rearrange("p (kc m) -> p kc m", kc=DC)
            E.wait(E.ten, E.s_v, m_h)
            for oc in range(DC):
                dst = pA[:, (oc % 2) * 512:(oc % 2) * 512 + TT].rearrange(
                    "p (b t) -> p b t", b=BL)
                if sp_prev is not None:
                    E.wait(E.ten, E.s_s, sp_prev)
                for kc in range(DC):
                    E.op(E.ten, "matmul", dst,
                         wov[:, kc, oc * 128:(oc + 1) * 128],
                         hbv[:, kc, :, 0:tcn],
                         start=(kc == 0), stop=(kc == DC - 1),
                         inc=(E.s_t if kc == DC - 1 else None))
                m_mm = E.s_t.n
                E.wait(E.sca, E.s_t, m_mm)
                E.op(E.sca, "copy", ov3[:, oc, :, 0:tcn], dst)
                E.drain(E.sca)
                E.op(E.sca, "activation", sq3[:, oc, :, 0:tcn], dst,
                     ACT.Square, inc=E.s_s)
                sp_prev = E.s_s.n
            m_oev = E.s_s.n

            E.wait(E.ten, E.s_s, m_oev)
            for b in range(BL):
                dst = pB[0:1, b * tcn:(b + 1) * tcn]
                for kc in range(DC):
                    E.op(E.ten, "matmul", dst, onescol[:, :],
                         sq3[:, kc, b, 0:tcn],
                         start=(kc == 0), stop=(kc == DC - 1),
                         inc=(E.s_t if (b == BL - 1 and kc == DC - 1)
                              else None))
            m_ms = E.s_t.n
            E.wait(E.vec, E.s_t, m_ms)
            msr = smalls[0:1, 0:TT]
            w1 = smalls[0:1, 512:512 + TT]
            w2 = smalls[0:1, 1024:1024 + TT]
            E.op(E.vec, "tensor_scalar", out=msr, in0=pB[0:1, 0:TT],
                 scalar1=1.0 / D, scalar2=1e-6, op0=ALU.mult, op1=ALU.add)
            E.drain(E.vec)
            emit_rsqrt(E, E.vec, msr, msr, w1, w2)
            rh = hlrows[0:1, 0:TT]
            rl = hlrows[0:1, 512:512 + TT]
            E.op(E.vec, "tensor_copy", rh, msr)
            E.drain(E.vec)
            E.op(E.vec, "tensor_sub", w1, msr, rh)
            E.drain(E.vec)
            E.op(E.vec, "tensor_copy", rl, w1, inc=E.s_v)
            m_rr = E.s_v.n
            E.wait(E.ten, E.s_v, m_rr)
            E.op(E.ten, "matmul", pC[:, 0:TT], ones1[:, :], rh,
                 start=True, stop=False)
            E.op(E.ten, "matmul", pC[:, 0:TT], ones1[:, :], rl,
                 start=False, stop=True, inc=E.s_t)
            m_rbc = E.s_t.n
            E.wait(E.vec, E.s_t, m_rbc)
            rbc = pC[:, 0:TT].rearrange("p (b t) -> p b t", b=BL)
            y3 = tmpb.rearrange("p (dc b t) -> p dc b t", dc=DC, b=BL)[
                :, :, :, 0:tcn]
            E.op(E.vec, "tensor_mul", y3, ov3[:, :, :, 0:tcn],
                 rbc[:, None, :, :].broadcast_to([128, DC, BL, tcn]))
            E.drain(E.vec)
            E.op(E.vec, "tensor_mul", y3, y3,
                 bview[:, 6, :, None, None].broadcast_to([128, DC, BL, tcn]))
            E.drain(E.vec)
            xs = xrv[:, :, :, 0:tcn]
            E.op(E.vec, "tensor_add", xs, xs, y3, inc=E.s_v)
            E.drain(E.vec)
            m_xres = E.s_v.n
            E.wait(E.sync, E.s_v, m_xres)
            for b in range(BL):
                E.op(E.sync, "dma_start", out=dst_b[:, :, b, tc0:tc0 + tcn],
                     in_=xrv[:, :, b, 0:tcn], inc=E.s_x, inck=16)
            m_xst = E.s_x.n

        if dbg_outs and "ab_o" in dbg_outs:
            E.wait(E.sync, E.s_v, E.s_v.n)
            E.op(E.sync, "dma_start", out=dbg_outs["ab_o"], in_=obuf[:, :],
                 inc=E.s_x, inck=16)
            E.op(E.sync, "dma_start", out=dbg_outs["ab_xc"], in_=xc[:, :],
                 inc=E.s_x, inck=16)
            E.wait(E.sync, E.s_x, E.s_x.n)
        E.barrier()
        ph.close()

    if "ab" in phases:
        emit_attn_b()

    if "f2" in phases:
        emit_ffn("f2_")


    if not ph_order:
        E.op(E.sync, "dma_start",
             out=x_out.rearrange("(dc p) b t -> p dc b t", p=128),
             in_=x_in.rearrange("(dc p) b t -> p dc b t", p=128),
             inc=E.s_x, inck=16)
    E.wait(E.sync, E.s_x, E.s_x.n)

    with ctx:
        blk = ctx.enter_context(nc.Block())
        blk.sync(lambda e: [f(e) for f in E.sync.q])
        blk.vector(lambda e: [f(e) for f in E.vec.q])
        blk.scalar(lambda e: [f(e) for f in E.sca.q])
        blk.tensor(lambda e: [f(e) for f in E.ten.q])
        blk.gpsimd(lambda e: [f(e) for f in E.gps.q])

    return nc, E, dbg_outs


def _prep_inputs(inp, core, Ll=2048):
    x = np.asarray(inp["L0_attn_a_x"], np.float32)
    xc = np.ascontiguousarray(
        np.transpose(x[2 * core:2 * core + 2, :Ll], (2, 0, 1)))

    def T(a):
        return np.ascontiguousarray(np.asarray(a, np.float32).T)

    def r128(v):
        return np.ascontiguousarray(np.asarray(v, np.float32).reshape(DC, 128).T)

    import ml_dtypes
    BFD = ml_dtypes.bfloat16
    sel = np.zeros((NST, NST * 128), BFD)
    for n in range(NST):
        sel[n, n * 128:(n + 1) * 128] = 1
    im = {
        "x_in": xc,
        "w_dtT": T(inp["L0_attn_a_W_dt"]),
        "w_BT": T(inp["L0_attn_a_W_B"]),
        "w_CT": T(inp["L0_attn_a_W_C"]),
        "b_dt_r": r128(inp["L0_attn_a_b_dt"]),
        "A_log_r": np.ascontiguousarray(
            np.asarray(inp["L0_attn_a_A_log"], np.float32)
            .reshape(DC, 128, NST).transpose(1, 0, 2).reshape(128, DC * NST)),
        "D_par_r": r128(inp["L0_attn_a_D_param"]),
        "conv_w_r": np.ascontiguousarray(
            np.asarray(inp["L0_attn_b_conv_weight"], np.float32)[:, 0, :]
            .reshape(DC, 128, 4).transpose(1, 0, 2).reshape(128, DC * 4)),
        "conv_b_r": r128(inp["L0_attn_b_conv_bias"]),
        "w_rT": T(inp["L0_attn_b_Wr_w"]),
        "w_iT": T(inp["L0_attn_b_Wi_w"]),
        "wr_b_r": r128(inp["L0_attn_b_Wr_b"]),
        "wi_b_r": r128(inp["L0_attn_b_Wi_b"]),
        "log_a_r": r128(inp["L0_attn_b_log_a"]),
        "w_oT": T(inp["L0_attn_b_out_w"]),
        "norm_w_r": r128(inp["L0_attn_b_norm_w"]),
        "gate_wT": T(inp["L0_ffn_gate_w"]),
        "e_bias_bc": np.ascontiguousarray(np.broadcast_to(
            np.asarray(inp["L0_ffn_e_bias"], np.float32)[None, :], (128, NE))),
        "gprojT": np.ascontiguousarray(
            np.asarray(inp["L0_ffn_gate_proj"], np.float32).transpose(0, 2, 1)),
        "uprojT": np.ascontiguousarray(
            np.asarray(inp["L0_ffn_up_proj"], np.float32).transpose(0, 2, 1)),
        "dprojT": np.ascontiguousarray(
            np.asarray(inp["L0_ffn_down_proj"], np.float32).transpose(0, 2, 1)),
        "sgT": T(inp["L0_ffn_shared_gate_w"]),
        "suT": T(inp["L0_ffn_shared_up_w"]),
        "sdT": T(inp["L0_ffn_shared_down_w"]),
        "c_ones1": np.ones((1, 128), BFD),
        "c_onescol": np.ones((128, 1), BFD),
        "c_ones128": np.ones((128, 128), BFD),
        "c_ident": np.eye(128, dtype=BFD),
        "c_bias": np.ascontiguousarray(np.broadcast_to(np.array(
            [ATANH_BI, 1.0, LN17, 0.0], np.float32)[None, :], (128, 4))),
        "c_sel": sel,
    }
    return im


def kernel(**inputs):
    nc, E, _ = build_program()
    in_maps = [_prep_inputs(inputs, c) for c in range(8)]
    res = run_bass_kernel_spmd(nc, in_maps, core_ids=list(range(8)))
    outs = []
    for c in range(8):
        xc = res.results[c]["x_out"]
        outs.append(np.transpose(xc, (1, 2, 0)))
    return np.concatenate(outs, axis=0).astype(np.float32)
